# version 17
# baseline (speedup 1.0000x reference)
import sys

for _p in ("/opt/trn_rl_repo", "/root/.axon_site/_ro/trn_rl_repo"):
    if _p not in sys.path:
        sys.path.append(_p)

import contextlib

import numpy as np
import ml_dtypes

import concourse.bass as bass
import concourse.mybir as mybir
from concourse.bass_utils import run_bass_kernel_spmd

F32 = mybir.dt.float32
BF16 = mybir.dt.bfloat16
AF = mybir.ActivationFunctionType
ALU = mybir.AluOpType
AX = mybir.AxisListType

H = 4096
DC = 32
NH = 16
SL = 4096
TOWN = 2048
QB = 512
NSL = 8
QORDER = [4, 0, 5, 1, 6, 2, 7, 3]
SCALE = float(1.0 / np.sqrt(128.0))
EPS = 1e-5
GROUPS = [[0, 1], [2, 3], [4, 5], [6, 7]]
OB3 = 256
NOB3 = H // OB3


class Sem:
    def __init__(self, handle):
        self.h = handle
        self.n = 0


class Prog:

    def __init__(self):
        self.ops = []

    def op(self, engine, fn, waits=(), inc=None, inc16=None):
        incs = []
        if inc is not None:
            inc.n += 1
            incs.append((inc.h, 1))
        if inc16 is not None:
            inc16.n += 16
            incs.append((inc16.h, 16))
        self.ops.append((engine, tuple((s.h, v) for (s, v) in waits), fn, incs))

    def wait(self, engine, waits):
        self.ops.append((engine, tuple((s.h, v) for (s, v) in waits), None, []))


def _check_deadlock(prog):
    qs = {}
    for it in prog.ops:
        qs.setdefault(it[0], []).append(it)
    heads = {k: 0 for k in qs}
    counts = {}
    progress = True
    while progress:
        progress = False
        for eng, lst in qs.items():
            while heads[eng] < len(lst):
                _, waits, _, incs = lst[heads[eng]]
                if all(counts.get(h, 0) >= v for (h, v) in waits):
                    for (h, amt) in incs:
                        counts[h] = counts.get(h, 0) + amt
                    heads[eng] += 1
                    progress = True
                else:
                    break
    stuck = {e: (heads[e], len(l)) for e, l in qs.items() if heads[e] < len(l)}
    if stuck:
        lines = []
        for e, (i, n) in stuck.items():
            _, waits, _, _ = qs[e][i]
            lines.append(f"{e}: op {i}/{n} waits " + str(
                [(getattr(h, "name", h), v, counts.get(h, 0)) for h, v in waits]))
        raise RuntimeError("sync deadlock:\n" + "\n".join(lines))


def build_program():
    nc = bass.Bass("TRN2", target_bir_lowering=False)
    es = contextlib.ExitStack()

    def dram(name, shape, dt, kind=None):
        if kind is None:
            return nc.dram_tensor(name, shape, dt).ap()
        return nc.dram_tensor(name, shape, dt, kind=kind).ap()

    x_l = dram("x_l", [SL, H], F32, "ExternalInput")
    wqk = dram("wqk", [32, 128, DC * 128], F32, "ExternalInput")
    wv = dram("wv", [8, 128, DC * 256], F32, "ExternalInput")
    qkvb_qk = dram("qkvb_qk", [128, 32], F32, "ExternalInput")
    vb_row = dram("vb_row", [2048], F32, "ExternalInput")
    ow_h = dram("ow_h", [16, 128, DC * 256], F32, "ExternalInput")
    ob_row = dram("ob_row", [H], F32, "ExternalInput")
    Wn = dram("Wn", [H, H], F32, "ExternalInput")
    b_row = dram("b_row", [H], F32, "ExternalInput")
    A_in = dram("A_in", [64, H], F32, "ExternalInput")
    B_in = dram("B_in", [128, DC * 64], F32, "ExternalInput")
    gw_in = dram("gw_in", [4, H], F32, "ExternalInput")
    gb_in = dram("gb_in", [4], F32, "ExternalInput")
    identb_in = dram("identb_in", [128, 128], BF16, "ExternalInput")
    identf_in = dram("identf_in", [128, 128], F32, "ExternalInput")
    onesf_in = dram("onesf_in", [1, 128], F32, "ExternalInput")
    onesb_in = dram("onesb_in", [128, 1], BF16, "ExternalInput")
    levels_in = dram("levels_in", [128, 64], F32, "ExternalInput")
    masks_in = dram("masks_in", [128, 2], F32, "ExternalInput")
    eps_in = dram("eps_in", [128, 1], F32, "ExternalInput")
    out_d = dram("out", [TOWN, H], F32, "ExternalOutput")

    qT_s = dram("qT_s", [NH, 128, SL], BF16)
    kT_s = dram("kT_s", [NH, 128, SL], BF16)
    v_s = dram("v_s", [NH, 128, DC, 128], BF16)
    cc_in = [dram(f"cc_in{j}", [2 * TOWN, QB], BF16) for j in range(4)]
    cc_out = [dram(f"cc_out{j}", [TOWN, QB], BF16) for j in range(4)]
    ow_bf_s = dram("ow_bf_s", [16, 128, DC * 256], BF16)
    x2_s = dram("x2_s", [TOWN, H], F32)
    ln2_s = dram("ln2_s", [TOWN, H], BF16)

    def sbuf(name, shape, dt, scope=es):
        return scope.enter_context(nc.sbuf_tensor(name, shape, dt))

    def psum(name, shape, dt):
        return es.enter_context(nc.psum_tensor(name, shape, dt))

    identb = sbuf("identb", [128, 128], BF16)
    identf = sbuf("identf", [128, 128], F32)
    onesf = sbuf("onesf", [1, 128], F32)
    onesb = sbuf("onesb", [128, 1], BF16)
    levels = sbuf("levels", [128, 64], F32)
    masks = sbuf("masks", [128, 2], F32)
    qkvb_sb = sbuf("qkvb_sb", [128, 32], F32)
    row_sb = sbuf("row_sb", [1, 1024], F32)
    gb_sb = sbuf("gb_sb", [4, 1], F32)
    eps_sb = sbuf("eps_sb", [128, 1], F32)

    ps_a = psum("ps_a", [128, 1024], F32)
    ps_b = psum("ps_b", [128, 1024], F32)
    ps_c0 = psum("ps_c0", [128, 512], F32)
    ps_c1 = psum("ps_c1", [128, 512], F32)
    ps_d = psum("ps_d", [128, 512], F32)
    ps_t = psum("ps_t", [128, 1024], BF16)

    semnames = ["s_cst", "s_row", "s_ln", "s_sc", "s_tp", "s_te", "s_mm",
                "s_ev", "s_exp", "s_ctx", "s_dt", "s_dmm", "s_rc", "s_bc",
                "s_bcp", "s_ce", "s_ccm", "s_ccd", "s_cc", "s_rx", "s_op",
                "s_oe", "s_l2", "s_bar", "s_xsl", "s_wr",
                "s_xh0", "s_xh1", "s_wp0", "s_wp1", "s_stp0", "s_stp1",
                "s_vsp0", "s_vsp1", "s_kvp0", "s_kvp1", "s_x2p0", "s_x2p1"]
    S = {n: Sem(es.enter_context(nc.semaphore(n))) for n in semnames}
    s_cst, s_row, s_ln, s_sc, s_tp, s_te = (S[k] for k in semnames[0:6])
    s_mm, s_ev, s_exp, s_ctx, s_dt, s_dmm = (S[k] for k in semnames[6:12])
    s_rc, s_bc, s_bcp, s_ce, s_ccm, s_ccd = (S[k] for k in semnames[12:18])
    s_cc, s_rx, s_op, s_oe, s_l2, s_bar = (S[k] for k in semnames[18:24])
    s_xsl, s_wr = S["s_xsl"], S["s_wr"]
    s_xh = [S["s_xh0"], S["s_xh1"]]
    s_wp = [S["s_wp0"], S["s_wp1"]]
    s_stp = [S["s_stp0"], S["s_stp1"]]
    s_vsp = [S["s_vsp0"], S["s_vsp1"]]
    s_kvp = [S["s_kvp0"], S["s_kvp1"]]
    s_x2p = [S["s_x2p0"], S["s_x2p1"]]

    P = Prog()

    for dst, src in [(identb, identb_in), (identf, identf_in), (onesf, onesf_in),
                     (onesb, onesb_in), (levels, levels_in), (masks, masks_in),
                     (qkvb_sb, qkvb_qk), (eps_sb, eps_in)]:
        P.op("sync", lambda e, d=dst, s=src: e.dma_start(out=d[:], in_=s[:]),
             inc16=s_cst)
    P.op("sync", lambda e: e.dma_start(
        out=gb_sb[:, :], in_=gb_in.rearrange("(p a) -> p a", a=1)), inc16=s_cst)
    consts_done = s_cst.n

    rb_state = {"tp": {}, "te": {}, "i": 0}

    def emit_row_bcast(row_src, n, dest_sb):
        row2 = row_src.rearrange("(a f) -> a f", a=1)
        rbanks = [ps_d, ps_c0, ps_c1]
        for i in range((n + 511) // 512):
            w = min(512, n - i * 512)
            gi = rb_state["i"]
            rb_state["i"] += 1
            par = gi % 2
            bank = rbanks[gi % 3]
            ldsem = [s_row, s_cst][par]
            wv = [(s_tp, rb_state["tp"][gi - 2])] if gi >= 2 else []
            P.op("sync", lambda e, i=i, w=w, par=par: e.dma_start(
                out=row_sb[:, par * 512:par * 512 + w],
                in_=row2[0:1, i * 512:i * 512 + w]),
                waits=wv, inc16=ldsem)
            ldv = ldsem.n
            wv2 = [(ldsem, ldv), (s_cst, consts_done)]
            if gi >= 3:
                wv2.append((s_te, rb_state["te"][gi - 3]))
            P.op("tensor", lambda e, w=w, par=par, bank=bank: e.matmul(
                bank[:, 0:w], lhsT=onesf[:, :],
                rhs=row_sb[:, par * 512:par * 512 + w],
                start=True, stop=True),
                waits=wv2, inc=s_tp)
            rb_state["tp"][gi] = s_tp.n
            P.op("scalar", lambda e, i=i, w=w, bank=bank: e.copy(
                out=dest_sb[:, i * 512:i * 512 + w], in_=bank[:, 0:w]),
                waits=[(s_tp, s_tp.n)], inc=s_te)
            rb_state["te"][gi] = s_te.n

    p1 = contextlib.ExitStack()
    vb_bc = sbuf("vb_bc", [128, 2048], BF16, p1)
    emit_row_bcast(vb_row, 2048, vb_bc)
    vbbc_done = s_te.n


    ln1T = sbuf("ln1T", [128, DC, 2048], BF16, p1)
    x_sb = [sbuf(f"x_sb{i}", [128, 2048], F32, p1) for i in range(2)]
    ln1h = sbuf("ln1h", [128, 2048], BF16, p1)
    w_u = [sbuf(f"w_u{i}", [128, 8192], BF16, p1) for i in range(2)]
    qkst = [sbuf(f"qkst{i}", [128, 512], BF16, p1) for i in range(2)]
    vst = [sbuf(f"vst{i}", [128, 256], BF16, p1) for i in range(2)]
    stats1 = sbuf("stats1", [128, 48], F32, p1)
    mv1 = [sbuf(f"mv1_{i}", [128, 2], F32, p1) for i in range(2)]
    rstd1 = [sbuf(f"rstd1_{i}", [128, 1], F32, p1) for i in range(2)]
    lnt1 = sbuf("lnt1", [128, 1], F32, p1)
    nmr1 = [sbuf(f"nmr1_{i}", [128, 1], F32, p1) for i in range(2)]

    x_free = {}
    ln_v = {}
    grp_idx = 0
    grp_ev = []
    qk_store_v = []
    v_store_v = []
    wu_free = {}
    wu_lidx = 0

    for tsb in range(2):
        t0 = tsb * 2048
        for tt in range(16):
            gt = tsb * 16 + tt
            row = t0 + tt * 128
            xv = {}
            for hf in range(2):
                w = [(s_sc, x_free[(gt - 1, hf)])] if gt >= 1 else \
                    [(s_cst, consts_done)]
                P.op("sync", lambda e, row=row, hf=hf: e.dma_start(
                    out=x_sb[hf][:],
                    in_=x_l[row:row + 128, hf * 2048:(hf + 1) * 2048]),
                    waits=w, inc16=s_xh[hf])
                xv[hf] = s_xh[hf].n
                for i in range(4):
                    P.op("vector", lambda e, hf=hf, i=i: e.bn_stats(
                        stats1[:, (hf * 4 + i) * 6:(hf * 4 + i) * 6 + 6],
                        x_sb[hf][:, i * 512:(i + 1) * 512]),
                        waits=[(s_xh[hf], xv[hf])],
                        inc=(s_ln if (hf == 1 and i == 3) else None))
            pp = gt % 2
            w = [(s_ln, s_ln.n)]
            if gt >= 2:
                w.append((s_sc, ln_v[gt - 2]))
            P.op("vector", lambda e, pp=pp: e.bn_aggr(mv1[pp][:], stats1[:]),
                 waits=w, inc=s_ln)
            P.op("scalar", lambda e, pp=pp: e.activation(
                out=lnt1[:], in_=mv1[pp][:, 1:2], func=AF.Ln,
                bias=eps_sb[:, 0:1]),
                waits=[(s_ln, s_ln.n)], inc=s_sc)
            ln_v[gt] = s_sc.n
            w = [(s_sc, s_sc.n)]
            if gt >= 2:
                w.append((s_sc, x_free[(gt - 2, 1)]))
            P.op("scalar", lambda e, pp=pp: e.activation(
                out=rstd1[pp][:], in_=lnt1[:], func=AF.Exp, scale=-0.5),
                waits=w, inc=s_sc)
            P.op("vector", lambda e, pp=pp: e.scalar_tensor_tensor(
                out=nmr1[pp][:], in0=mv1[pp][:, 0:1], scalar=-1.0,
                in1=rstd1[pp][:], op0=ALU.mult, op1=ALU.mult),
                waits=[(s_sc, s_sc.n)], inc=s_ln)
            nmr_v = s_ln.n
            for hf in range(2):
                P.op("scalar", lambda e, hf=hf, pp=pp: e.activation(
                    out=ln1h[:], in_=x_sb[hf][:], func=AF.Identity,
                    bias=nmr1[pp][:, 0:1], scale=rstd1[pp][:, 0:1]),
                    waits=[(s_ln, nmr_v), (s_tp, s_tp.n)], inc=s_sc)
                x_free[(gt, hf)] = s_sc.n
                lh = s_sc.n
                for grp in range(2):
                    for k in range(8):
                        c_loc = grp * 8 + k
                        w = [(s_sc, lh)]
                        if k == 0:
                            w.append((s_te, s_te.n))
                        P.op("tensor", lambda e, c_loc=c_loc, k=k: e.transpose(
                            ps_t[:, k * 128:(k + 1) * 128],
                            in_=ln1h[:, c_loc * 128:(c_loc + 1) * 128],
                            identity=identb[:]),
                            waits=w, inc=(s_tp if k == 7 else None))
                    c0 = hf * 16 + grp * 8
                    P.op("scalar", lambda e, c0=c0, tt=tt: e.copy(
                        out=ln1T[:, c0:c0 + 8, tt * 128:(tt + 1) * 128],
                        in_=ps_t[:].rearrange("p (c f) -> p c f", c=8)),
                        waits=[(s_tp, s_tp.n)], inc=s_te)
        ln1_done = s_te.n

        for ft in range(32):
            w = [(s_te, ln1_done)]
            if wu_lidx >= 2:
                w.append((s_mm, wu_free[wu_lidx - 2]))
            my_wu = wu_lidx % 2
            P.op("gpsimd", lambda e, ft=ft, my_wu=my_wu: e.dma_start(
                out=w_u[my_wu][:, 0:4096], in_=wqk[ft][:, 0:4096]),
                waits=w, inc16=s_wp[my_wu])
            wld = s_wp[my_wu].n
            wu_lidx += 1
            for tb in range(4):
                pq = [ps_a, ps_b][grp_idx % 2]
                w = [(s_wp[my_wu], wld), (s_te, ln1_done)]
                if grp_idx >= 2:
                    w.append((s_ev, grp_ev[grp_idx - 2]))
                for c in range(DC):
                    P.op("tensor", lambda e, c=c, tb=tb, pq=pq, my_wu=my_wu:
                         e.matmul(pq[:, 0:512],
                                  lhsT=w_u[my_wu][:, c * 128:(c + 1) * 128],
                                  rhs=ln1T[:, c, tb * 512:(tb + 1) * 512],
                                  start=(c == 0), stop=(c == DC - 1)),
                         waits=(w if c == 0 else ()),
                         inc=(s_mm if c == DC - 1 else None))
                wu_free[wu_lidx - 1] = s_mm.n
                qe = len(qk_store_v)
                w = [(s_mm, s_mm.n)]
                if qe >= 2:
                    w.append((s_stp[qe % 2], qk_store_v[qe - 2]))
                P.op("scalar", lambda e, ft=ft, pq=pq, i=qe % 2: e.activation(
                    out=qkst[i][:], in_=pq[:, 0:512], func=AF.Identity,
                    bias=qkvb_sb[:, ft:ft + 1]),
                    waits=w, inc=s_ev)
                grp_ev.append(s_ev.n)
                grp_idx += 1
                dst = qT_s if ft < 16 else kT_s
                hh = ft % 16
                P.op("sync", lambda e, dst=dst, hh=hh, t0=t0, tb=tb, i=qe % 2:
                     e.dma_start(
                         out=dst[hh, :, t0 + tb * 512:t0 + (tb + 1) * 512],
                         in_=qkst[i][:]),
                     waits=[(s_ev, s_ev.n)], inc16=s_stp[qe % 2])
                qk_store_v.append(s_stp[qe % 2].n)

        for fb in range(8):
            w = [(s_te, ln1_done)]
            if wu_lidx >= 2:
                w.append((s_mm, wu_free[wu_lidx - 2]))
            my_wu = wu_lidx % 2
            P.op("gpsimd", lambda e, fb=fb, my_wu=my_wu: e.dma_start(
                out=w_u[my_wu][:], in_=wv[fb][:]),
                waits=w, inc16=s_wp[my_wu])
            wld = s_wp[my_wu].n
            wu_lidx += 1
            for tt in range(16):
                pq = [ps_a, ps_b][grp_idx % 2]
                w = [(s_wp[my_wu], wld)]
                if grp_idx >= 2:
                    w.append((s_ev, grp_ev[grp_idx - 2]))
                for c in range(DC):
                    P.op("tensor", lambda e, c=c, tt=tt, pq=pq, my_wu=my_wu:
                         e.matmul(pq[:, 0:256],
                                  lhsT=ln1T[:, c, tt * 128:(tt + 1) * 128],
                                  rhs=w_u[my_wu][:, c * 256:(c + 1) * 256],
                                  start=(c == 0), stop=(c == DC - 1)),
                         waits=(w if c == 0 else ()),
                         inc=(s_mm if c == DC - 1 else None))
                wu_free[wu_lidx - 1] = s_mm.n
                ve = len(v_store_v)
                w = [(s_mm, s_mm.n), (s_te, vbbc_done)]
                if ve >= 2:
                    w.append((s_vsp[ve % 2], v_store_v[ve - 2]))
                P.op("vector", lambda e, fb=fb, pq=pq, i=ve % 2: e.tensor_tensor(
                    out=vst[i][:], in0=pq[:, 0:256],
                    in1=vb_bc[:, fb * 256:(fb + 1) * 256], op=ALU.add),
                    waits=w, inc=s_ev)
                grp_ev.append(s_ev.n)
                grp_idx += 1
                ktile = tsb * 16 + tt
                for hh in range(2):
                    head = fb * 2 + hh
                    P.op("sync", lambda e, head=head, ktile=ktile, hh=hh, i=ve % 2:
                         e.dma_start(out=v_s[head, :, ktile, :],
                                     in_=vst[i][:, hh * 128:(hh + 1) * 128]),
                         waits=[(s_ev, s_ev.n)], inc16=s_vsp[ve % 2])
                v_store_v.append(s_vsp[ve % 2].n)

    P.op("sync", lambda e: e.nop(),
         waits=[(s_stp[0], s_stp[0].n), (s_stp[1], s_stp[1].n),
                (s_vsp[0], s_vsp[0].n), (s_vsp[1], s_vsp[1].n)], inc=s_bar)
    P.op("tensor", lambda e: e.nop(), waits=[(s_mm, s_mm.n)], inc=s_bar)
    P.op("scalar", lambda e: e.nop(), waits=[(s_ev, s_ev.n)], inc=s_bar)
    P.op("vector", lambda e: e.nop(), waits=[(s_ev, s_ev.n)], inc=s_bar)
    P.op("gpsimd", lambda e: e.nop(),
         waits=[(s_wp[0], s_wp[0].n), (s_wp[1], s_wp[1].n)], inc=s_bar)
    bar1 = s_bar.n
    p1.close()

    p2 = contextlib.ExitStack()
    probsT2 = [sbuf(f"probsT{i}", [128, 16, 1024], BF16, p2) for i in range(2)]
    kT_sb = [sbuf(f"kT_sb{i}", [128, SL], BF16, p2) for i in range(2)]
    v_sb = [sbuf(f"v_sb{i}", [128, DC, 128], BF16, p2) for i in range(2)]
    qT_sb = [sbuf(f"qT_sb{i}", [128, QB], BF16, p2) for i in range(2)]
    ctxT = sbuf("ctxT", [128, NH, QB], BF16, p2)
    rx = sbuf("rx", [128, NH, QB], BF16, p2)
    ccm = [sbuf(f"ccm{i}", [128, 1024], BF16, p2) for i in range(2)]
    acc4 = sbuf("acc4", [128, 4, 512], BF16, p2)
    dtmp = sbuf("dtmp", [128, 512], F32, p2)
    dfin = sbuf("dfin", [128, 512], F32, p2)
    den_bf = sbuf("den_bf", [128, 512], BF16, p2)
    recip = [sbuf(f"recip{i}", [1, 512], F32, p2) for i in range(2)]
    bcast_sb = [sbuf(f"bcast{i}", [128, 512], F32, p2) for i in range(2)]
    ow_cb = [sbuf(f"ow_cb{i}", [128, 4096], BF16, p2) for i in range(3)]
    xsl = [sbuf(f"xsl{i}", [128, 256], F32, p2) for i in range(4)]
    t1sl = sbuf("t1sl", [128, 256], F32, p2)
    x2st = [sbuf(f"x2st{i}", [128, 256], F32, p2) for i in range(2)]
    stats2 = sbuf("stats2", [128, 4, 96], F32, p2)
    mv2 = [sbuf(f"mv2_{i}", [128, 2], F32, p2) for i in range(4)]
    lnt2 = sbuf("lnt2", [128, 1], F32, p2)
    rstd2 = [sbuf(f"rstd2_{i}", [128, 1], F32, p2) for i in range(4)]
    nmr2 = [sbuf(f"nmr2_{i}", [128, 1], F32, p2) for i in range(4)]
    x2rd = [sbuf(f"x2rd{i}", [128, 512], F32, p2) for i in range(2)]
    ln2h = sbuf("ln2h", [128, 512], BF16, p2)
    ob_bc = sbuf("ob_bc", [128, H], BF16, p2)

    for e_ in ("sync", "tensor", "scalar", "vector", "gpsimd"):
        P.wait(e_, [(s_bar, bar1)])
    emit_row_bcast(ob_row, H, ob_bc)
    obbc_done = s_te.n

    ow_ld = 0
    op_group_v = []
    oe_vals = []
    x2_stores = {}
    l2_stores = {}
    x2_idx = 0
    x2_store_v = []
    ln2_apply_v = []
    ln2_items = []
    ln2_ln_v = {}
    ln2_gt = 0

    spill_v = {}
    ow_state = {"i": 0, "free": {}}
    oe_slot_end = {}
    ccd_hist = []
    n_cc_slots = [0]
    prev_op_end = None

    def emit_ln2_items(n):
        for _ in range(n):
            if not ln2_items:
                return
            qb_, tt, hf, pp, nv, xa, xb = ln2_items.pop(0)
            k = len(ln2_apply_v)
            ii = k % 2
            w = [(s_x2p[0], xa), (s_x2p[1], xb)]
            if k >= 2:
                w.append((s_sc, ln2_apply_v[k - 2]))
            P.op("gpsimd", lambda e, qb_=qb_, tt=tt, hf=hf, ii=ii: e.dma_start(
                out=x2rd[ii][:],
                in_=x2_s[qb_ * 512 + tt * 128:qb_ * 512 + (tt + 1) * 128,
                         hf * 512:(hf + 1) * 512]),
                waits=w, inc16=s_xh[ii])
            xv = s_xh[ii].n
            P.op("scalar", lambda e, ii=ii, pp=pp: e.activation(
                out=ln2h[:], in_=x2rd[ii][:], func=AF.Identity,
                bias=nmr2[pp][:, 0:1], scale=rstd2[pp][:, 0:1]),
                waits=[(s_ln, nv), (s_xh[ii], xv), (s_l2, s_l2.n)],
                inc=s_sc)
            ln2_apply_v.append(s_sc.n)
            P.op("gpsimd", lambda e, qb_=qb_, tt=tt, hf=hf: e.dma_start(
                out=ln2_s[qb_ * 512 + tt * 128:qb_ * 512 + (tt + 1) * 128,
                          hf * 512:(hf + 1) * 512],
                in_=ln2h[:]),
                waits=[(s_sc, s_sc.n)], inc16=s_l2)
            if not ln2_items:
                l2_stores[qb_] = s_l2.n

    for s in range(NSL):
        qb = QORDER[s]
        if s in (0, 2):
            for ob in range((s // 2) * 8, (s // 2) * 8 + 8):
                i = ob % 2
                P.op("gpsimd", lambda e, ob=ob: e.dma_start(
                    out=ow_bf_s[ob][:], in_=ow_h[ob][:]),
                    waits=[(s_bar, bar1)], inc16=s_vsp[i])
                spill_v[ob] = s_vsp[i].n
        if qb < 4:
            j = qb
            w = [(s_cc, j + 1)]
            if op_group_v:
                w.append((s_op, op_group_v[-1]))
            P.op("gpsimd", lambda e, j=j: e.dma_start(
                out=rx[:], in_=cc_out[j].rearrange("(h p) f -> p h f", p=128)),
                waits=w, inc16=s_rx)
        rx_v = s_rx.n
        has_prev_op = op_group_v and prev_op_end is not None

        def emit_den(hh):
            w = [(s_dt, hh + 1)]
            if hh >= 1:
                w.append((s_bcp, hh))
            else:
                w.append((s_te, obbc_done))
            P.op("tensor", lambda e, hh=hh: e.matmul(
                ps_d[0:1, :], lhsT=onesb[:], rhs=den_bf[:], start=True,
                stop=True), waits=w, inc=s_dmm)

        def emit_recip(hh):
            w = [(s_dmm, hh + 1)]
            if hh >= 2:
                w.append((s_bc, hh - 1))
            P.op("vector", lambda e, i=hh % 2: e.reciprocal(
                recip[i][:], ps_d[0:1, :]), waits=w, inc=s_rc)

        def emit_bcast(hh):
            P.op("tensor", lambda e, i=hh % 2: e.matmul(
                ps_d[:], lhsT=onesf[:], rhs=recip[i][:], start=True, stop=True),
                waits=[(s_rc, hh + 1)], inc=s_bc)

        def emit_bcopy(hh):
            w = [(s_bc, hh + 1)]
            if hh >= 2:
                w.append((s_ce, hh - 1))
            P.op("scalar", lambda e, i=hh % 2: e.copy(
                out=bcast_sb[i][:], in_=ps_d[:]), waits=w, inc=s_bcp)

        def emit_evac(hh):
            hloc = hh % NH
            pc2 = [ps_c0, ps_c1][hh % 2]
            P.op("vector", lambda e, hloc=hloc, pc2=pc2, i=hh % 2: e.tensor_tensor(
                ctxT[:, hloc, :], pc2[:], bcast_sb[i][:], op=ALU.mult),
                waits=[(s_ctx, (hh + 1) * 16), (s_bcp, hh + 1)], inc=s_ce)

        def emit_ccm(hh):
            hloc = hh % NH
            ss = QORDER[hh // NH] - 4
            w = [(s_ccd, ccd_hist[-2])] if len(ccd_hist) >= 2 else []
            P.op("vector", lambda e, hloc=hloc, i=hh % 2: e.tensor_scalar(
                out=ccm[i][:, 0:512], in0=ctxT[:, hloc, :],
                scalar1=masks[:, 0:1], scalar2=None, op0=ALU.mult),
                waits=w)
            P.op("vector", lambda e, hloc=hloc, i=hh % 2: e.tensor_scalar(
                out=ccm[i][:, 512:1024], in0=ctxT[:, hloc, :],
                scalar1=masks[:, 1:2], scalar2=None, op0=ALU.mult),
                inc=s_ccm)
            P.op("gpsimd", lambda e, ss=ss, hloc=hloc, i=hh % 2: e.dma_start(
                out=cc_in[ss][hloc * 128:(hloc + 1) * 128, :],
                in_=ccm[i][:, 0:512]),
                waits=[(s_ccm, s_ccm.n)], inc16=s_ccd)
            P.op("gpsimd", lambda e, ss=ss, hloc=hloc, i=hh % 2: e.dma_start(
                out=cc_in[ss][TOWN + hloc * 128:TOWN + (hloc + 1) * 128, :],
                in_=ccm[i][:, 512:1024]), inc16=s_ccd)
            ccd_hist.append(s_ccd.n)

        def emit_back(hh):
            emit_evac(hh)
            if QORDER[hh // NH] >= 4:
                emit_ccm(hh)

        for h in range(NH):
            hg = s * NH + h
            par = hg % 2
            w = [(s_ctx, (hg - 1) * 16)] if hg >= 2 else []
            P.op("sync", lambda e, h=h, par=par: e.dma_start(
                out=kT_sb[par][:], in_=kT_s[h]), waits=w, inc16=s_kvp[par])
            P.op("sync", lambda e, h=h, par=par: e.dma_start(
                out=v_sb[par][:].rearrange("p c f -> p (c f)"),
                in_=v_s[h].rearrange("p c f -> p (c f)")), inc16=s_kvp[par])
            P.op("sync", lambda e, h=h, par=par, qb=qb: e.dma_start(
                out=qT_sb[par][:], in_=qT_s[h, :, qb * QB:(qb + 1) * QB]),
                inc16=s_kvp[par])
            kvv = s_kvp[par].n

            sc_base = s_mm.n
            for gk in range(16):
                G = hg * 16 + gk
                pq = [ps_a, ps_b][G % 2]
                w = [(s_kvp[par], kvv)]
                if G >= 2:
                    w.append((s_exp, G - 1))
                if gk == 0 and h == 0 and prev_op_end is not None:
                    w.append((s_oe, prev_op_end))
                for half in range(2):
                    kt = gk * 2 + half
                    P.op("tensor", lambda e, kt=kt, half=half, par=par, pq=pq:
                         e.matmul(pq[:, half * 512:(half + 1) * 512],
                                  lhsT=kT_sb[par][:, kt * 128:(kt + 1) * 128],
                                  rhs=qT_sb[par][:], start=True, stop=True),
                         waits=(w if half == 0 else ()),
                         inc=(s_mm if half == 1 else None))
            if hg >= 1 and (hg - 1) % NH != NH - 1:
                emit_den(hg - 1)

            for gk in range(16):
                G = hg * 16 + gk
                pq = [ps_a, ps_b][G % 2]
                w = [(s_mm, sc_base + gk + 1)]
                if hg >= 2:
                    w.append((s_ctx, (hg - 2) * 16 + gk + 1))
                    w.append((s_dt, hg - 1))
                P.op("scalar", lambda e, gk=gk, pq=pq, pb=hg % 2: e.activation(
                    out=probsT2[pb][:, gk, :], in_=pq[:], func=AF.Exp, scale=SCALE),
                    waits=w, inc=s_exp)
            if hg >= 1 and (hg - 1) % NH != NH - 1:
                emit_bcopy(hg - 1)
            emit_ln2_items(2)

            if hg >= 2 and (hg - 2) % NH < NH - 2:
                emit_back(hg - 2)
            if hg >= 1 and (hg - 1) % NH != NH - 1:
                emit_recip(hg - 1)
            for gk in range(16):
                G = hg * 16 + gk
                q = gk // 4
                w = [(s_exp, G + 1)]
                if gk % 4 == 0:
                    P.op("vector", lambda e, gk=gk, q=q, pb=hg % 2: e.tensor_tensor(
                        acc4[:, q, :], probsT2[pb][:, gk, 0:512],
                        probsT2[pb][:, gk, 512:1024], op=ALU.add), waits=w)
                else:
                    P.op("vector", lambda e, gk=gk, q=q, pb=hg % 2: e.tensor_tensor(
                        acc4[:, q, :], acc4[:, q, :], probsT2[pb][:, gk, 0:512],
                        op=ALU.add), waits=w)
                    P.op("vector", lambda e, gk=gk, q=q, pb=hg % 2: e.tensor_tensor(
                        acc4[:, q, :], acc4[:, q, :], probsT2[pb][:, gk, 512:1024],
                        op=ALU.add))
            P.op("vector", lambda e: e.tensor_tensor(
                dfin[:], acc4[:, 0, :], acc4[:, 1, :], op=ALU.add))
            P.op("vector", lambda e: e.tensor_tensor(
                dtmp[:], acc4[:, 2, :], acc4[:, 3, :], op=ALU.add))
            P.op("vector", lambda e: e.tensor_tensor(
                dfin[:], dfin[:], dtmp[:], op=ALU.add))
            w = [(s_dmm, hg)] if hg >= 1 else []
            P.op("vector", lambda e: e.tensor_copy(den_bf[:], dfin[:]),
                 waits=w, inc=s_dt)

            pc = [ps_c0, ps_c1][hg % 2]
            for p in range(16):
                w = [(s_exp, hg * 16 + p + 1)]
                if p == 0 and hg >= 2:
                    w.append((s_ce, hg - 1))
                if p == 0 and h < 2 and prev_op_end is not None:
                    w.append((s_oe, prev_op_end))
                for half in range(2):
                    kt = p * 2 + half
                    P.op("tensor", lambda e, kt=kt, p=p, half=half, par=par,
                         pc=pc, pb=hg % 2: e.matmul(
                             pc[:], lhsT=v_sb[par][:, kt, :],
                             rhs=probsT2[pb][:, p, half * 512:(half + 1) * 512],
                             start=(kt == 0), stop=(kt == 31)),
                         waits=(w if half == 0 else ()),
                         inc=(s_ctx if half == 1 else None))
            if hg >= 1 and (hg - 1) % NH != NH - 1:
                emit_bcast(hg - 1)

        hg_last = s * NH + NH - 1
        emit_den(hg_last)
        emit_recip(hg_last)
        emit_bcast(hg_last)
        emit_bcopy(hg_last)
        emit_back(hg_last - 1)
        emit_back(hg_last)

        if qb >= 4:
            n_cc_slots[0] += 1
            cj = qb - 4
            P.op("gpsimd", lambda e, cj=cj: e.collective_compute(
                "ReduceScatter", ALU.add, ins=[cc_in[cj][:]],
                outs=[cc_out[cj][:]], replica_groups=GROUPS),
                waits=[(s_ccd, n_cc_slots[0] * NH * 32)], inc=s_cc)
        else:
            j = qb
            equads = [(ps_a, 0, 256), (ps_a, 512, 256), (ps_b, 0, 256),
                      (ps_b, 512, 256)]
            oquads = [(ps_c0, 0, 256), (ps_c0, 256, 256), (ps_c1, 0, 256),
                      (ps_c1, 256, 256)]
            slot_exp = (s + 1) * NH * 16
            slot_ce = (s + 1) * NH
            oe_slot0 = s_oe.n
            use_spill = s >= 4
            ow_sems = [s_wp[0], s_wp[1], s_row]
            for ob in range(16):
                quads = equads if ob % 2 == 0 else oquads
                hb = {}
                hv = {}
                for half in range(2):
                    oi = ow_state["i"]
                    ow_state["i"] += 1
                    bi = oi % 3
                    hb[half] = bi
                    w = []
                    if bi in ow_state["free"]:
                        w.append(ow_state["free"][bi])
                    if use_spill and ob < 2 and half == 0:
                        w.append((s_vsp[0], s_vsp[0].n))
                        w.append((s_vsp[1], s_vsp[1].n))
                    osrc = ow_bf_s[ob] if use_spill else ow_h[ob]
                    P.op("gpsimd", lambda e, osrc=osrc, bi=bi, half=half:
                         e.dma_start(out=ow_cb[bi][:],
                                     in_=osrc[:, half * 4096:(half + 1) * 4096]),
                         waits=w, inc16=ow_sems[bi])
                    hv[half] = ow_sems[bi].n
                ow_ld += 1
                for tt in range(4):
                    w = []
                    if len(oe_vals) >= 4:
                        w.append((s_oe, oe_vals[len(oe_vals) - 4 + tt]))
                    P.op("sync", lambda e, qb=qb, tt=tt, ob=ob:
                         e.dma_start(
                             out=xsl[tt][:],
                             in_=x_l[qb * 512 + tt * 128:qb * 512 + (tt + 1) * 128,
                                     ob * 256:(ob + 1) * 256]),
                         waits=w, inc16=s_xsl)
                xslv = s_xsl.n
                w = [(ow_sems[hb[0]], hv[0]), (ow_sems[hb[1]], hv[1]),
                     (s_exp, slot_exp), (s_ce, slot_ce), (s_rx, rx_v)]
                if ob >= 2:
                    w.append((s_oe, oe_vals[-5]))
                for cf in range(DC):
                    src_ = ctxT if cf < 16 else rx
                    hh = cf % 16
                    bi = hb[cf // 16]
                    for tt in range(4):
                        pq, off, _qw = quads[tt]
                        P.op("tensor", lambda e, src_=src_, hh=hh, tt=tt, cf=cf,
                             pq=pq, off=off, bi=bi: e.matmul(
                                 pq[:, off:off + 256],
                                 lhsT=src_[:, hh, tt * 128:(tt + 1) * 128],
                                 rhs=ow_cb[bi][:, (cf % 16) * 256:
                                               (cf % 16 + 1) * 256],
                                 start=(cf == 0), stop=(cf == DC - 1)),
                             waits=(w if (cf == 0 and tt == 0) else ()),
                             inc=(s_op if (tt == 3 and cf in (15, DC - 1))
                                  else None))
                    if cf == 15:
                        ow_state["free"][hb[0]] = (s_op, s_op.n)
                op_group_v.append(s_op.n)
                ow_state["free"][hb[1]] = (s_op, s_op.n)
                for tt in range(4):
                    gi = ob * 4 + tt
                    pq, off, _qw = quads[tt]
                    P.op("vector", lambda e, pq=pq, off=off, ob=ob: e.tensor_tensor(
                        t1sl[:], pq[:, off:off + 256],
                        ob_bc[:, ob * 256:(ob + 1) * 256], op=ALU.add),
                        waits=[(s_op, s_op.n), (s_xsl, xslv)])
                    ii = gi % 2
                    w2 = [(s_x2p[x2_idx % 2], x2_store_v[x2_idx - 2])] \
                        if x2_idx >= 2 else []
                    P.op("vector", lambda e, ii=ii, tt=tt: e.tensor_tensor(
                        x2st[ii][:], t1sl[:], xsl[tt][:], op=ALU.add), waits=w2)
                    P.op("vector", lambda e, ii=ii, tt=tt, ob=ob: e.bn_stats(
                        stats2[:, tt, ob * 6:(ob + 1) * 6], x2st[ii][:]),
                        inc=s_oe)
                    oe_vals.append(s_oe.n)
                    P.op("sync", lambda e, ii=ii, qb=qb, tt=tt, ob=ob: e.dma_start(
                        out=x2_s[qb * 512 + tt * 128:qb * 512 + (tt + 1) * 128,
                                 ob * 256:(ob + 1) * 256],
                        in_=x2st[ii][:]),
                        waits=[(s_oe, s_oe.n)], inc16=s_x2p[x2_idx % 2])
                    x2_store_v.append(s_x2p[x2_idx % 2].n)
                    x2_idx += 1
            x2_stores[qb] = (s_x2p[0].n, s_x2p[1].n)
            oe_slot_end[s] = s_oe.n
            prev_op_end = s_oe.n
            for tt in range(4):
                pp = ln2_gt % 4
                w = [(s_oe, s_oe.n)]
                if ln2_gt >= 4:
                    w.append((s_sc, ln2_ln_v[ln2_gt - 4]))
                P.op("vector", lambda e, tt=tt, pp=pp: e.bn_aggr(
                    mv2[pp][:], stats2[:, tt, :]), waits=w, inc=s_ln)
                P.op("scalar", lambda e, pp=pp: e.activation(
                    out=lnt2[:], in_=mv2[pp][:, 1:2], func=AF.Ln,
                    bias=eps_sb[:, 0:1]),
                    waits=[(s_ln, s_ln.n)], inc=s_sc)
                ln2_ln_v[ln2_gt] = s_sc.n
                w = [(s_sc, s_sc.n)]
                if ln2_gt >= 4:
                    w.append((s_sc, ln2_apply_v[8 * (ln2_gt - 4) + 7]))
                P.op("scalar", lambda e, pp=pp: e.activation(
                    out=rstd2[pp][:], in_=lnt2[:], func=AF.Exp, scale=-0.5),
                    waits=w, inc=s_sc)
                P.op("vector", lambda e, pp=pp: e.scalar_tensor_tensor(
                    out=nmr2[pp][:], in0=mv2[pp][:, 0:1], scalar=-1.0,
                    in1=rstd2[pp][:], op0=ALU.mult, op1=ALU.mult),
                    waits=[(s_sc, s_sc.n)], inc=s_ln)
                nv = s_ln.n
                ln2_gt += 1
                for hf in range(8):
                    ln2_items.append((qb, tt, hf, pp, nv,
                                      x2_stores[qb][0], x2_stores[qb][1]))

    emit_ln2_items(len(ln2_items))

    P.op("sync", lambda e: e.nop(),
         waits=[(s_l2, s_l2.n), (s_x2p[0], s_x2p[0].n),
                (s_x2p[1], s_x2p[1].n)], inc=s_bar)
    P.op("tensor", lambda e: e.nop(), waits=[(s_op, s_op.n)], inc=s_bar)
    P.op("scalar", lambda e: e.nop(), waits=[(s_bcp, s_bcp.n)], inc=s_bar)
    P.op("vector", lambda e: e.nop(), waits=[(s_ln, s_ln.n)], inc=s_bar)
    P.op("gpsimd", lambda e: e.nop(), waits=[(s_cc, 4)], inc=s_bar)
    bar2 = s_bar.n
    p2.close()

    p3 = contextlib.ExitStack()
    ln2T = sbuf("ln2T", [128, DC, 1024], BF16, p3)
    ln2rd = [sbuf(f"ln2rd{i}", [128, H], BF16, p3) for i in range(2)]
    WT = [sbuf(f"WT{i}", [128, DC, OB3], BF16, p3) for i in range(2)]
    wrow = [sbuf(f"wrow{i}", [128, 2048], F32, p3) for i in range(4)]
    x2sl = [sbuf(f"x2sl{i}", [128, OB3], F32, p3) for i in range(2)]
    outst = [sbuf(f"outst{i}", [128, OB3], F32, p3) for i in range(2)]
    t1sl3 = sbuf("t1sl3", [128, OB3], F32, p3)
    b_bc = sbuf("b_bc", [128, H], BF16, p3)
    a_bf = sbuf("a_bf", [64, H], BF16, p3)
    AT_sb = sbuf("AT_sb", [128, DC, 64], BF16, p3)
    b2_bf = sbuf("b2_bf", [128, DC * 64], BF16, p3)
    BT_sb = sbuf("BT_sb", [64, H], BF16, p3)
    gw_bf = sbuf("gw_bf", [4, H], BF16, p3)
    gwT_sb = sbuf("gwT_sb", [128, DC, 4], BF16, p3)
    h1T = sbuf("h1T", [64, 1024], BF16, p3)
    mT_sb = sbuf("mT_sb", [64, 1024], BF16, p3)
    m_sb = sbuf("m_sb", [64, 1024], BF16, p3)
    g_f32 = sbuf("g_f32", [4, 512], F32, p3)
    gn_sb = sbuf("gn_sb", [128, 16], F32, p3)
    gm1 = sbuf("gm1", [128, 4], F32, p3)
    ga = sbuf("ga", [128, 1], F32, p3)
    maskn = sbuf("maskn", [128, 64], BF16, p3)

    for e_ in ("sync", "tensor", "scalar", "vector", "gpsimd"):
        P.wait(e_, [(s_bar, bar2)])

    emit_row_bcast(b_row, H, b_bc)

    P.op("gpsimd", lambda e: e.dma_start(out=a_bf[:], in_=A_in[:]), inc16=s_wr)
    P.op("gpsimd", lambda e: e.dma_start(out=b2_bf[:], in_=B_in[:]), inc16=s_wr)
    P.op("gpsimd", lambda e: e.dma_start(out=gw_bf[:], in_=gw_in[:]), inc16=s_wr)
    pre_w = s_wr.n

    for grp in range(2):
        for k in range(16):
            c = grp * 16 + k
            w = [(s_wr, pre_w)]
            if k == 0:
                w.append((s_te, s_te.n))
            P.op("tensor", lambda e, c=c, k=k: e.transpose(
                ps_t[:, k * 64:(k + 1) * 64],
                in_=a_bf[0:64, c * 128:(c + 1) * 128],
                identity=identb[0:64, 0:64]),
                waits=w, inc=(s_tp if k == 15 else None))
        P.op("scalar", lambda e, grp=grp: e.copy(
            out=AT_sb[:, grp * 16:(grp + 1) * 16, :],
            in_=ps_t[:].rearrange("p (c f) -> p c f", c=16)),
            waits=[(s_tp, s_tp.n)], inc=s_te)
    for grp in range(4):
        for k in range(8):
            c = grp * 8 + k
            w = [(s_wr, pre_w)]
            if k == 0:
                w.append((s_te, s_te.n))
            P.op("tensor", lambda e, c=c, k=k: e.transpose(
                ps_t[0:64, k * 128:(k + 1) * 128],
                in_=b2_bf[:, c * 64:(c + 1) * 64], identity=identb[:]),
                waits=w, inc=(s_tp if k == 7 else None))
        P.op("scalar", lambda e, grp=grp: e.copy(
            out=BT_sb[0:64, grp * 1024:(grp + 1) * 1024], in_=ps_t[0:64, :]),
            waits=[(s_tp, s_tp.n)], inc=s_te)
    for k in range(DC):
        w = [(s_wr, pre_w)]
        if k == 0:
            w.append((s_te, s_te.n))
        P.op("tensor", lambda e, k=k: e.transpose(
            ps_t[:, k * 4:(k + 1) * 4], in_=gw_bf[0:4, k * 128:(k + 1) * 128],
            identity=identb[0:4, 0:4]),
            waits=w, inc=(s_tp if k == DC - 1 else None))
    P.op("scalar", lambda e: e.copy(
        out=gwT_sb[:], in_=ps_t[:, 0:128].rearrange("p (c f) -> p c f", c=32)),
        waits=[(s_tp, s_tp.n)], inc=s_te)

    ffn_grp = 0
    wtb_state = {"g": 0, "bcp": {}}
    wrow_half_tp = {}
    nonlocal_state = {"ffn_grp": 0, "out_ct": 0}
    pend_mm = None
    out_store_v = []
    g_tp_v = {}
    gn_free_v = {}
    ffn_grp_mm = []
    wrow_tp = {}
    out_ct = 0
    ln2rd_tp = {}

    for th in range(2):
        tbase = th * 1024
        if pend_mm is not None:
            pend_mm()
            pend_mm = None
        for tt in range(8):
            gti = th * 8 + tt
            row = tbase + tt * 128
            qb_src = row // 512
            w = [(s_l2, l2_stores[qb_src])]
            if gti >= 2:
                w.append((s_tp, ln2rd_tp[gti - 2]))
            P.op("sync", lambda e, row=row, i=tt % 2: e.dma_start(
                out=ln2rd[i][:], in_=ln2_s[row:row + 128, :]),
                waits=w, inc16=s_xh[tt % 2])
            xv = s_xh[tt % 2].n
            for grp in range(4):
                for k in range(8):
                    c = grp * 8 + k
                    w2 = [(s_xh[tt % 2], xv)]
                    if k == 0:
                        w2.append((s_te, s_te.n))
                    P.op("tensor", lambda e, c=c, k=k, i=tt % 2: e.transpose(
                        ps_t[:, k * 128:(k + 1) * 128],
                        in_=ln2rd[i][:, c * 128:(c + 1) * 128],
                        identity=identb[:]),
                        waits=w2, inc=(s_tp if k == 7 else None))
                P.op("scalar", lambda e, grp=grp, tt=tt: e.copy(
                    out=ln2T[:, grp * 8:(grp + 1) * 8, tt * 128:(tt + 1) * 128],
                    in_=ps_t[:].rearrange("p (c f) -> p c f", c=8)),
                    waits=[(s_tp, s_tp.n)], inc=s_te)
            ln2rd_tp[gti] = s_tp.n
        ln2T_done = s_te.n

        for tq in range(2):
            gtq = th * 2 + tq
            gate_ps_free = [(s_ev, s_ev.n)] if (th > 0 or tq > 0) else []
            if th > 0 and tq == 0:
                gate_ps_free.append((s_bcp, s_bcp.n))
            for c in range(DC):
                w = [(s_te, ln2T_done)] + (gate_ps_free if c == 0 else [])
                P.op("tensor", lambda e, c=c, tq=tq: e.matmul(
                    ps_c0[0:4, :], lhsT=gwT_sb[:, c, :],
                    rhs=ln2T[:, c, tq * 512:(tq + 1) * 512],
                    start=(c == 0), stop=(c == DC - 1)),
                    waits=(w if c == 0 else ()),
                    inc=(s_mm if c == DC - 1 else None))
            w = [(s_mm, s_mm.n)]
            if gtq >= 1:
                w.append((s_tp, g_tp_v[gtq - 1]))
            P.op("vector", lambda e: e.tensor_scalar(
                out=g_f32[:], in0=ps_c0[0:4, :], scalar1=gb_sb[0:4, 0:1],
                scalar2=None, op0=ALU.add),
                waits=w, inc=s_ev)
            gev = s_ev.n
            for k in range(4):
                w = [(s_ev, gev), (s_bcp, s_bcp.n)]
                if k > 0:
                    w = []
                P.op("tensor", lambda e, k=k: e.transpose(
                    ps_d[:, k * 4:(k + 1) * 4],
                    in_=g_f32[0:4, k * 128:(k + 1) * 128],
                    identity=identf[0:4, 0:4]),
                    waits=w, inc=(s_tp if k == 3 else None))
            g_tp_v[gtq] = s_tp.n
            w = [(s_tp, s_tp.n)]
            if gtq >= 1:
                w.append((s_ln, gn_free_v[gtq - 1]))
            P.op("scalar", lambda e: e.copy(out=gn_sb[:], in_=ps_d[:, 0:16]),
                 waits=w, inc=s_bcp)
            gnv = s_bcp.n
            for k in range(4):
                base = k * 4
                P.op("vector", lambda e, base=base: e.reduce_max(
                    gm1[:, 0:1], gn_sb[:, base + 1:base + 4], axis=AX.X),
                    waits=[(s_bcp, gnv)])
                P.op("vector", lambda e, base=base: e.reduce_max(
                    gm1[:, 1:2], gn_sb[:, base + 2:base + 4], axis=AX.X))
                P.op("vector", lambda e, base=base: e.reduce_max(
                    gm1[:, 2:3], gn_sb[:, base + 0:base + 2], axis=AX.X))
                P.op("vector", lambda e, base=base: e.reduce_max(
                    gm1[:, 3:4], gn_sb[:, base + 0:base + 3], axis=AX.X),
                    inc=s_ln)
                mxv = s_ln.n
                P.op("vector", lambda e, base=base: e.tensor_tensor(
                    ga[:], gm1[:, 0:1], gn_sb[:, base:base + 1], op=ALU.is_gt),
                    waits=[(s_ln, mxv)])
                P.op("vector", lambda e: e.tensor_tensor(
                    gm1[:, 0:1], gm1[:, 1:2], gm1[:, 2:3], op=ALU.is_gt))
                P.op("vector", lambda e, base=base: e.tensor_tensor(
                    gm1[:, 1:2], gn_sb[:, base + 3:base + 4], gm1[:, 3:4],
                    op=ALU.is_gt), inc=s_ln)
                cv = s_ln.n
                gn_free_v[gtq] = cv
                P.op("vector", lambda e: e.tensor_tensor(
                    gm1[:, 2:3], gm1[:, 0:1], gm1[:, 1:2], op=ALU.add),
                    waits=[(s_ln, cv)], inc=s_ln)
                sumv = s_ln.n
                P.op("vector", lambda e: e.tensor_tensor(
                    ga[:], ga[:], gm1[:, 2:3], op=ALU.add),
                    waits=[(s_ln, sumv)], inc=s_ln)
                av = s_ln.n
                P.op("vector", lambda e: e.tensor_scalar(
                    out=maskn[:], in0=levels[:], scalar1=ga[:, 0:1],
                    scalar2=None, op0=ALU.is_le),
                    waits=[(s_ln, av), (s_te, s_te.n)], inc=s_ln)
                mk = s_ln.n
                P.op("tensor", lambda e: e.transpose(
                    ps_t[0:64, 0:128], in_=maskn[:], identity=identb[:]),
                    waits=[(s_ln, mk), (s_te, s_te.n)], inc=s_tp)
                P.op("scalar", lambda e, tq=tq, k=k: e.copy(
                    out=mT_sb[0:64, (tq * 4 + k) * 128:(tq * 4 + k + 1) * 128],
                    in_=ps_t[0:64, 0:128]),
                    waits=[(s_tp, s_tp.n)], inc=s_te)
            h1_ps_free = [(s_ev, s_ev.n)] if (th > 0 or tq > 0) else []
            if th > 0 and tq == 0:
                h1_ps_free.append((s_bcp, s_bcp.n))
            for c in range(DC):
                w = [(s_te, ln2T_done)] + (h1_ps_free if c == 0 else [])
                P.op("tensor", lambda e, c=c, tq=tq: e.matmul(
                    ps_c1[0:64, :], lhsT=AT_sb[:, c, :],
                    rhs=ln2T[:, c, tq * 512:(tq + 1) * 512],
                    start=(c == 0), stop=(c == DC - 1)),
                    waits=(w if c == 0 else ()),
                    inc=(s_mm if c == DC - 1 else None))
            P.op("scalar", lambda e, tq=tq: e.copy(
                out=h1T[0:64, tq * 512:(tq + 1) * 512], in_=ps_c1[0:64, :]),
                waits=[(s_mm, s_mm.n)], inc=s_ev)
        P.op("vector", lambda e: e.tensor_tensor(
            m_sb[0:64, :], h1T[0:64, :], mT_sb[0:64, :], op=ALU.mult),
            waits=[(s_ev, s_ev.n), (s_te, s_te.n)], inc=s_ln)
        msk_done = s_ln.n

        for ob in range(NOB3):
            gob = th * NOB3 + ob
            wldv = {}
            for li in range(4):
                osb, dhalf = li // 2, li % 2
                gli = gob * 4 + li
                w = []
                if gli >= 4:
                    w.append((s_tp, wrow_half_tp[gli - 4]))
                P.op("sync", lambda e, ob=ob, osb=osb, dhalf=dhalf, li=li:
                     e.dma_start(
                         out=wrow[li][:],
                         in_=Wn[ob * 256 + osb * 128:ob * 256 + (osb + 1) * 128,
                                dhalf * 2048:(dhalf + 1) * 2048]),
                     waits=w, inc16=s_wr)
                wldv[li] = s_wr.n
            wti = gob % 2
            wtbanks = [ps_d, ps_c0, ps_c1]
            for osb in range(2):
                for grp in range(8):
                    wg = wtb_state["g"]
                    wtb_state["g"] += 1
                    bank = wtbanks[wg % 3]
                    li = osb * 2 + grp // 4
                    for k in range(4):
                        c = grp * 4 + k
                        w = [(s_wr, wldv[li])]
                        if k == 0 and wg >= 3:
                            w.append((s_bcp, wtb_state["bcp"][wg - 3]))
                        if k == 0 and (wg % (NOB3 * 16)) < 3:
                            w.append((s_ev, s_ev.n))
                            w.append((s_bcp, s_bcp.n))
                        if k == 0 and grp == 0 and osb == 0 and gob >= 2:
                            w.append((s_mm, ffn_grp_mm[(gob - 2) * 8 + 7]))
                        P.op("tensor", lambda e, c=c, k=k, li=li, bank=bank:
                             e.transpose(
                                 bank[:, k * 128:(k + 1) * 128],
                                 in_=wrow[li][:, (c % 16) * 128:
                                              (c % 16 + 1) * 128],
                                 identity=identf[:]),
                             waits=w, inc=(s_tp if k == 3 else None))
                    P.op("scalar", lambda e, grp=grp, osb=osb, wti=wti, bank=bank:
                         e.copy(
                             out=WT[wti][:, grp * 4:(grp + 1) * 4,
                                         osb * 128:(osb + 1) * 128],
                             in_=bank[:].rearrange("p (c f) -> p c f", c=4)),
                         waits=[(s_tp, s_tp.n)], inc=s_bcp)
                    wtb_state["bcp"][wg] = s_bcp.n
                    if grp % 4 == 3:
                        wrow_half_tp[gob * 4 + osb * 2 + grp // 4] = s_tp.n
            wrow_tp[gob] = s_tp.n
            wt_done = s_bcp.n

            def mk_mm(ob, tbase, wt_done, wti, msk_done):
                def go():
                    global_unused = None
                    for tt in range(8):
                        nonlocal_state["ffn_grp"] += 1
                        fg = nonlocal_state["ffn_grp"] - 1
                        pq = [ps_a, ps_b][fg % 2]
                        w = [(s_bcp, wt_done), (s_ln, msk_done)]
                        if fg >= 2:
                            w.append((s_oe, oe_vals[-2]))
                        for c in range(DC):
                            P.op("tensor", lambda e, c=c, tt=tt, pq=pq, wti=wti:
                                 e.matmul(pq[:, 0:OB3],
                                          lhsT=ln2T[:, c, tt * 128:(tt + 1) * 128],
                                          rhs=WT[wti][:, c, :],
                                          start=(c == 0), stop=False),
                                 waits=(w if c == 0 else ()))
                        P.op("tensor", lambda e, tt=tt, ob=ob, pq=pq: e.matmul(
                            pq[:, 0:OB3],
                            lhsT=m_sb[0:64, tt * 128:(tt + 1) * 128],
                            rhs=BT_sb[0:64, ob * 256:(ob + 1) * 256],
                            start=False, stop=True), inc=s_mm)
                        mmv = s_mm.n
                        ffn_grp_mm.append(mmv)
                        ii = (fg + 1) % 2
                        qb_src = (tbase + tt * 128) // 512
                        w2 = [(s_x2p[0], x2_stores[qb_src][0]),
                              (s_x2p[1], x2_stores[qb_src][1])]
                        if fg >= 2:
                            w2.append((s_oe, oe_vals[-2]))
                        P.op("sync", lambda e, tbase=tbase, tt=tt, ob=ob, ii=ii:
                             e.dma_start(
                                 out=x2sl[ii][:],
                                 in_=x2_s[tbase + tt * 128:tbase + (tt + 1) * 128,
                                          ob * 256:(ob + 1) * 256]),
                             waits=w2, inc16=s_wp[ii])
                        xv2 = s_wp[ii].n
                        P.op("vector", lambda e, pq=pq, ob=ob, ii=ii:
                             e.tensor_tensor(
                                 t1sl3[:], pq[:, 0:OB3],
                                 b_bc[:, ob * 256:(ob + 1) * 256],
                                 op=ALU.add), waits=[(s_mm, mmv), (s_wp[ii], xv2)])
                        oc = nonlocal_state["out_ct"]
                        w3 = [(s_stp[oc % 2], out_store_v[oc - 2])] \
                            if oc >= 2 else []
                        P.op("vector", lambda e, ii=ii: e.tensor_tensor(
                            outst[ii][:], t1sl3[:], x2sl[ii][:], op=ALU.add),
                            waits=w3, inc=s_oe)
                        oe_vals.append(s_oe.n)
                        P.op("sync", lambda e, tbase=tbase, tt=tt, ob=ob, ii=ii:
                             e.dma_start(
                                 out=out_d[tbase + tt * 128:tbase + (tt + 1) * 128,
                                           ob * 256:(ob + 1) * 256],
                                 in_=outst[ii][:]),
                             waits=[(s_oe, s_oe.n)], inc16=s_stp[oc % 2])
                        out_store_v.append(s_stp[oc % 2].n)
                        nonlocal_state["out_ct"] += 1
                return go

            new_mm = mk_mm(ob, tbase, wt_done, wti, msk_done)
            if pend_mm is not None:
                pend_mm()
            pend_mm = new_mm
    if pend_mm is not None:
        pend_mm()
        pend_mm = None

    P.wait("sync", [(s_stp[0], s_stp[0].n), (s_stp[1], s_stp[1].n)])

    _check_deadlock(P)

    with nc.Block() as blk:
        for eng_name in ("sync", "gpsimd", "tensor", "scalar", "vector"):
            items = [o for o in P.ops if o[0] == eng_name]

            def make_body(items):
                def body(eng):
                    for _, waits, fn, incs in items:
                        for (hh, v) in waits:
                            eng.wait_ge(hh, v)
                        if fn is None:
                            continue
                        ins = fn(eng)
                        for (hh, amt) in incs:
                            ins = ins.then_inc(hh, amt)
                return body

            getattr(blk, eng_name)(make_body(items))

    p3.close()
    es.close()
    return nc


_CACHE = {}


def _get_program():
    if "nc" not in _CACHE:
        _CACHE["nc"] = build_program()
    return _CACHE["nc"]


def _prep_inputs(c, inputs):
    bidx, gg = c // 2, c % 2
    f32 = np.float32
    x = np.asarray(inputs["L0_attn_x"], f32)
    qkvw = np.asarray(inputs["L0_attn_attn_qkvw"], f32)
    qkvb = np.asarray(inputs["L0_attn_attn_qkvb"], f32)
    ow = np.asarray(inputs["L0_attn_attn_ow"], f32)
    obv = np.asarray(inputs["L0_attn_attn_ob"], f32)
    W = np.asarray(inputs["L0_ffn_W"], f32)
    bv = np.asarray(inputs["L0_ffn_b"], f32)
    A = np.asarray(inputs["L0_ffn_A"], f32)
    Bm = np.asarray(inputs["L0_ffn_B"], f32)
    gw = np.asarray(inputs["L0_ffn_gate_w"], f32)
    gb = np.asarray(inputs["L0_ffn_gate_b"], f32)

    own = slice(2048 * gg, 2048 * gg + 2048)
    oth = slice(2048 * (1 - gg), 2048 * (1 - gg) + 2048)
    x_loc = np.ascontiguousarray(np.concatenate([x[bidx, own], x[bidx, oth]], 0))

    qcols = qkvw[:, 2048 * gg:2048 * gg + 2048]
    kcols = qkvw[:, 4096 + 2048 * gg:4096 + 2048 * gg + 2048]
    qk = np.concatenate([qcols, kcols], 1)
    wqk_h = np.ascontiguousarray(
        qk.reshape(32, 128, 32, 128).transpose(2, 1, 0, 3).reshape(32, 128, -1))
    vcols = qkvw[:, 8192 + 2048 * gg:8192 + 2048 * gg + 2048]
    wv_h = np.ascontiguousarray(
        vcols.reshape(32, 128, 8, 256).transpose(2, 1, 0, 3).reshape(8, 128, -1))
    qkvb_qk = np.ascontiguousarray(np.concatenate(
        [qkvb[2048 * gg:2048 * gg + 2048],
         qkvb[4096 + 2048 * gg:4096 + 2048 * gg + 2048]]).reshape(32, 128).T)
    vb = np.ascontiguousarray(qkvb[8192 + 2048 * gg:8192 + 2048 * gg + 2048])
    ow_p = np.concatenate([ow[2048 * gg:2048 * gg + 2048],
                           ow[2048 * (1 - gg):2048 * (1 - gg) + 2048]], 0)
    ow_hh = np.ascontiguousarray(
        ow_p.reshape(32, 128, 16, 256).transpose(2, 1, 0, 3).reshape(16, 128, -1))
    B_h = np.ascontiguousarray(
        Bm.reshape(32, 128, 64).transpose(1, 0, 2).reshape(128, -1))

    bf = ml_dtypes.bfloat16
    lev = np.array([0] * 8 + [1] * 8 + [2] * 16 + [3] * 32, f32)
    return {
        "x_l": x_loc, "wqk": wqk_h, "wv": wv_h, "qkvb_qk": qkvb_qk,
        "vb_row": vb, "ow_h": ow_hh, "ob_row": obv, "Wn": W, "b_row": bv,
        "A_in": A, "B_in": B_h, "gw_in": gw, "gb_in": gb,
        "identb_in": np.eye(128, dtype=bf),
        "identf_in": np.eye(128, dtype=f32),
        "onesf_in": np.ones((1, 128), f32),
        "onesb_in": np.ones((128, 1), bf),
        "levels_in": np.tile(lev, (128, 1)),
        "masks_in": np.tile(np.array([0.0, 1.0] if gg == 0 else [1.0, 0.0], f32),
                            (128, 1)),
        "eps_in": np.full((128, 1), 1e-5, f32),
    }


def kernel(**inputs):
    nc = _get_program()
    in_maps = [_prep_inputs(c, inputs) for c in range(8)]
    res = run_bass_kernel_spmd(nc, in_maps, core_ids=list(range(8)))
    out = np.empty((4, 4096, 4096), np.float32)
    for c in range(8):
        bidx, gg = c // 2, c % 2
        out[bidx, 2048 * gg:2048 * gg + 2048] = res.results[c]["out"]
    return out


# version 18
# speedup vs baseline: 1.0518x; 1.0518x over previous
import sys

for _p in ("/opt/trn_rl_repo", "/root/.axon_site/_ro/trn_rl_repo"):
    if _p not in sys.path:
        sys.path.append(_p)

import contextlib

import numpy as np
import ml_dtypes

import concourse.bass as bass
import concourse.mybir as mybir
from concourse.bass_utils import run_bass_kernel_spmd

F32 = mybir.dt.float32
BF16 = mybir.dt.bfloat16
AF = mybir.ActivationFunctionType
ALU = mybir.AluOpType
AX = mybir.AxisListType

H = 4096
DC = 32
NH = 16
SL = 4096
TOWN = 2048
QB = 512
NSL = 8
QORDER = [4, 0, 5, 1, 6, 2, 7, 3]
SCALE = float(1.0 / np.sqrt(128.0))
EPS = 1e-5
GROUPS = [[0, 1], [2, 3], [4, 5], [6, 7]]
OB3 = 256
NOB3 = H // OB3


class Sem:
    def __init__(self, handle):
        self.h = handle
        self.n = 0


class Prog:

    def __init__(self):
        self.ops = []

    def op(self, engine, fn, waits=(), inc=None, inc16=None):
        incs = []
        if inc is not None:
            inc.n += 1
            incs.append((inc.h, 1))
        if inc16 is not None:
            inc16.n += 16
            incs.append((inc16.h, 16))
        self.ops.append((engine, tuple((s.h, v) for (s, v) in waits), fn, incs))

    def wait(self, engine, waits):
        self.ops.append((engine, tuple((s.h, v) for (s, v) in waits), None, []))


def _check_deadlock(prog):
    qs = {}
    for it in prog.ops:
        qs.setdefault(it[0], []).append(it)
    heads = {k: 0 for k in qs}
    counts = {}
    progress = True
    while progress:
        progress = False
        for eng, lst in qs.items():
            while heads[eng] < len(lst):
                _, waits, _, incs = lst[heads[eng]]
                if all(counts.get(h, 0) >= v for (h, v) in waits):
                    for (h, amt) in incs:
                        counts[h] = counts.get(h, 0) + amt
                    heads[eng] += 1
                    progress = True
                else:
                    break
    stuck = {e: (heads[e], len(l)) for e, l in qs.items() if heads[e] < len(l)}
    if stuck:
        lines = []
        for e, (i, n) in stuck.items():
            _, waits, _, _ = qs[e][i]
            lines.append(f"{e}: op {i}/{n} waits " + str(
                [(getattr(h, "name", h), v, counts.get(h, 0)) for h, v in waits]))
        raise RuntimeError("sync deadlock:\n" + "\n".join(lines))


def build_program():
    nc = bass.Bass("TRN2", target_bir_lowering=False)
    es = contextlib.ExitStack()

    def dram(name, shape, dt, kind=None):
        if kind is None:
            return nc.dram_tensor(name, shape, dt).ap()
        return nc.dram_tensor(name, shape, dt, kind=kind).ap()

    x_l = dram("x_l", [SL, H], F32, "ExternalInput")
    wqk = dram("wqk", [32, 128, DC * 128], F32, "ExternalInput")
    wv = dram("wv", [8, 128, DC * 256], F32, "ExternalInput")
    qkvb_qk = dram("qkvb_qk", [128, 32], F32, "ExternalInput")
    vb_row = dram("vb_row", [2048], F32, "ExternalInput")
    ow_h = dram("ow_h", [16, 128, DC * 256], F32, "ExternalInput")
    ob_row = dram("ob_row", [H], F32, "ExternalInput")
    Wn = dram("Wn", [H, H], F32, "ExternalInput")
    b_row = dram("b_row", [H], F32, "ExternalInput")
    A_in = dram("A_in", [64, H], F32, "ExternalInput")
    B_in = dram("B_in", [128, DC * 64], F32, "ExternalInput")
    gw_in = dram("gw_in", [4, H], F32, "ExternalInput")
    gb_in = dram("gb_in", [4], F32, "ExternalInput")
    identb_in = dram("identb_in", [128, 128], BF16, "ExternalInput")
    identf_in = dram("identf_in", [128, 128], F32, "ExternalInput")
    onesf_in = dram("onesf_in", [1, 128], F32, "ExternalInput")
    onesb_in = dram("onesb_in", [128, 1], BF16, "ExternalInput")
    levels_in = dram("levels_in", [128, 64], F32, "ExternalInput")
    masks_in = dram("masks_in", [128, 2], F32, "ExternalInput")
    eps_in = dram("eps_in", [128, 1], F32, "ExternalInput")
    out_d = dram("out", [TOWN, H], F32, "ExternalOutput")

    qT_s = dram("qT_s", [NH, 128, SL], BF16)
    kT_s = dram("kT_s", [NH, 128, SL], BF16)
    v_s = dram("v_s", [NH, 128, DC, 128], BF16)
    cc_in = [dram(f"cc_in{j}", [2 * TOWN, QB], BF16) for j in range(4)]
    cc_out = [dram(f"cc_out{j}", [TOWN, QB], BF16) for j in range(4)]
    ow_bf_s = dram("ow_bf_s", [16, 128, DC * 256], BF16)
    x2_s = dram("x2_s", [TOWN, H], F32)
    ln2_s = dram("ln2_s", [TOWN, H], BF16)

    def sbuf(name, shape, dt, scope=es):
        return scope.enter_context(nc.sbuf_tensor(name, shape, dt))

    def psum(name, shape, dt):
        return es.enter_context(nc.psum_tensor(name, shape, dt))

    identb = sbuf("identb", [128, 128], BF16)
    identf = sbuf("identf", [128, 128], F32)
    onesf = sbuf("onesf", [1, 128], F32)
    onesb = sbuf("onesb", [128, 1], BF16)
    levels = sbuf("levels", [128, 64], F32)
    masks = sbuf("masks", [128, 2], F32)
    qkvb_sb = sbuf("qkvb_sb", [128, 32], F32)
    row_sb = sbuf("row_sb", [1, 1024], F32)
    gb_sb = sbuf("gb_sb", [4, 1], F32)
    eps_sb = sbuf("eps_sb", [128, 1], F32)

    ps_a = psum("ps_a", [128, 1024], F32)
    ps_b = psum("ps_b", [128, 1024], F32)
    ps_c0 = psum("ps_c0", [128, 512], F32)
    ps_c1 = psum("ps_c1", [128, 512], F32)
    ps_d = psum("ps_d", [128, 512], F32)
    ps_t = psum("ps_t", [128, 1024], BF16)

    semnames = ["s_cst", "s_row", "s_ln", "s_sc", "s_tp", "s_te", "s_mm",
                "s_ev", "s_exp", "s_ctx", "s_dt", "s_dmm", "s_rc", "s_bc",
                "s_bcp", "s_ce", "s_ccm", "s_ccd", "s_cc", "s_rx", "s_op",
                "s_oe", "s_l2", "s_bar", "s_xsl", "s_wr",
                "s_xh0", "s_xh1", "s_wp0", "s_wp1", "s_stp0", "s_stp1",
                "s_vsp0", "s_vsp1", "s_kvp0", "s_kvp1", "s_x2p0", "s_x2p1",
                "s_xr2", "s_xr3"]
    S = {n: Sem(es.enter_context(nc.semaphore(n))) for n in semnames}
    s_cst, s_row, s_ln, s_sc, s_tp, s_te = (S[k] for k in semnames[0:6])
    s_mm, s_ev, s_exp, s_ctx, s_dt, s_dmm = (S[k] for k in semnames[6:12])
    s_rc, s_bc, s_bcp, s_ce, s_ccm, s_ccd = (S[k] for k in semnames[12:18])
    s_cc, s_rx, s_op, s_oe, s_l2, s_bar = (S[k] for k in semnames[18:24])
    s_xsl, s_wr = S["s_xsl"], S["s_wr"]
    s_xh = [S["s_xh0"], S["s_xh1"]]
    s_xr = [S["s_xh0"], S["s_xh1"], S["s_xr2"], S["s_xr3"]]
    s_wp = [S["s_wp0"], S["s_wp1"]]
    s_stp = [S["s_stp0"], S["s_stp1"]]
    s_vsp = [S["s_vsp0"], S["s_vsp1"]]
    s_kvp = [S["s_kvp0"], S["s_kvp1"]]
    s_x2p = [S["s_x2p0"], S["s_x2p1"]]

    P = Prog()

    for dst, src in [(identb, identb_in), (identf, identf_in), (onesf, onesf_in),
                     (onesb, onesb_in), (levels, levels_in), (masks, masks_in),
                     (qkvb_sb, qkvb_qk), (eps_sb, eps_in)]:
        P.op("sync", lambda e, d=dst, s=src: e.dma_start(out=d[:], in_=s[:]),
             inc16=s_cst)
    P.op("sync", lambda e: e.dma_start(
        out=gb_sb[:, :], in_=gb_in.rearrange("(p a) -> p a", a=1)), inc16=s_cst)
    consts_done = s_cst.n

    rb_state = {"tp": {}, "te": {}, "i": 0}

    def emit_row_bcast(row_src, n, dest_sb):
        row2 = row_src.rearrange("(a f) -> a f", a=1)
        rbanks = [ps_d, ps_c0, ps_c1]
        for i in range((n + 511) // 512):
            w = min(512, n - i * 512)
            gi = rb_state["i"]
            rb_state["i"] += 1
            par = gi % 2
            bank = rbanks[gi % 3]
            ldsem = [s_row, s_cst][par]
            wv = [(s_tp, rb_state["tp"][gi - 2])] if gi >= 2 else []
            P.op("sync", lambda e, i=i, w=w, par=par: e.dma_start(
                out=row_sb[:, par * 512:par * 512 + w],
                in_=row2[0:1, i * 512:i * 512 + w]),
                waits=wv, inc16=ldsem)
            ldv = ldsem.n
            wv2 = [(ldsem, ldv), (s_cst, consts_done)]
            if gi >= 3:
                wv2.append((s_te, rb_state["te"][gi - 3]))
            P.op("tensor", lambda e, w=w, par=par, bank=bank: e.matmul(
                bank[:, 0:w], lhsT=onesf[:, :],
                rhs=row_sb[:, par * 512:par * 512 + w],
                start=True, stop=True),
                waits=wv2, inc=s_tp)
            rb_state["tp"][gi] = s_tp.n
            P.op("scalar", lambda e, i=i, w=w, bank=bank: e.copy(
                out=dest_sb[:, i * 512:i * 512 + w], in_=bank[:, 0:w]),
                waits=[(s_tp, s_tp.n)], inc=s_te)
            rb_state["te"][gi] = s_te.n

    p1 = contextlib.ExitStack()
    vb_bc = sbuf("vb_bc", [128, 2048], BF16, p1)
    emit_row_bcast(vb_row, 2048, vb_bc)
    vbbc_done = s_te.n


    ln1T = sbuf("ln1T", [128, DC, 2048], BF16, p1)
    x_sb = [sbuf(f"x_sb{i}", [128, 2048], F32, p1) for i in range(2)]
    ln1h = sbuf("ln1h", [128, 2048], BF16, p1)
    w_u = [sbuf(f"w_u{i}", [128, 8192], BF16, p1) for i in range(2)]
    qkst = [sbuf(f"qkst{i}", [128, 512], BF16, p1) for i in range(2)]
    vst = [sbuf(f"vst{i}", [128, 256], BF16, p1) for i in range(2)]
    stats1 = sbuf("stats1", [128, 48], F32, p1)
    mv1 = [sbuf(f"mv1_{i}", [128, 2], F32, p1) for i in range(2)]
    rstd1 = [sbuf(f"rstd1_{i}", [128, 1], F32, p1) for i in range(2)]
    lnt1 = sbuf("lnt1", [128, 1], F32, p1)
    nmr1 = [sbuf(f"nmr1_{i}", [128, 1], F32, p1) for i in range(2)]

    x_free = {}
    ln_v = {}
    grp_idx = 0
    grp_ev = []
    qk_store_v = []
    v_store_v = []
    wu_free = {}
    wu_lidx = 0

    for tsb in range(2):
        t0 = tsb * 2048
        for tt in range(16):
            gt = tsb * 16 + tt
            row = t0 + tt * 128
            xv = {}
            for hf in range(2):
                w = [(s_sc, x_free[(gt - 1, hf)])] if gt >= 1 else \
                    [(s_cst, consts_done)]
                P.op("sync", lambda e, row=row, hf=hf: e.dma_start(
                    out=x_sb[hf][:],
                    in_=x_l[row:row + 128, hf * 2048:(hf + 1) * 2048]),
                    waits=w, inc16=s_xh[hf])
                xv[hf] = s_xh[hf].n
                for i in range(4):
                    P.op("vector", lambda e, hf=hf, i=i: e.bn_stats(
                        stats1[:, (hf * 4 + i) * 6:(hf * 4 + i) * 6 + 6],
                        x_sb[hf][:, i * 512:(i + 1) * 512]),
                        waits=[(s_xh[hf], xv[hf])],
                        inc=(s_ln if (hf == 1 and i == 3) else None))
            pp = gt % 2
            w = [(s_ln, s_ln.n)]
            if gt >= 2:
                w.append((s_sc, ln_v[gt - 2]))
            P.op("vector", lambda e, pp=pp: e.bn_aggr(mv1[pp][:], stats1[:]),
                 waits=w, inc=s_ln)
            P.op("scalar", lambda e, pp=pp: e.activation(
                out=lnt1[:], in_=mv1[pp][:, 1:2], func=AF.Ln,
                bias=eps_sb[:, 0:1]),
                waits=[(s_ln, s_ln.n)], inc=s_sc)
            ln_v[gt] = s_sc.n
            w = [(s_sc, s_sc.n)]
            if gt >= 2:
                w.append((s_sc, x_free[(gt - 2, 1)]))
            P.op("scalar", lambda e, pp=pp: e.activation(
                out=rstd1[pp][:], in_=lnt1[:], func=AF.Exp, scale=-0.5),
                waits=w, inc=s_sc)
            P.op("vector", lambda e, pp=pp: e.scalar_tensor_tensor(
                out=nmr1[pp][:], in0=mv1[pp][:, 0:1], scalar=-1.0,
                in1=rstd1[pp][:], op0=ALU.mult, op1=ALU.mult),
                waits=[(s_sc, s_sc.n)], inc=s_ln)
            nmr_v = s_ln.n
            for hf in range(2):
                P.op("scalar", lambda e, hf=hf, pp=pp: e.activation(
                    out=ln1h[:], in_=x_sb[hf][:], func=AF.Identity,
                    bias=nmr1[pp][:, 0:1], scale=rstd1[pp][:, 0:1]),
                    waits=[(s_ln, nmr_v), (s_tp, s_tp.n)], inc=s_sc)
                x_free[(gt, hf)] = s_sc.n
                lh = s_sc.n
                for grp in range(2):
                    for k in range(8):
                        c_loc = grp * 8 + k
                        w = [(s_sc, lh)]
                        if k == 0:
                            w.append((s_te, s_te.n))
                        P.op("tensor", lambda e, c_loc=c_loc, k=k: e.transpose(
                            ps_t[:, k * 128:(k + 1) * 128],
                            in_=ln1h[:, c_loc * 128:(c_loc + 1) * 128],
                            identity=identb[:]),
                            waits=w, inc=(s_tp if k == 7 else None))
                    c0 = hf * 16 + grp * 8
                    P.op("scalar", lambda e, c0=c0, tt=tt: e.copy(
                        out=ln1T[:, c0:c0 + 8, tt * 128:(tt + 1) * 128],
                        in_=ps_t[:].rearrange("p (c f) -> p c f", c=8)),
                        waits=[(s_tp, s_tp.n)], inc=s_te)
        ln1_done = s_te.n

        for ft in range(32):
            w = [(s_te, ln1_done)]
            if wu_lidx >= 2:
                w.append((s_mm, wu_free[wu_lidx - 2]))
            my_wu = wu_lidx % 2
            P.op("gpsimd", lambda e, ft=ft, my_wu=my_wu: e.dma_start(
                out=w_u[my_wu][:, 0:4096], in_=wqk[ft][:, 0:4096]),
                waits=w, inc16=s_wp[my_wu])
            wld = s_wp[my_wu].n
            wu_lidx += 1
            for tb in range(4):
                pq = [ps_a, ps_b][grp_idx % 2]
                w = [(s_wp[my_wu], wld), (s_te, ln1_done)]
                if grp_idx >= 2:
                    w.append((s_ev, grp_ev[grp_idx - 2]))
                for c in range(DC):
                    P.op("tensor", lambda e, c=c, tb=tb, pq=pq, my_wu=my_wu:
                         e.matmul(pq[:, 0:512],
                                  lhsT=w_u[my_wu][:, c * 128:(c + 1) * 128],
                                  rhs=ln1T[:, c, tb * 512:(tb + 1) * 512],
                                  start=(c == 0), stop=(c == DC - 1)),
                         waits=(w if c == 0 else ()),
                         inc=(s_mm if c == DC - 1 else None))
                wu_free[wu_lidx - 1] = s_mm.n
                qe = len(qk_store_v)
                w = [(s_mm, s_mm.n)]
                if qe >= 2:
                    w.append((s_stp[qe % 2], qk_store_v[qe - 2]))
                P.op("scalar", lambda e, ft=ft, pq=pq, i=qe % 2: e.activation(
                    out=qkst[i][:], in_=pq[:, 0:512], func=AF.Identity,
                    bias=qkvb_sb[:, ft:ft + 1]),
                    waits=w, inc=s_ev)
                grp_ev.append(s_ev.n)
                grp_idx += 1
                dst = qT_s if ft < 16 else kT_s
                hh = ft % 16
                P.op("sync", lambda e, dst=dst, hh=hh, t0=t0, tb=tb, i=qe % 2:
                     e.dma_start(
                         out=dst[hh, :, t0 + tb * 512:t0 + (tb + 1) * 512],
                         in_=qkst[i][:]),
                     waits=[(s_ev, s_ev.n)], inc16=s_stp[qe % 2])
                qk_store_v.append(s_stp[qe % 2].n)

        for fb in range(8):
            w = [(s_te, ln1_done)]
            if wu_lidx >= 2:
                w.append((s_mm, wu_free[wu_lidx - 2]))
            my_wu = wu_lidx % 2
            P.op("gpsimd", lambda e, fb=fb, my_wu=my_wu: e.dma_start(
                out=w_u[my_wu][:], in_=wv[fb][:]),
                waits=w, inc16=s_wp[my_wu])
            wld = s_wp[my_wu].n
            wu_lidx += 1
            for tt in range(16):
                pq = [ps_a, ps_b][grp_idx % 2]
                w = [(s_wp[my_wu], wld)]
                if grp_idx >= 2:
                    w.append((s_ev, grp_ev[grp_idx - 2]))
                for c in range(DC):
                    P.op("tensor", lambda e, c=c, tt=tt, pq=pq, my_wu=my_wu:
                         e.matmul(pq[:, 0:256],
                                  lhsT=ln1T[:, c, tt * 128:(tt + 1) * 128],
                                  rhs=w_u[my_wu][:, c * 256:(c + 1) * 256],
                                  start=(c == 0), stop=(c == DC - 1)),
                         waits=(w if c == 0 else ()),
                         inc=(s_mm if c == DC - 1 else None))
                wu_free[wu_lidx - 1] = s_mm.n
                ve = len(v_store_v)
                w = [(s_mm, s_mm.n), (s_te, vbbc_done)]
                if ve >= 2:
                    w.append((s_vsp[ve % 2], v_store_v[ve - 2]))
                P.op("vector", lambda e, fb=fb, pq=pq, i=ve % 2: e.tensor_tensor(
                    out=vst[i][:], in0=pq[:, 0:256],
                    in1=vb_bc[:, fb * 256:(fb + 1) * 256], op=ALU.add),
                    waits=w, inc=s_ev)
                grp_ev.append(s_ev.n)
                grp_idx += 1
                ktile = tsb * 16 + tt
                for hh in range(2):
                    head = fb * 2 + hh
                    P.op("sync", lambda e, head=head, ktile=ktile, hh=hh, i=ve % 2:
                         e.dma_start(out=v_s[head, :, ktile, :],
                                     in_=vst[i][:, hh * 128:(hh + 1) * 128]),
                         waits=[(s_ev, s_ev.n)], inc16=s_vsp[ve % 2])
                v_store_v.append(s_vsp[ve % 2].n)

    P.op("sync", lambda e: e.nop(),
         waits=[(s_stp[0], s_stp[0].n), (s_stp[1], s_stp[1].n),
                (s_vsp[0], s_vsp[0].n), (s_vsp[1], s_vsp[1].n)], inc=s_bar)
    P.op("tensor", lambda e: e.nop(), waits=[(s_mm, s_mm.n)], inc=s_bar)
    P.op("scalar", lambda e: e.nop(), waits=[(s_ev, s_ev.n)], inc=s_bar)
    P.op("vector", lambda e: e.nop(), waits=[(s_ev, s_ev.n)], inc=s_bar)
    P.op("gpsimd", lambda e: e.nop(),
         waits=[(s_wp[0], s_wp[0].n), (s_wp[1], s_wp[1].n)], inc=s_bar)
    bar1 = s_bar.n
    p1.close()

    p2 = contextlib.ExitStack()
    probsT2 = [sbuf(f"probsT{i}", [128, 16, 1024], BF16, p2) for i in range(2)]
    kT_sb = [sbuf(f"kT_sb{i}", [128, SL], BF16, p2) for i in range(2)]
    v_sb = [sbuf(f"v_sb{i}", [128, DC, 128], BF16, p2) for i in range(2)]
    qT_sb = [sbuf(f"qT_sb{i}", [128, QB], BF16, p2) for i in range(2)]
    ctxT = sbuf("ctxT", [128, NH, QB], BF16, p2)
    rx = sbuf("rx", [128, NH, QB], BF16, p2)
    ccm = [sbuf(f"ccm{i}", [128, 1024], BF16, p2) for i in range(2)]
    acc4 = sbuf("acc4", [128, 4, 512], BF16, p2)
    dtmp = sbuf("dtmp", [128, 512], F32, p2)
    dfin = sbuf("dfin", [128, 512], F32, p2)
    den_bf = sbuf("den_bf", [128, 512], BF16, p2)
    recip = [sbuf(f"recip{i}", [1, 512], F32, p2) for i in range(2)]
    bcast_sb = [sbuf(f"bcast{i}", [128, 512], F32, p2) for i in range(2)]
    ow_cb = [sbuf(f"ow_cb{i}", [128, 4096], BF16, p2) for i in range(3)]
    xsl = [sbuf(f"xsl{i}", [128, 256], F32, p2) for i in range(4)]
    t1sl = sbuf("t1sl", [128, 256], F32, p2)
    x2st = [sbuf(f"x2st{i}", [128, 256], F32, p2) for i in range(2)]
    stats2 = sbuf("stats2", [128, 4, 96], F32, p2)
    mv2 = [sbuf(f"mv2_{i}", [128, 2], F32, p2) for i in range(4)]
    lnt2 = sbuf("lnt2", [128, 1], F32, p2)
    rstd2 = [sbuf(f"rstd2_{i}", [128, 1], F32, p2) for i in range(4)]
    nmr2 = [sbuf(f"nmr2_{i}", [128, 1], F32, p2) for i in range(4)]
    x2rd = [sbuf(f"x2rd{i}", [128, 512], F32, p2) for i in range(4)]
    ln2h = sbuf("ln2h", [128, 512], BF16, p2)
    ob_bc = sbuf("ob_bc", [128, H], BF16, p2)

    for e_ in ("sync", "tensor", "scalar", "vector", "gpsimd"):
        P.wait(e_, [(s_bar, bar1)])
    emit_row_bcast(ob_row, H, ob_bc)
    obbc_done = s_te.n

    ow_ld = 0
    op_group_v = []
    oe_vals = []
    x2_stores = {}
    l2_stores = {}
    x2_idx = 0
    x2_store_v = []
    ln2_apply_v = []
    ln2_items = []
    ln2_ln_v = {}
    ln2_gt = 0

    spill_v = {}
    ow_state = {"i": 0, "free": {}}
    oe_slot_end = {}
    ccd_hist = []
    n_cc_slots = [0]
    prev_op_end = None

    def emit_ln2_items(n):
        for _ in range(n):
            if not ln2_items:
                return
            qb_, tt, hf, pp, nv, xa, xb = ln2_items.pop(0)
            k = len(ln2_apply_v)
            ii = k % 4
            w = [(s_x2p[0], xa), (s_x2p[1], xb)]
            if k >= 4:
                w.append((s_sc, ln2_apply_v[k - 4]))
            P.op("sync", lambda e, qb_=qb_, tt=tt, hf=hf, ii=ii: e.dma_start(
                out=x2rd[ii][:],
                in_=x2_s[qb_ * 512 + tt * 128:qb_ * 512 + (tt + 1) * 128,
                         hf * 512:(hf + 1) * 512]),
                waits=w, inc16=s_xr[ii])
            xv = s_xr[ii].n
            P.op("scalar", lambda e, ii=ii, pp=pp: e.activation(
                out=ln2h[:], in_=x2rd[ii][:], func=AF.Identity,
                bias=nmr2[pp][:, 0:1], scale=rstd2[pp][:, 0:1]),
                waits=[(s_ln, nv), (s_xr[ii], xv), (s_l2, s_l2.n)],
                inc=s_sc)
            ln2_apply_v.append(s_sc.n)
            P.op("gpsimd", lambda e, qb_=qb_, tt=tt, hf=hf: e.dma_start(
                out=ln2_s[qb_ * 512 + tt * 128:qb_ * 512 + (tt + 1) * 128,
                          hf * 512:(hf + 1) * 512],
                in_=ln2h[:]),
                waits=[(s_sc, s_sc.n)], inc16=s_l2)
            if not ln2_items:
                l2_stores[qb_] = s_l2.n

    for s in range(NSL):
        qb = QORDER[s]
        if s in (0, 2):
            for ob in range((s // 2) * 8, (s // 2) * 8 + 8):
                i = ob % 2
                P.op("gpsimd", lambda e, ob=ob: e.dma_start(
                    out=ow_bf_s[ob][:], in_=ow_h[ob][:]),
                    waits=[(s_bar, bar1)], inc16=s_vsp[i])
                spill_v[ob] = s_vsp[i].n
        if qb < 4:
            j = qb
            w = [(s_cc, j + 1)]
            if op_group_v:
                w.append((s_op, op_group_v[-1]))
            P.op("gpsimd", lambda e, j=j: e.dma_start(
                out=rx[:], in_=cc_out[j].rearrange("(h p) f -> p h f", p=128)),
                waits=w, inc16=s_rx)
        rx_v = s_rx.n
        has_prev_op = op_group_v and prev_op_end is not None

        def emit_den(hh):
            w = [(s_dt, hh + 1)]
            if hh >= 1:
                w.append((s_bcp, hh))
            else:
                w.append((s_te, obbc_done))
            P.op("tensor", lambda e, hh=hh: e.matmul(
                ps_d[0:1, :], lhsT=onesb[:], rhs=den_bf[:], start=True,
                stop=True), waits=w, inc=s_dmm)

        def emit_recip(hh):
            w = [(s_dmm, hh + 1)]
            if hh >= 2:
                w.append((s_bc, hh - 1))
            P.op("vector", lambda e, i=hh % 2: e.reciprocal(
                recip[i][:], ps_d[0:1, :]), waits=w, inc=s_rc)

        def emit_bcast(hh):
            P.op("tensor", lambda e, i=hh % 2: e.matmul(
                ps_d[:], lhsT=onesf[:], rhs=recip[i][:], start=True, stop=True),
                waits=[(s_rc, hh + 1)], inc=s_bc)

        def emit_bcopy(hh):
            w = [(s_bc, hh + 1)]
            if hh >= 2:
                w.append((s_ce, hh - 1))
            P.op("scalar", lambda e, i=hh % 2: e.copy(
                out=bcast_sb[i][:], in_=ps_d[:]), waits=w, inc=s_bcp)

        def emit_evac(hh):
            hloc = hh % NH
            pc2 = [ps_c0, ps_c1][hh % 2]
            P.op("vector", lambda e, hloc=hloc, pc2=pc2, i=hh % 2: e.tensor_tensor(
                ctxT[:, hloc, :], pc2[:], bcast_sb[i][:], op=ALU.mult),
                waits=[(s_ctx, (hh + 1) * 16), (s_bcp, hh + 1)], inc=s_ce)

        def emit_ccm(hh):
            hloc = hh % NH
            ss = QORDER[hh // NH] - 4
            w = [(s_ccd, ccd_hist[-2])] if len(ccd_hist) >= 2 else []
            P.op("vector", lambda e, hloc=hloc, i=hh % 2: e.tensor_scalar(
                out=ccm[i][:, 0:512], in0=ctxT[:, hloc, :],
                scalar1=masks[:, 0:1], scalar2=None, op0=ALU.mult),
                waits=w)
            P.op("vector", lambda e, hloc=hloc, i=hh % 2: e.tensor_scalar(
                out=ccm[i][:, 512:1024], in0=ctxT[:, hloc, :],
                scalar1=masks[:, 1:2], scalar2=None, op0=ALU.mult),
                inc=s_ccm)
            P.op("gpsimd", lambda e, ss=ss, hloc=hloc, i=hh % 2: e.dma_start(
                out=cc_in[ss][hloc * 128:(hloc + 1) * 128, :],
                in_=ccm[i][:, 0:512]),
                waits=[(s_ccm, s_ccm.n)], inc16=s_ccd)
            P.op("gpsimd", lambda e, ss=ss, hloc=hloc, i=hh % 2: e.dma_start(
                out=cc_in[ss][TOWN + hloc * 128:TOWN + (hloc + 1) * 128, :],
                in_=ccm[i][:, 512:1024]), inc16=s_ccd)
            ccd_hist.append(s_ccd.n)

        def emit_back(hh):
            emit_evac(hh)
            if QORDER[hh // NH] >= 4:
                emit_ccm(hh)

        for h in range(NH):
            hg = s * NH + h
            par = hg % 2
            w = [(s_ctx, (hg - 1) * 16)] if hg >= 2 else []
            P.op("sync", lambda e, h=h, par=par: e.dma_start(
                out=kT_sb[par][:], in_=kT_s[h]), waits=w, inc16=s_kvp[par])
            P.op("sync", lambda e, h=h, par=par: e.dma_start(
                out=v_sb[par][:].rearrange("p c f -> p (c f)"),
                in_=v_s[h].rearrange("p c f -> p (c f)")), inc16=s_kvp[par])
            P.op("sync", lambda e, h=h, par=par, qb=qb: e.dma_start(
                out=qT_sb[par][:], in_=qT_s[h, :, qb * QB:(qb + 1) * QB]),
                inc16=s_kvp[par])
            kvv = s_kvp[par].n

            sc_base = s_mm.n
            for gk in range(16):
                G = hg * 16 + gk
                pq = [ps_a, ps_b][G % 2]
                w = [(s_kvp[par], kvv)]
                if G >= 2:
                    w.append((s_exp, G - 1))
                if gk == 0 and h == 0 and prev_op_end is not None:
                    w.append((s_oe, prev_op_end))
                for half in range(2):
                    kt = gk * 2 + half
                    P.op("tensor", lambda e, kt=kt, half=half, par=par, pq=pq:
                         e.matmul(pq[:, half * 512:(half + 1) * 512],
                                  lhsT=kT_sb[par][:, kt * 128:(kt + 1) * 128],
                                  rhs=qT_sb[par][:], start=True, stop=True),
                         waits=(w if half == 0 else ()),
                         inc=(s_mm if half == 1 else None))
            if hg >= 1 and (hg - 1) % NH != NH - 1:
                emit_den(hg - 1)

            for gk in range(16):
                G = hg * 16 + gk
                pq = [ps_a, ps_b][G % 2]
                w = [(s_mm, sc_base + gk + 1)]
                if hg >= 2:
                    w.append((s_ctx, (hg - 2) * 16 + gk + 1))
                    w.append((s_dt, hg - 1))
                P.op("scalar", lambda e, gk=gk, pq=pq, pb=hg % 2: e.activation(
                    out=probsT2[pb][:, gk, :], in_=pq[:], func=AF.Exp, scale=SCALE),
                    waits=w, inc=s_exp)
            if hg >= 1 and (hg - 1) % NH != NH - 1:
                emit_bcopy(hg - 1)
            emit_ln2_items(2)

            if hg >= 2 and (hg - 2) % NH < NH - 2:
                emit_back(hg - 2)
            if hg >= 1 and (hg - 1) % NH != NH - 1:
                emit_recip(hg - 1)
            for gk in range(16):
                G = hg * 16 + gk
                q = gk // 4
                w = [(s_exp, G + 1)]
                if gk % 4 == 0:
                    P.op("vector", lambda e, gk=gk, q=q, pb=hg % 2: e.tensor_tensor(
                        acc4[:, q, :], probsT2[pb][:, gk, 0:512],
                        probsT2[pb][:, gk, 512:1024], op=ALU.add), waits=w)
                else:
                    P.op("vector", lambda e, gk=gk, q=q, pb=hg % 2: e.tensor_tensor(
                        acc4[:, q, :], acc4[:, q, :], probsT2[pb][:, gk, 0:512],
                        op=ALU.add), waits=w)
                    P.op("vector", lambda e, gk=gk, q=q, pb=hg % 2: e.tensor_tensor(
                        acc4[:, q, :], acc4[:, q, :], probsT2[pb][:, gk, 512:1024],
                        op=ALU.add))
            P.op("vector", lambda e: e.tensor_tensor(
                dfin[:], acc4[:, 0, :], acc4[:, 1, :], op=ALU.add))
            P.op("vector", lambda e: e.tensor_tensor(
                dtmp[:], acc4[:, 2, :], acc4[:, 3, :], op=ALU.add))
            P.op("vector", lambda e: e.tensor_tensor(
                dfin[:], dfin[:], dtmp[:], op=ALU.add))
            w = [(s_dmm, hg)] if hg >= 1 else []
            P.op("vector", lambda e: e.tensor_copy(den_bf[:], dfin[:]),
                 waits=w, inc=s_dt)

            pc = [ps_c0, ps_c1][hg % 2]
            for p in range(16):
                w = [(s_exp, hg * 16 + p + 1)]
                if p == 0 and hg >= 2:
                    w.append((s_ce, hg - 1))
                if p == 0 and h < 2 and prev_op_end is not None:
                    w.append((s_oe, prev_op_end))
                for half in range(2):
                    kt = p * 2 + half
                    P.op("tensor", lambda e, kt=kt, p=p, half=half, par=par,
                         pc=pc, pb=hg % 2: e.matmul(
                             pc[:], lhsT=v_sb[par][:, kt, :],
                             rhs=probsT2[pb][:, p, half * 512:(half + 1) * 512],
                             start=(kt == 0), stop=(kt == 31)),
                         waits=(w if half == 0 else ()),
                         inc=(s_ctx if half == 1 else None))
            if hg >= 1 and (hg - 1) % NH != NH - 1:
                emit_bcast(hg - 1)

        hg_last = s * NH + NH - 1
        emit_den(hg_last)
        emit_recip(hg_last)
        emit_bcast(hg_last)
        emit_bcopy(hg_last)
        emit_back(hg_last - 1)
        emit_back(hg_last)

        if qb >= 4:
            n_cc_slots[0] += 1
            cj = qb - 4
            P.op("gpsimd", lambda e, cj=cj: e.collective_compute(
                "ReduceScatter", ALU.add, ins=[cc_in[cj][:]],
                outs=[cc_out[cj][:]], replica_groups=GROUPS),
                waits=[(s_ccd, n_cc_slots[0] * NH * 32)], inc=s_cc)
        else:
            j = qb
            equads = [(ps_a, 0, 256), (ps_a, 512, 256), (ps_b, 0, 256),
                      (ps_b, 512, 256)]
            oquads = [(ps_c0, 0, 256), (ps_c0, 256, 256), (ps_c1, 0, 256),
                      (ps_c1, 256, 256)]
            slot_exp = (s + 1) * NH * 16
            slot_ce = (s + 1) * NH
            oe_slot0 = s_oe.n
            use_spill = s >= 4
            ow_sems = [s_wp[0], s_wp[1], s_row]
            for ob in range(16):
                quads = equads if ob % 2 == 0 else oquads
                hb = {}
                hv = {}
                for half in range(2):
                    oi = ow_state["i"]
                    ow_state["i"] += 1
                    bi = oi % 3
                    hb[half] = bi
                    w = []
                    if bi in ow_state["free"]:
                        w.append(ow_state["free"][bi])
                    if use_spill and ob < 2 and half == 0:
                        w.append((s_vsp[0], s_vsp[0].n))
                        w.append((s_vsp[1], s_vsp[1].n))
                    osrc = ow_bf_s[ob] if use_spill else ow_h[ob]
                    P.op("gpsimd", lambda e, osrc=osrc, bi=bi, half=half:
                         e.dma_start(out=ow_cb[bi][:],
                                     in_=osrc[:, half * 4096:(half + 1) * 4096]),
                         waits=w, inc16=ow_sems[bi])
                    hv[half] = ow_sems[bi].n
                ow_ld += 1
                for tt in range(4):
                    w = []
                    if len(oe_vals) >= 4:
                        w.append((s_oe, oe_vals[len(oe_vals) - 4 + tt]))
                    P.op("sync", lambda e, qb=qb, tt=tt, ob=ob:
                         e.dma_start(
                             out=xsl[tt][:],
                             in_=x_l[qb * 512 + tt * 128:qb * 512 + (tt + 1) * 128,
                                     ob * 256:(ob + 1) * 256]),
                         waits=w, inc16=s_xsl)
                xslv = s_xsl.n
                w = [(ow_sems[hb[0]], hv[0]), (ow_sems[hb[1]], hv[1]),
                     (s_exp, slot_exp), (s_ce, slot_ce), (s_rx, rx_v)]
                if ob >= 2:
                    w.append((s_oe, oe_vals[-5]))
                for cf in range(DC):
                    src_ = ctxT if cf < 16 else rx
                    hh = cf % 16
                    bi = hb[cf // 16]
                    for tt in range(4):
                        pq, off, _qw = quads[tt]
                        P.op("tensor", lambda e, src_=src_, hh=hh, tt=tt, cf=cf,
                             pq=pq, off=off, bi=bi: e.matmul(
                                 pq[:, off:off + 256],
                                 lhsT=src_[:, hh, tt * 128:(tt + 1) * 128],
                                 rhs=ow_cb[bi][:, (cf % 16) * 256:
                                               (cf % 16 + 1) * 256],
                                 start=(cf == 0), stop=(cf == DC - 1)),
                             waits=(w if (cf == 0 and tt == 0) else ()),
                             inc=(s_op if (tt == 3 and cf in (15, DC - 1))
                                  else None))
                    if cf == 15:
                        ow_state["free"][hb[0]] = (s_op, s_op.n)
                op_group_v.append(s_op.n)
                ow_state["free"][hb[1]] = (s_op, s_op.n)
                for tt in range(4):
                    gi = ob * 4 + tt
                    pq, off, _qw = quads[tt]
                    P.op("vector", lambda e, pq=pq, off=off, ob=ob: e.tensor_tensor(
                        t1sl[:], pq[:, off:off + 256],
                        ob_bc[:, ob * 256:(ob + 1) * 256], op=ALU.add),
                        waits=[(s_op, s_op.n), (s_xsl, xslv)])
                    ii = gi % 2
                    w2 = [(s_x2p[x2_idx % 2], x2_store_v[x2_idx - 2])] \
                        if x2_idx >= 2 else []
                    P.op("vector", lambda e, ii=ii, tt=tt: e.tensor_tensor(
                        x2st[ii][:], t1sl[:], xsl[tt][:], op=ALU.add), waits=w2)
                    P.op("vector", lambda e, ii=ii, tt=tt, ob=ob: e.bn_stats(
                        stats2[:, tt, ob * 6:(ob + 1) * 6], x2st[ii][:]),
                        inc=s_oe)
                    oe_vals.append(s_oe.n)
                    P.op("sync", lambda e, ii=ii, qb=qb, tt=tt, ob=ob: e.dma_start(
                        out=x2_s[qb * 512 + tt * 128:qb * 512 + (tt + 1) * 128,
                                 ob * 256:(ob + 1) * 256],
                        in_=x2st[ii][:]),
                        waits=[(s_oe, s_oe.n)], inc16=s_x2p[x2_idx % 2])
                    x2_store_v.append(s_x2p[x2_idx % 2].n)
                    x2_idx += 1
            x2_stores[qb] = (s_x2p[0].n, s_x2p[1].n)
            oe_slot_end[s] = s_oe.n
            prev_op_end = s_oe.n
            for tt in range(4):
                pp = ln2_gt % 4
                w = [(s_oe, s_oe.n)]
                if ln2_gt >= 4:
                    w.append((s_sc, ln2_ln_v[ln2_gt - 4]))
                P.op("vector", lambda e, tt=tt, pp=pp: e.bn_aggr(
                    mv2[pp][:], stats2[:, tt, :]), waits=w, inc=s_ln)
                P.op("scalar", lambda e, pp=pp: e.activation(
                    out=lnt2[:], in_=mv2[pp][:, 1:2], func=AF.Ln,
                    bias=eps_sb[:, 0:1]),
                    waits=[(s_ln, s_ln.n)], inc=s_sc)
                ln2_ln_v[ln2_gt] = s_sc.n
                w = [(s_sc, s_sc.n)]
                if ln2_gt >= 4:
                    w.append((s_sc, ln2_apply_v[8 * (ln2_gt - 4) + 7]))
                P.op("scalar", lambda e, pp=pp: e.activation(
                    out=rstd2[pp][:], in_=lnt2[:], func=AF.Exp, scale=-0.5),
                    waits=w, inc=s_sc)
                P.op("vector", lambda e, pp=pp: e.scalar_tensor_tensor(
                    out=nmr2[pp][:], in0=mv2[pp][:, 0:1], scalar=-1.0,
                    in1=rstd2[pp][:], op0=ALU.mult, op1=ALU.mult),
                    waits=[(s_sc, s_sc.n)], inc=s_ln)
                nv = s_ln.n
                ln2_gt += 1
                for hf in range(8):
                    ln2_items.append((qb, tt, hf, pp, nv,
                                      x2_stores[qb][0], x2_stores[qb][1]))

    emit_ln2_items(len(ln2_items))

    P.op("sync", lambda e: e.nop(),
         waits=[(s_l2, s_l2.n), (s_x2p[0], s_x2p[0].n),
                (s_x2p[1], s_x2p[1].n)], inc=s_bar)
    P.op("tensor", lambda e: e.nop(), waits=[(s_op, s_op.n)], inc=s_bar)
    P.op("scalar", lambda e: e.nop(), waits=[(s_bcp, s_bcp.n)], inc=s_bar)
    P.op("vector", lambda e: e.nop(), waits=[(s_ln, s_ln.n)], inc=s_bar)
    P.op("gpsimd", lambda e: e.nop(), waits=[(s_cc, 4)], inc=s_bar)
    bar2 = s_bar.n
    p2.close()

    p3 = contextlib.ExitStack()
    ln2T = sbuf("ln2T", [128, DC, 1024], BF16, p3)
    ln2rd = [sbuf(f"ln2rd{i}", [128, H], BF16, p3) for i in range(2)]
    WT = [sbuf(f"WT{i}", [128, DC, OB3], BF16, p3) for i in range(2)]
    wrow = [sbuf(f"wrow{i}", [128, 2048], F32, p3) for i in range(4)]
    x2sl = [sbuf(f"x2sl{i}", [128, OB3], F32, p3) for i in range(2)]
    outst = [sbuf(f"outst{i}", [128, OB3], F32, p3) for i in range(2)]
    t1sl3 = sbuf("t1sl3", [128, OB3], F32, p3)
    b_bc = sbuf("b_bc", [128, H], BF16, p3)
    a_bf = sbuf("a_bf", [64, H], BF16, p3)
    AT_sb = sbuf("AT_sb", [128, DC, 64], BF16, p3)
    b2_bf = sbuf("b2_bf", [128, DC * 64], BF16, p3)
    BT_sb = sbuf("BT_sb", [64, H], BF16, p3)
    gw_bf = sbuf("gw_bf", [4, H], BF16, p3)
    gwT_sb = sbuf("gwT_sb", [128, DC, 4], BF16, p3)
    h1T = sbuf("h1T", [64, 1024], BF16, p3)
    mT_sb = sbuf("mT_sb", [64, 1024], BF16, p3)
    m_sb = sbuf("m_sb", [64, 1024], BF16, p3)
    g_f32 = sbuf("g_f32", [4, 512], F32, p3)
    gn_sb = sbuf("gn_sb", [128, 16], F32, p3)
    gm1 = sbuf("gm1", [128, 4], F32, p3)
    ga = sbuf("ga", [128, 1], F32, p3)
    maskn = sbuf("maskn", [128, 64], BF16, p3)

    for e_ in ("sync", "tensor", "scalar", "vector", "gpsimd"):
        P.wait(e_, [(s_bar, bar2)])

    emit_row_bcast(b_row, H, b_bc)

    P.op("gpsimd", lambda e: e.dma_start(out=a_bf[:], in_=A_in[:]), inc16=s_wr)
    P.op("gpsimd", lambda e: e.dma_start(out=b2_bf[:], in_=B_in[:]), inc16=s_wr)
    P.op("gpsimd", lambda e: e.dma_start(out=gw_bf[:], in_=gw_in[:]), inc16=s_wr)
    pre_w = s_wr.n

    for grp in range(2):
        for k in range(16):
            c = grp * 16 + k
            w = [(s_wr, pre_w)]
            if k == 0:
                w.append((s_te, s_te.n))
            P.op("tensor", lambda e, c=c, k=k: e.transpose(
                ps_t[:, k * 64:(k + 1) * 64],
                in_=a_bf[0:64, c * 128:(c + 1) * 128],
                identity=identb[0:64, 0:64]),
                waits=w, inc=(s_tp if k == 15 else None))
        P.op("scalar", lambda e, grp=grp: e.copy(
            out=AT_sb[:, grp * 16:(grp + 1) * 16, :],
            in_=ps_t[:].rearrange("p (c f) -> p c f", c=16)),
            waits=[(s_tp, s_tp.n)], inc=s_te)
    for grp in range(4):
        for k in range(8):
            c = grp * 8 + k
            w = [(s_wr, pre_w)]
            if k == 0:
                w.append((s_te, s_te.n))
            P.op("tensor", lambda e, c=c, k=k: e.transpose(
                ps_t[0:64, k * 128:(k + 1) * 128],
                in_=b2_bf[:, c * 64:(c + 1) * 64], identity=identb[:]),
                waits=w, inc=(s_tp if k == 7 else None))
        P.op("scalar", lambda e, grp=grp: e.copy(
            out=BT_sb[0:64, grp * 1024:(grp + 1) * 1024], in_=ps_t[0:64, :]),
            waits=[(s_tp, s_tp.n)], inc=s_te)
    for k in range(DC):
        w = [(s_wr, pre_w)]
        if k == 0:
            w.append((s_te, s_te.n))
        P.op("tensor", lambda e, k=k: e.transpose(
            ps_t[:, k * 4:(k + 1) * 4], in_=gw_bf[0:4, k * 128:(k + 1) * 128],
            identity=identb[0:4, 0:4]),
            waits=w, inc=(s_tp if k == DC - 1 else None))
    P.op("scalar", lambda e: e.copy(
        out=gwT_sb[:], in_=ps_t[:, 0:128].rearrange("p (c f) -> p c f", c=32)),
        waits=[(s_tp, s_tp.n)], inc=s_te)

    ffn_grp = 0
    wtb_state = {"g": 0, "bcp": {}}
    wrow_half_tp = {}
    nonlocal_state = {"ffn_grp": 0, "out_ct": 0}
    pend_mm = None
    out_store_v = []
    g_tp_v = {}
    gn_free_v = {}
    ffn_grp_mm = []
    wrow_tp = {}
    out_ct = 0
    ln2rd_tp = {}

    for th in range(2):
        tbase = th * 1024
        if pend_mm is not None:
            pend_mm()
            pend_mm = None
        for tt in range(8):
            gti = th * 8 + tt
            row = tbase + tt * 128
            qb_src = row // 512
            w = [(s_l2, l2_stores[qb_src])]
            if gti >= 2:
                w.append((s_tp, ln2rd_tp[gti - 2]))
            P.op("sync", lambda e, row=row, i=tt % 2: e.dma_start(
                out=ln2rd[i][:], in_=ln2_s[row:row + 128, :]),
                waits=w, inc16=s_xh[tt % 2])
            xv = s_xh[tt % 2].n
            for grp in range(4):
                for k in range(8):
                    c = grp * 8 + k
                    w2 = [(s_xh[tt % 2], xv)]
                    if k == 0:
                        w2.append((s_te, s_te.n))
                    P.op("tensor", lambda e, c=c, k=k, i=tt % 2: e.transpose(
                        ps_t[:, k * 128:(k + 1) * 128],
                        in_=ln2rd[i][:, c * 128:(c + 1) * 128],
                        identity=identb[:]),
                        waits=w2, inc=(s_tp if k == 7 else None))
                P.op("scalar", lambda e, grp=grp, tt=tt: e.copy(
                    out=ln2T[:, grp * 8:(grp + 1) * 8, tt * 128:(tt + 1) * 128],
                    in_=ps_t[:].rearrange("p (c f) -> p c f", c=8)),
                    waits=[(s_tp, s_tp.n)], inc=s_te)
            ln2rd_tp[gti] = s_tp.n
        ln2T_done = s_te.n

        for tq in range(2):
            gtq = th * 2 + tq
            gate_ps_free = [(s_ev, s_ev.n)] if (th > 0 or tq > 0) else []
            if th > 0 and tq == 0:
                gate_ps_free.append((s_bcp, s_bcp.n))
            for c in range(DC):
                w = [(s_te, ln2T_done)] + (gate_ps_free if c == 0 else [])
                P.op("tensor", lambda e, c=c, tq=tq: e.matmul(
                    ps_c0[0:4, :], lhsT=gwT_sb[:, c, :],
                    rhs=ln2T[:, c, tq * 512:(tq + 1) * 512],
                    start=(c == 0), stop=(c == DC - 1)),
                    waits=(w if c == 0 else ()),
                    inc=(s_mm if c == DC - 1 else None))
            w = [(s_mm, s_mm.n)]
            if gtq >= 1:
                w.append((s_tp, g_tp_v[gtq - 1]))
            P.op("vector", lambda e: e.tensor_scalar(
                out=g_f32[:], in0=ps_c0[0:4, :], scalar1=gb_sb[0:4, 0:1],
                scalar2=None, op0=ALU.add),
                waits=w, inc=s_ev)
            gev = s_ev.n
            for k in range(4):
                w = [(s_ev, gev), (s_bcp, s_bcp.n)]
                if k > 0:
                    w = []
                P.op("tensor", lambda e, k=k: e.transpose(
                    ps_d[:, k * 4:(k + 1) * 4],
                    in_=g_f32[0:4, k * 128:(k + 1) * 128],
                    identity=identf[0:4, 0:4]),
                    waits=w, inc=(s_tp if k == 3 else None))
            g_tp_v[gtq] = s_tp.n
            w = [(s_tp, s_tp.n)]
            if gtq >= 1:
                w.append((s_ln, gn_free_v[gtq - 1]))
            P.op("scalar", lambda e: e.copy(out=gn_sb[:], in_=ps_d[:, 0:16]),
                 waits=w, inc=s_bcp)
            gnv = s_bcp.n
            for k in range(4):
                base = k * 4
                P.op("vector", lambda e, base=base: e.reduce_max(
                    gm1[:, 0:1], gn_sb[:, base + 1:base + 4], axis=AX.X),
                    waits=[(s_bcp, gnv)])
                P.op("vector", lambda e, base=base: e.reduce_max(
                    gm1[:, 1:2], gn_sb[:, base + 2:base + 4], axis=AX.X))
                P.op("vector", lambda e, base=base: e.reduce_max(
                    gm1[:, 2:3], gn_sb[:, base + 0:base + 2], axis=AX.X))
                P.op("vector", lambda e, base=base: e.reduce_max(
                    gm1[:, 3:4], gn_sb[:, base + 0:base + 3], axis=AX.X),
                    inc=s_ln)
                mxv = s_ln.n
                P.op("vector", lambda e, base=base: e.tensor_tensor(
                    ga[:], gm1[:, 0:1], gn_sb[:, base:base + 1], op=ALU.is_gt),
                    waits=[(s_ln, mxv)])
                P.op("vector", lambda e: e.tensor_tensor(
                    gm1[:, 0:1], gm1[:, 1:2], gm1[:, 2:3], op=ALU.is_gt))
                P.op("vector", lambda e, base=base: e.tensor_tensor(
                    gm1[:, 1:2], gn_sb[:, base + 3:base + 4], gm1[:, 3:4],
                    op=ALU.is_gt), inc=s_ln)
                cv = s_ln.n
                gn_free_v[gtq] = cv
                P.op("vector", lambda e: e.tensor_tensor(
                    gm1[:, 2:3], gm1[:, 0:1], gm1[:, 1:2], op=ALU.add),
                    waits=[(s_ln, cv)], inc=s_ln)
                sumv = s_ln.n
                P.op("vector", lambda e: e.tensor_tensor(
                    ga[:], ga[:], gm1[:, 2:3], op=ALU.add),
                    waits=[(s_ln, sumv)], inc=s_ln)
                av = s_ln.n
                P.op("vector", lambda e: e.tensor_scalar(
                    out=maskn[:], in0=levels[:], scalar1=ga[:, 0:1],
                    scalar2=None, op0=ALU.is_le),
                    waits=[(s_ln, av), (s_te, s_te.n)], inc=s_ln)
                mk = s_ln.n
                P.op("tensor", lambda e: e.transpose(
                    ps_t[0:64, 0:128], in_=maskn[:], identity=identb[:]),
                    waits=[(s_ln, mk), (s_te, s_te.n)], inc=s_tp)
                P.op("scalar", lambda e, tq=tq, k=k: e.copy(
                    out=mT_sb[0:64, (tq * 4 + k) * 128:(tq * 4 + k + 1) * 128],
                    in_=ps_t[0:64, 0:128]),
                    waits=[(s_tp, s_tp.n)], inc=s_te)
            h1_ps_free = [(s_ev, s_ev.n)] if (th > 0 or tq > 0) else []
            if th > 0 and tq == 0:
                h1_ps_free.append((s_bcp, s_bcp.n))
            for c in range(DC):
                w = [(s_te, ln2T_done)] + (h1_ps_free if c == 0 else [])
                P.op("tensor", lambda e, c=c, tq=tq: e.matmul(
                    ps_c1[0:64, :], lhsT=AT_sb[:, c, :],
                    rhs=ln2T[:, c, tq * 512:(tq + 1) * 512],
                    start=(c == 0), stop=(c == DC - 1)),
                    waits=(w if c == 0 else ()),
                    inc=(s_mm if c == DC - 1 else None))
            P.op("scalar", lambda e, tq=tq: e.copy(
                out=h1T[0:64, tq * 512:(tq + 1) * 512], in_=ps_c1[0:64, :]),
                waits=[(s_mm, s_mm.n)], inc=s_ev)
        P.op("vector", lambda e: e.tensor_tensor(
            m_sb[0:64, :], h1T[0:64, :], mT_sb[0:64, :], op=ALU.mult),
            waits=[(s_ev, s_ev.n), (s_te, s_te.n)], inc=s_ln)
        msk_done = s_ln.n

        for ob in range(NOB3):
            gob = th * NOB3 + ob
            wldv = {}
            for li in range(4):
                osb, dhalf = li // 2, li % 2
                gli = gob * 4 + li
                w = []
                if gli >= 4:
                    w.append((s_tp, wrow_half_tp[gli - 4]))
                P.op("sync", lambda e, ob=ob, osb=osb, dhalf=dhalf, li=li:
                     e.dma_start(
                         out=wrow[li][:],
                         in_=Wn[ob * 256 + osb * 128:ob * 256 + (osb + 1) * 128,
                                dhalf * 2048:(dhalf + 1) * 2048]),
                     waits=w, inc16=s_wr)
                wldv[li] = s_wr.n
            wti = gob % 2
            wtbanks = [ps_d, ps_c0, ps_c1]
            for osb in range(2):
                for grp in range(8):
                    wg = wtb_state["g"]
                    wtb_state["g"] += 1
                    bank = wtbanks[wg % 3]
                    li = osb * 2 + grp // 4
                    for k in range(4):
                        c = grp * 4 + k
                        w = [(s_wr, wldv[li])]
                        if k == 0 and wg >= 3:
                            w.append((s_bcp, wtb_state["bcp"][wg - 3]))
                        if k == 0 and (wg % (NOB3 * 16)) < 3:
                            w.append((s_ev, s_ev.n))
                            w.append((s_bcp, s_bcp.n))
                        if k == 0 and grp == 0 and osb == 0 and gob >= 2:
                            w.append((s_mm, ffn_grp_mm[(gob - 2) * 8 + 7]))
                        P.op("tensor", lambda e, c=c, k=k, li=li, bank=bank:
                             e.transpose(
                                 bank[:, k * 128:(k + 1) * 128],
                                 in_=wrow[li][:, (c % 16) * 128:
                                              (c % 16 + 1) * 128],
                                 identity=identf[:]),
                             waits=w, inc=(s_tp if k == 3 else None))
                    P.op("scalar", lambda e, grp=grp, osb=osb, wti=wti, bank=bank:
                         e.copy(
                             out=WT[wti][:, grp * 4:(grp + 1) * 4,
                                         osb * 128:(osb + 1) * 128],
                             in_=bank[:].rearrange("p (c f) -> p c f", c=4)),
                         waits=[(s_tp, s_tp.n)], inc=s_bcp)
                    wtb_state["bcp"][wg] = s_bcp.n
                    if grp % 4 == 3:
                        wrow_half_tp[gob * 4 + osb * 2 + grp // 4] = s_tp.n
            wrow_tp[gob] = s_tp.n
            wt_done = s_bcp.n

            def mk_mm(ob, tbase, wt_done, wti, msk_done):
                def go():
                    global_unused = None
                    for tt in range(8):
                        nonlocal_state["ffn_grp"] += 1
                        fg = nonlocal_state["ffn_grp"] - 1
                        pq = [ps_a, ps_b][fg % 2]
                        w = [(s_bcp, wt_done), (s_ln, msk_done)]
                        if fg >= 2:
                            w.append((s_oe, oe_vals[-2]))
                        for c in range(DC):
                            P.op("tensor", lambda e, c=c, tt=tt, pq=pq, wti=wti:
                                 e.matmul(pq[:, 0:OB3],
                                          lhsT=ln2T[:, c, tt * 128:(tt + 1) * 128],
                                          rhs=WT[wti][:, c, :],
                                          start=(c == 0), stop=False),
                                 waits=(w if c == 0 else ()))
                        P.op("tensor", lambda e, tt=tt, ob=ob, pq=pq: e.matmul(
                            pq[:, 0:OB3],
                            lhsT=m_sb[0:64, tt * 128:(tt + 1) * 128],
                            rhs=BT_sb[0:64, ob * 256:(ob + 1) * 256],
                            start=False, stop=True), inc=s_mm)
                        mmv = s_mm.n
                        ffn_grp_mm.append(mmv)
                        ii = (fg + 1) % 2
                        qb_src = (tbase + tt * 128) // 512
                        w2 = [(s_x2p[0], x2_stores[qb_src][0]),
                              (s_x2p[1], x2_stores[qb_src][1])]
                        if fg >= 2:
                            w2.append((s_oe, oe_vals[-2]))
                        P.op("sync", lambda e, tbase=tbase, tt=tt, ob=ob, ii=ii:
                             e.dma_start(
                                 out=x2sl[ii][:],
                                 in_=x2_s[tbase + tt * 128:tbase + (tt + 1) * 128,
                                          ob * 256:(ob + 1) * 256]),
                             waits=w2, inc16=s_wp[ii])
                        xv2 = s_wp[ii].n
                        P.op("vector", lambda e, pq=pq, ob=ob, ii=ii:
                             e.tensor_tensor(
                                 t1sl3[:], pq[:, 0:OB3],
                                 b_bc[:, ob * 256:(ob + 1) * 256],
                                 op=ALU.add), waits=[(s_mm, mmv), (s_wp[ii], xv2)])
                        oc = nonlocal_state["out_ct"]
                        w3 = [(s_stp[oc % 2], out_store_v[oc - 2])] \
                            if oc >= 2 else []
                        P.op("vector", lambda e, ii=ii: e.tensor_tensor(
                            outst[ii][:], t1sl3[:], x2sl[ii][:], op=ALU.add),
                            waits=w3, inc=s_oe)
                        oe_vals.append(s_oe.n)
                        P.op("sync", lambda e, tbase=tbase, tt=tt, ob=ob, ii=ii:
                             e.dma_start(
                                 out=out_d[tbase + tt * 128:tbase + (tt + 1) * 128,
                                           ob * 256:(ob + 1) * 256],
                                 in_=outst[ii][:]),
                             waits=[(s_oe, s_oe.n)], inc16=s_stp[oc % 2])
                        out_store_v.append(s_stp[oc % 2].n)
                        nonlocal_state["out_ct"] += 1
                return go

            new_mm = mk_mm(ob, tbase, wt_done, wti, msk_done)
            if pend_mm is not None:
                pend_mm()
            pend_mm = new_mm
    if pend_mm is not None:
        pend_mm()
        pend_mm = None

    P.wait("sync", [(s_stp[0], s_stp[0].n), (s_stp[1], s_stp[1].n)])

    _check_deadlock(P)

    with nc.Block() as blk:
        for eng_name in ("sync", "gpsimd", "tensor", "scalar", "vector"):
            items = [o for o in P.ops if o[0] == eng_name]

            def make_body(items):
                def body(eng):
                    for _, waits, fn, incs in items:
                        for (hh, v) in waits:
                            eng.wait_ge(hh, v)
                        if fn is None:
                            continue
                        ins = fn(eng)
                        for (hh, amt) in incs:
                            ins = ins.then_inc(hh, amt)
                return body

            getattr(blk, eng_name)(make_body(items))

    p3.close()
    es.close()
    return nc


_CACHE = {}


def _get_program():
    if "nc" not in _CACHE:
        _CACHE["nc"] = build_program()
    return _CACHE["nc"]


def _prep_inputs(c, inputs):
    bidx, gg = c // 2, c % 2
    f32 = np.float32
    x = np.asarray(inputs["L0_attn_x"], f32)
    qkvw = np.asarray(inputs["L0_attn_attn_qkvw"], f32)
    qkvb = np.asarray(inputs["L0_attn_attn_qkvb"], f32)
    ow = np.asarray(inputs["L0_attn_attn_ow"], f32)
    obv = np.asarray(inputs["L0_attn_attn_ob"], f32)
    W = np.asarray(inputs["L0_ffn_W"], f32)
    bv = np.asarray(inputs["L0_ffn_b"], f32)
    A = np.asarray(inputs["L0_ffn_A"], f32)
    Bm = np.asarray(inputs["L0_ffn_B"], f32)
    gw = np.asarray(inputs["L0_ffn_gate_w"], f32)
    gb = np.asarray(inputs["L0_ffn_gate_b"], f32)

    own = slice(2048 * gg, 2048 * gg + 2048)
    oth = slice(2048 * (1 - gg), 2048 * (1 - gg) + 2048)
    x_loc = np.ascontiguousarray(np.concatenate([x[bidx, own], x[bidx, oth]], 0))

    qcols = qkvw[:, 2048 * gg:2048 * gg + 2048]
    kcols = qkvw[:, 4096 + 2048 * gg:4096 + 2048 * gg + 2048]
    qk = np.concatenate([qcols, kcols], 1)
    wqk_h = np.ascontiguousarray(
        qk.reshape(32, 128, 32, 128).transpose(2, 1, 0, 3).reshape(32, 128, -1))
    vcols = qkvw[:, 8192 + 2048 * gg:8192 + 2048 * gg + 2048]
    wv_h = np.ascontiguousarray(
        vcols.reshape(32, 128, 8, 256).transpose(2, 1, 0, 3).reshape(8, 128, -1))
    qkvb_qk = np.ascontiguousarray(np.concatenate(
        [qkvb[2048 * gg:2048 * gg + 2048],
         qkvb[4096 + 2048 * gg:4096 + 2048 * gg + 2048]]).reshape(32, 128).T)
    vb = np.ascontiguousarray(qkvb[8192 + 2048 * gg:8192 + 2048 * gg + 2048])
    ow_p = np.concatenate([ow[2048 * gg:2048 * gg + 2048],
                           ow[2048 * (1 - gg):2048 * (1 - gg) + 2048]], 0)
    ow_hh = np.ascontiguousarray(
        ow_p.reshape(32, 128, 16, 256).transpose(2, 1, 0, 3).reshape(16, 128, -1))
    B_h = np.ascontiguousarray(
        Bm.reshape(32, 128, 64).transpose(1, 0, 2).reshape(128, -1))

    bf = ml_dtypes.bfloat16
    lev = np.array([0] * 8 + [1] * 8 + [2] * 16 + [3] * 32, f32)
    return {
        "x_l": x_loc, "wqk": wqk_h, "wv": wv_h, "qkvb_qk": qkvb_qk,
        "vb_row": vb, "ow_h": ow_hh, "ob_row": obv, "Wn": W, "b_row": bv,
        "A_in": A, "B_in": B_h, "gw_in": gw, "gb_in": gb,
        "identb_in": np.eye(128, dtype=bf),
        "identf_in": np.eye(128, dtype=f32),
        "onesf_in": np.ones((1, 128), f32),
        "onesb_in": np.ones((128, 1), bf),
        "levels_in": np.tile(lev, (128, 1)),
        "masks_in": np.tile(np.array([0.0, 1.0] if gg == 0 else [1.0, 0.0], f32),
                            (128, 1)),
        "eps_in": np.full((128, 1), 1e-5, f32),
    }


def kernel(**inputs):
    nc = _get_program()
    in_maps = [_prep_inputs(c, inputs) for c in range(8)]
    res = run_bass_kernel_spmd(nc, in_maps, core_ids=list(range(8)))
    out = np.empty((4, 4096, 4096), np.float32)
    for c in range(8):
        bidx, gg = c // 2, c % 2
        out[bidx, 2048 * gg:2048 * gg + 2048] = res.results[c]["out"]
    return out


# version 20
# speedup vs baseline: 1.0715x; 1.0187x over previous
import sys

for _p in ("/opt/trn_rl_repo", "/root/.axon_site/_ro/trn_rl_repo"):
    if _p not in sys.path:
        sys.path.append(_p)

import contextlib

import numpy as np
import ml_dtypes

import concourse.bass as bass
import concourse.mybir as mybir
from concourse.bass_utils import run_bass_kernel_spmd

F32 = mybir.dt.float32
BF16 = mybir.dt.bfloat16
AF = mybir.ActivationFunctionType
ALU = mybir.AluOpType
AX = mybir.AxisListType

H = 4096
DC = 32
NH = 16
SL = 4096
TOWN = 2048
QB = 512
NSL = 8
QORDER = [4, 0, 5, 1, 6, 2, 7, 3]
SCALE = float(1.0 / np.sqrt(128.0))
EPS = 1e-5
GROUPS = [[0, 1], [2, 3], [4, 5], [6, 7]]
OB3 = 256
NOB3 = H // OB3


class Sem:
    def __init__(self, handle):
        self.h = handle
        self.n = 0


class Prog:

    def __init__(self):
        self.ops = []

    def op(self, engine, fn, waits=(), inc=None, inc16=None):
        incs = []
        if inc is not None:
            inc.n += 1
            incs.append((inc.h, 1))
        if inc16 is not None:
            inc16.n += 16
            incs.append((inc16.h, 16))
        self.ops.append((engine, tuple((s.h, v) for (s, v) in waits), fn, incs))

    def wait(self, engine, waits):
        self.ops.append((engine, tuple((s.h, v) for (s, v) in waits), None, []))


def _check_deadlock(prog):
    qs = {}
    for it in prog.ops:
        qs.setdefault(it[0], []).append(it)
    heads = {k: 0 for k in qs}
    counts = {}
    progress = True
    while progress:
        progress = False
        for eng, lst in qs.items():
            while heads[eng] < len(lst):
                _, waits, _, incs = lst[heads[eng]]
                if all(counts.get(h, 0) >= v for (h, v) in waits):
                    for (h, amt) in incs:
                        counts[h] = counts.get(h, 0) + amt
                    heads[eng] += 1
                    progress = True
                else:
                    break
    stuck = {e: (heads[e], len(l)) for e, l in qs.items() if heads[e] < len(l)}
    if stuck:
        lines = []
        for e, (i, n) in stuck.items():
            _, waits, _, _ = qs[e][i]
            lines.append(f"{e}: op {i}/{n} waits " + str(
                [(getattr(h, "name", h), v, counts.get(h, 0)) for h, v in waits]))
        raise RuntimeError("sync deadlock:\n" + "\n".join(lines))


def build_program():
    nc = bass.Bass("TRN2", target_bir_lowering=False)
    es = contextlib.ExitStack()

    def dram(name, shape, dt, kind=None):
        if kind is None:
            return nc.dram_tensor(name, shape, dt).ap()
        return nc.dram_tensor(name, shape, dt, kind=kind).ap()

    x_l = dram("x_l", [SL, H], F32, "ExternalInput")
    wqk = dram("wqk", [32, 128, DC * 128], F32, "ExternalInput")
    wv = dram("wv", [8, 128, DC * 256], F32, "ExternalInput")
    qkvb_qk = dram("qkvb_qk", [128, 32], F32, "ExternalInput")
    vb_row = dram("vb_row", [2048], F32, "ExternalInput")
    ow_h = dram("ow_h", [16, 128, DC * 256], F32, "ExternalInput")
    ob_row = dram("ob_row", [H], F32, "ExternalInput")
    Wn = dram("Wn", [H, H], F32, "ExternalInput")
    b_row = dram("b_row", [H], F32, "ExternalInput")
    A_in = dram("A_in", [64, H], F32, "ExternalInput")
    B_in = dram("B_in", [128, DC * 64], F32, "ExternalInput")
    gw_in = dram("gw_in", [4, H], F32, "ExternalInput")
    gb_in = dram("gb_in", [4], F32, "ExternalInput")
    identb_in = dram("identb_in", [128, 128], BF16, "ExternalInput")
    identf_in = dram("identf_in", [128, 128], F32, "ExternalInput")
    onesf_in = dram("onesf_in", [1, 128], F32, "ExternalInput")
    onesb_in = dram("onesb_in", [128, 1], BF16, "ExternalInput")
    levels_in = dram("levels_in", [128, 64], F32, "ExternalInput")
    masks_in = dram("masks_in", [128, 2], F32, "ExternalInput")
    eps_in = dram("eps_in", [128, 1], F32, "ExternalInput")
    out_d = dram("out", [TOWN, H], F32, "ExternalOutput")

    qT_s = dram("qT_s", [NH, 128, SL], BF16)
    kT_s = dram("kT_s", [NH, 128, SL], BF16)
    v_s = dram("v_s", [NH, 128, DC, 128], BF16)
    cc_in = [dram(f"cc_in{j}", [2 * TOWN, QB], BF16) for j in range(4)]
    cc_out = [dram(f"cc_out{j}", [TOWN, QB], BF16) for j in range(4)]
    ow_bf_s = dram("ow_bf_s", [16, 128, DC * 256], BF16)
    x2_s = dram("x2_s", [TOWN, H], F32)
    ln2_s = dram("ln2_s", [TOWN, H], BF16)

    def sbuf(name, shape, dt, scope=es):
        return scope.enter_context(nc.sbuf_tensor(name, shape, dt))

    def psum(name, shape, dt):
        return es.enter_context(nc.psum_tensor(name, shape, dt))

    identb = sbuf("identb", [128, 128], BF16)
    identf = sbuf("identf", [128, 128], F32)
    onesf = sbuf("onesf", [1, 128], F32)
    onesb = sbuf("onesb", [128, 1], BF16)
    levels = sbuf("levels", [128, 64], F32)
    masks = sbuf("masks", [128, 2], F32)
    qkvb_sb = sbuf("qkvb_sb", [128, 32], F32)
    row_sb = sbuf("row_sb", [1, 1024], F32)
    gb_sb = sbuf("gb_sb", [4, 1], F32)
    eps_sb = sbuf("eps_sb", [128, 1], F32)

    ps_a = psum("ps_a", [128, 1024], F32)
    ps_b = psum("ps_b", [128, 1024], F32)
    ps_c0 = psum("ps_c0", [128, 512], F32)
    ps_c1 = psum("ps_c1", [128, 512], F32)
    ps_d = psum("ps_d", [128, 512], F32)
    ps_t = psum("ps_t", [128, 1024], BF16)

    semnames = ["s_cst", "s_row", "s_ln", "s_sc", "s_tp", "s_te", "s_mm",
                "s_ev", "s_exp", "s_ctx", "s_dt", "s_dmm", "s_rc", "s_bc",
                "s_bcp", "s_ce", "s_ccm", "s_ccd", "s_cc", "s_rx", "s_op",
                "s_oe", "s_l2", "s_bar", "s_xsl", "s_wr",
                "s_xh0", "s_xh1", "s_wp0", "s_wp1", "s_stp0", "s_stp1",
                "s_vsp0", "s_vsp1", "s_kvp0", "s_kvp1", "s_x2p0", "s_x2p1",
                "s_xr2", "s_xr3", "s_l2b"]
    S = {n: Sem(es.enter_context(nc.semaphore(n))) for n in semnames}
    s_cst, s_row, s_ln, s_sc, s_tp, s_te = (S[k] for k in semnames[0:6])
    s_mm, s_ev, s_exp, s_ctx, s_dt, s_dmm = (S[k] for k in semnames[6:12])
    s_rc, s_bc, s_bcp, s_ce, s_ccm, s_ccd = (S[k] for k in semnames[12:18])
    s_cc, s_rx, s_op, s_oe, s_l2, s_bar = (S[k] for k in semnames[18:24])
    s_xsl, s_wr = S["s_xsl"], S["s_wr"]
    s_xh = [S["s_xh0"], S["s_xh1"]]
    s_xr = [S["s_xh0"], S["s_xh1"], S["s_xr2"], S["s_xr3"]]
    s_l2p = [S["s_l2"], S["s_l2b"]]
    s_wp = [S["s_wp0"], S["s_wp1"]]
    s_stp = [S["s_stp0"], S["s_stp1"]]
    s_vsp = [S["s_vsp0"], S["s_vsp1"]]
    s_kvp = [S["s_kvp0"], S["s_kvp1"]]
    s_x2p = [S["s_x2p0"], S["s_x2p1"]]

    P = Prog()

    for dst, src in [(identb, identb_in), (identf, identf_in), (onesf, onesf_in),
                     (onesb, onesb_in), (levels, levels_in), (masks, masks_in),
                     (qkvb_sb, qkvb_qk), (eps_sb, eps_in)]:
        P.op("sync", lambda e, d=dst, s=src: e.dma_start(out=d[:], in_=s[:]),
             inc16=s_cst)
    P.op("sync", lambda e: e.dma_start(
        out=gb_sb[:, :], in_=gb_in.rearrange("(p a) -> p a", a=1)), inc16=s_cst)
    consts_done = s_cst.n

    rb_state = {"tp": {}, "te": {}, "i": 0}

    def emit_row_bcast(row_src, n, dest_sb):
        row2 = row_src.rearrange("(a f) -> a f", a=1)
        rbanks = [ps_d, ps_c0, ps_c1]
        for i in range((n + 511) // 512):
            w = min(512, n - i * 512)
            gi = rb_state["i"]
            rb_state["i"] += 1
            par = gi % 2
            bank = rbanks[gi % 3]
            ldsem = [s_row, s_cst][par]
            wv = [(s_tp, rb_state["tp"][gi - 2])] if gi >= 2 else []
            P.op("sync", lambda e, i=i, w=w, par=par: e.dma_start(
                out=row_sb[:, par * 512:par * 512 + w],
                in_=row2[0:1, i * 512:i * 512 + w]),
                waits=wv, inc16=ldsem)
            ldv = ldsem.n
            wv2 = [(ldsem, ldv), (s_cst, consts_done)]
            if gi >= 3:
                wv2.append((s_te, rb_state["te"][gi - 3]))
            P.op("tensor", lambda e, w=w, par=par, bank=bank: e.matmul(
                bank[:, 0:w], lhsT=onesf[:, :],
                rhs=row_sb[:, par * 512:par * 512 + w],
                start=True, stop=True),
                waits=wv2, inc=s_tp)
            rb_state["tp"][gi] = s_tp.n
            P.op("scalar", lambda e, i=i, w=w, bank=bank: e.copy(
                out=dest_sb[:, i * 512:i * 512 + w], in_=bank[:, 0:w]),
                waits=[(s_tp, s_tp.n)], inc=s_te)
            rb_state["te"][gi] = s_te.n

    p1 = contextlib.ExitStack()
    vb_bc = sbuf("vb_bc", [128, 2048], BF16, p1)
    emit_row_bcast(vb_row, 2048, vb_bc)
    vbbc_done = s_te.n


    ln1T = sbuf("ln1T", [128, DC, 2048], BF16, p1)
    x_sb = [sbuf(f"x_sb{i}", [128, 2048], F32, p1) for i in range(2)]
    ln1h = sbuf("ln1h", [128, 2048], BF16, p1)
    w_u = [sbuf(f"w_u{i}", [128, 8192], BF16, p1) for i in range(2)]
    qkst = [sbuf(f"qkst{i}", [128, 512], BF16, p1) for i in range(2)]
    vst = [sbuf(f"vst{i}", [128, 256], BF16, p1) for i in range(2)]
    stats1 = sbuf("stats1", [128, 48], F32, p1)
    mv1 = [sbuf(f"mv1_{i}", [128, 2], F32, p1) for i in range(2)]
    rstd1 = [sbuf(f"rstd1_{i}", [128, 1], F32, p1) for i in range(2)]
    lnt1 = sbuf("lnt1", [128, 1], F32, p1)
    nmr1 = [sbuf(f"nmr1_{i}", [128, 1], F32, p1) for i in range(2)]

    x_free = {}
    ln_v = {}
    grp_idx = 0
    grp_ev = []
    qk_store_v = []
    v_store_v = []
    wu_free = {}
    wu_lidx = 0

    for tsb in range(2):
        t0 = tsb * 2048
        for tt in range(16):
            gt = tsb * 16 + tt
            row = t0 + tt * 128
            xv = {}
            for hf in range(2):
                w = [(s_sc, x_free[(gt - 1, hf)])] if gt >= 1 else \
                    [(s_cst, consts_done)]
                P.op("sync", lambda e, row=row, hf=hf: e.dma_start(
                    out=x_sb[hf][:],
                    in_=x_l[row:row + 128, hf * 2048:(hf + 1) * 2048]),
                    waits=w, inc16=s_xh[hf])
                xv[hf] = s_xh[hf].n
                for i in range(4):
                    P.op("vector", lambda e, hf=hf, i=i: e.bn_stats(
                        stats1[:, (hf * 4 + i) * 6:(hf * 4 + i) * 6 + 6],
                        x_sb[hf][:, i * 512:(i + 1) * 512]),
                        waits=[(s_xh[hf], xv[hf])],
                        inc=(s_ln if (hf == 1 and i == 3) else None))
            pp = gt % 2
            w = [(s_ln, s_ln.n)]
            if gt >= 2:
                w.append((s_sc, ln_v[gt - 2]))
            P.op("vector", lambda e, pp=pp: e.bn_aggr(mv1[pp][:], stats1[:]),
                 waits=w, inc=s_ln)
            P.op("scalar", lambda e, pp=pp: e.activation(
                out=lnt1[:], in_=mv1[pp][:, 1:2], func=AF.Ln,
                bias=eps_sb[:, 0:1]),
                waits=[(s_ln, s_ln.n)], inc=s_sc)
            ln_v[gt] = s_sc.n
            w = [(s_sc, s_sc.n)]
            if gt >= 2:
                w.append((s_sc, x_free[(gt - 2, 1)]))
            P.op("scalar", lambda e, pp=pp: e.activation(
                out=rstd1[pp][:], in_=lnt1[:], func=AF.Exp, scale=-0.5),
                waits=w, inc=s_sc)
            P.op("vector", lambda e, pp=pp: e.scalar_tensor_tensor(
                out=nmr1[pp][:], in0=mv1[pp][:, 0:1], scalar=-1.0,
                in1=rstd1[pp][:], op0=ALU.mult, op1=ALU.mult),
                waits=[(s_sc, s_sc.n)], inc=s_ln)
            nmr_v = s_ln.n
            for hf in range(2):
                P.op("scalar", lambda e, hf=hf, pp=pp: e.activation(
                    out=ln1h[:], in_=x_sb[hf][:], func=AF.Identity,
                    bias=nmr1[pp][:, 0:1], scale=rstd1[pp][:, 0:1]),
                    waits=[(s_ln, nmr_v), (s_tp, s_tp.n)], inc=s_sc)
                x_free[(gt, hf)] = s_sc.n
                lh = s_sc.n
                for grp in range(2):
                    for k in range(8):
                        c_loc = grp * 8 + k
                        w = [(s_sc, lh)]
                        if k == 0:
                            w.append((s_te, s_te.n))
                        P.op("tensor", lambda e, c_loc=c_loc, k=k: e.transpose(
                            ps_t[:, k * 128:(k + 1) * 128],
                            in_=ln1h[:, c_loc * 128:(c_loc + 1) * 128],
                            identity=identb[:]),
                            waits=w, inc=(s_tp if k == 7 else None))
                    c0 = hf * 16 + grp * 8
                    P.op("scalar", lambda e, c0=c0, tt=tt: e.copy(
                        out=ln1T[:, c0:c0 + 8, tt * 128:(tt + 1) * 128],
                        in_=ps_t[:].rearrange("p (c f) -> p c f", c=8)),
                        waits=[(s_tp, s_tp.n)], inc=s_te)
        ln1_done = s_te.n

        for ft in range(32):
            w = [(s_te, ln1_done)]
            if wu_lidx >= 2:
                w.append((s_mm, wu_free[wu_lidx - 2]))
            my_wu = wu_lidx % 2
            P.op("gpsimd", lambda e, ft=ft, my_wu=my_wu: e.dma_start(
                out=w_u[my_wu][:, 0:4096], in_=wqk[ft][:, 0:4096]),
                waits=w, inc16=s_wp[my_wu])
            wld = s_wp[my_wu].n
            wu_lidx += 1
            for tb in range(4):
                pq = [ps_a, ps_b][grp_idx % 2]
                w = [(s_wp[my_wu], wld), (s_te, ln1_done)]
                if grp_idx >= 2:
                    w.append((s_ev, grp_ev[grp_idx - 2]))
                for c in range(DC):
                    P.op("tensor", lambda e, c=c, tb=tb, pq=pq, my_wu=my_wu:
                         e.matmul(pq[:, 0:512],
                                  lhsT=w_u[my_wu][:, c * 128:(c + 1) * 128],
                                  rhs=ln1T[:, c, tb * 512:(tb + 1) * 512],
                                  start=(c == 0), stop=(c == DC - 1)),
                         waits=(w if c == 0 else ()),
                         inc=(s_mm if c == DC - 1 else None))
                wu_free[wu_lidx - 1] = s_mm.n
                qe = len(qk_store_v)
                w = [(s_mm, s_mm.n)]
                if qe >= 2:
                    w.append((s_stp[qe % 2], qk_store_v[qe - 2]))
                P.op("scalar", lambda e, ft=ft, pq=pq, i=qe % 2: e.activation(
                    out=qkst[i][:], in_=pq[:, 0:512], func=AF.Identity,
                    bias=qkvb_sb[:, ft:ft + 1]),
                    waits=w, inc=s_ev)
                grp_ev.append(s_ev.n)
                grp_idx += 1
                dst = qT_s if ft < 16 else kT_s
                hh = ft % 16
                P.op("sync", lambda e, dst=dst, hh=hh, t0=t0, tb=tb, i=qe % 2:
                     e.dma_start(
                         out=dst[hh, :, t0 + tb * 512:t0 + (tb + 1) * 512],
                         in_=qkst[i][:]),
                     waits=[(s_ev, s_ev.n)], inc16=s_stp[qe % 2])
                qk_store_v.append(s_stp[qe % 2].n)

        for fb in range(8):
            w = [(s_te, ln1_done)]
            if wu_lidx >= 2:
                w.append((s_mm, wu_free[wu_lidx - 2]))
            my_wu = wu_lidx % 2
            P.op("gpsimd", lambda e, fb=fb, my_wu=my_wu: e.dma_start(
                out=w_u[my_wu][:], in_=wv[fb][:]),
                waits=w, inc16=s_wp[my_wu])
            wld = s_wp[my_wu].n
            wu_lidx += 1
            for tt in range(16):
                pq = [ps_a, ps_b][grp_idx % 2]
                w = [(s_wp[my_wu], wld)]
                if grp_idx >= 2:
                    w.append((s_ev, grp_ev[grp_idx - 2]))
                for c in range(DC):
                    P.op("tensor", lambda e, c=c, tt=tt, pq=pq, my_wu=my_wu:
                         e.matmul(pq[:, 0:256],
                                  lhsT=ln1T[:, c, tt * 128:(tt + 1) * 128],
                                  rhs=w_u[my_wu][:, c * 256:(c + 1) * 256],
                                  start=(c == 0), stop=(c == DC - 1)),
                         waits=(w if c == 0 else ()),
                         inc=(s_mm if c == DC - 1 else None))
                wu_free[wu_lidx - 1] = s_mm.n
                ve = len(v_store_v)
                w = [(s_mm, s_mm.n), (s_te, vbbc_done)]
                if ve >= 2:
                    w.append((s_vsp[ve % 2], v_store_v[ve - 2]))
                P.op("vector", lambda e, fb=fb, pq=pq, i=ve % 2: e.tensor_tensor(
                    out=vst[i][:], in0=pq[:, 0:256],
                    in1=vb_bc[:, fb * 256:(fb + 1) * 256], op=ALU.add),
                    waits=w, inc=s_ev)
                grp_ev.append(s_ev.n)
                grp_idx += 1
                ktile = tsb * 16 + tt
                for hh in range(2):
                    head = fb * 2 + hh
                    P.op("sync", lambda e, head=head, ktile=ktile, hh=hh, i=ve % 2:
                         e.dma_start(out=v_s[head, :, ktile, :],
                                     in_=vst[i][:, hh * 128:(hh + 1) * 128]),
                         waits=[(s_ev, s_ev.n)], inc16=s_vsp[ve % 2])
                v_store_v.append(s_vsp[ve % 2].n)

    P.op("sync", lambda e: e.nop(),
         waits=[(s_stp[0], s_stp[0].n), (s_stp[1], s_stp[1].n),
                (s_vsp[0], s_vsp[0].n), (s_vsp[1], s_vsp[1].n)], inc=s_bar)
    P.op("tensor", lambda e: e.nop(), waits=[(s_mm, s_mm.n)], inc=s_bar)
    P.op("scalar", lambda e: e.nop(), waits=[(s_ev, s_ev.n)], inc=s_bar)
    P.op("vector", lambda e: e.nop(), waits=[(s_ev, s_ev.n)], inc=s_bar)
    P.op("gpsimd", lambda e: e.nop(),
         waits=[(s_wp[0], s_wp[0].n), (s_wp[1], s_wp[1].n)], inc=s_bar)
    bar1 = s_bar.n
    p1.close()

    p2 = contextlib.ExitStack()
    probsT2 = [sbuf(f"probsT{i}", [128, 16, 1024], BF16, p2) for i in range(2)]
    kT_sb = [sbuf(f"kT_sb{i}", [128, SL], BF16, p2) for i in range(2)]
    v_sb = [sbuf(f"v_sb{i}", [128, DC, 128], BF16, p2) for i in range(2)]
    qT_sb = [sbuf(f"qT_sb{i}", [128, QB], BF16, p2) for i in range(2)]
    ctxT = sbuf("ctxT", [128, NH, QB], BF16, p2)
    rx = sbuf("rx", [128, NH, QB], BF16, p2)
    ccm = [sbuf(f"ccm{i}", [128, 1024], BF16, p2) for i in range(2)]
    acc4 = sbuf("acc4", [128, 4, 512], BF16, p2)
    dtmp = sbuf("dtmp", [128, 512], F32, p2)
    dfin = sbuf("dfin", [128, 512], F32, p2)
    den_bf = sbuf("den_bf", [128, 512], BF16, p2)
    recip = [sbuf(f"recip{i}", [1, 512], F32, p2) for i in range(2)]
    bcast_sb = [sbuf(f"bcast{i}", [128, 512], F32, p2) for i in range(2)]
    ow_cb = [sbuf(f"ow_cb{i}", [128, 4096], BF16, p2) for i in range(3)]
    xsl = [sbuf(f"xsl{i}", [128, 256], F32, p2) for i in range(4)]
    t1sl = sbuf("t1sl", [128, 256], F32, p2)
    x2st = [sbuf(f"x2st{i}", [128, 256], F32, p2) for i in range(2)]
    stats2 = sbuf("stats2", [128, 4, 96], F32, p2)
    mv2 = [sbuf(f"mv2_{i}", [128, 2], F32, p2) for i in range(2)]
    lnt2 = sbuf("lnt2", [128, 1], F32, p2)
    rstd2 = [sbuf(f"rstd2_{i}", [128, 1], F32, p2) for i in range(2)]
    nmr2 = [sbuf(f"nmr2_{i}", [128, 1], F32, p2) for i in range(2)]
    x2rd = [sbuf(f"x2rd{i}", [128, 512], F32, p2) for i in range(4)]
    ln2h = [sbuf(f"ln2h{i}", [128, 512], BF16, p2) for i in range(2)]
    ob_bc = sbuf("ob_bc", [128, H], BF16, p2)

    for e_ in ("sync", "tensor", "scalar", "vector", "gpsimd"):
        P.wait(e_, [(s_bar, bar1)])
    emit_row_bcast(ob_row, H, ob_bc)
    obbc_done = s_te.n

    ow_ld = 0
    op_group_v = []
    oe_vals = []
    x2_stores = {}
    l2_stores = {}
    x2_idx = 0
    x2_store_v = []
    ln2_apply_v = []
    ln2_st_v = []
    ln2_ln_v = {}
    ln2_gt = 0

    spill_v = {}
    ow_state = {"i": 0, "free": {}}
    oe_slot_end = {}
    ccd_hist = []
    n_cc_slots = [0]
    prev_op_end = None
    kv_pref = {}

    def emit_kv_load(h, hg, par, qb_ld):
        w = [(s_ctx, (hg - 1) * 16)] if hg >= 2 else []
        P.op("sync", lambda e, h=h, par=par: e.dma_start(
            out=kT_sb[par][:], in_=kT_s[h]), waits=w, inc16=s_kvp[par])
        P.op("sync", lambda e, h=h, par=par: e.dma_start(
            out=v_sb[par][:].rearrange("p c f -> p (c f)"),
            in_=v_s[h].rearrange("p c f -> p (c f)")), inc16=s_kvp[par])
        P.op("sync", lambda e, h=h, par=par, qb_ld=qb_ld: e.dma_start(
            out=qT_sb[par][:], in_=qT_s[h, :, qb_ld * QB:(qb_ld + 1) * QB]),
            inc16=s_kvp[par])
        return s_kvp[par].n

    for s in range(NSL):
        qb = QORDER[s]
        if s in (0, 2):
            for ob in range((s // 2) * 8, (s // 2) * 8 + 8):
                i = ob % 2
                P.op("gpsimd", lambda e, ob=ob: e.dma_start(
                    out=ow_bf_s[ob][:], in_=ow_h[ob][:]),
                    waits=[(s_bar, bar1)], inc16=s_vsp[i])
                spill_v[ob] = s_vsp[i].n
        if qb < 4:
            j = qb
            w = [(s_cc, j + 1)]
            if op_group_v:
                w.append((s_op, op_group_v[-1]))
            P.op("gpsimd", lambda e, j=j: e.dma_start(
                out=rx[:], in_=cc_out[j].rearrange("(h p) f -> p h f", p=128)),
                waits=w, inc16=s_rx)
        rx_v = s_rx.n
        has_prev_op = op_group_v and prev_op_end is not None

        def emit_den(hh):
            w = [(s_dt, hh + 1)]
            if hh >= 1:
                w.append((s_bcp, hh))
            else:
                w.append((s_te, obbc_done))
            P.op("tensor", lambda e, hh=hh: e.matmul(
                ps_d[0:1, :], lhsT=onesb[:], rhs=den_bf[:], start=True,
                stop=True), waits=w, inc=s_dmm)

        def emit_recip(hh):
            w = [(s_dmm, hh + 1)]
            if hh >= 2:
                w.append((s_bc, hh - 1))
            P.op("vector", lambda e, i=hh % 2: e.reciprocal(
                recip[i][:], ps_d[0:1, :]), waits=w, inc=s_rc)

        def emit_bcast(hh):
            P.op("tensor", lambda e, i=hh % 2: e.matmul(
                ps_d[:], lhsT=onesf[:], rhs=recip[i][:], start=True, stop=True),
                waits=[(s_rc, hh + 1)], inc=s_bc)

        def emit_bcopy(hh):
            w = [(s_bc, hh + 1)]
            if hh >= 2:
                w.append((s_ce, hh - 1))
            P.op("scalar", lambda e, i=hh % 2: e.copy(
                out=bcast_sb[i][:], in_=ps_d[:]), waits=w, inc=s_bcp)

        def emit_evac(hh):
            hloc = hh % NH
            pc2 = [ps_c0, ps_c1][hh % 2]
            P.op("vector", lambda e, hloc=hloc, pc2=pc2, i=hh % 2: e.tensor_tensor(
                ctxT[:, hloc, :], pc2[:], bcast_sb[i][:], op=ALU.mult),
                waits=[(s_ctx, (hh + 1) * 16), (s_bcp, hh + 1)], inc=s_ce)

        def emit_ccm(hh):
            hloc = hh % NH
            ss = QORDER[hh // NH] - 4
            w = [(s_ccd, ccd_hist[-2])] if len(ccd_hist) >= 2 else []
            P.op("vector", lambda e, hloc=hloc, i=hh % 2: e.tensor_scalar(
                out=ccm[i][:, 0:512], in0=ctxT[:, hloc, :],
                scalar1=masks[:, 0:1], scalar2=None, op0=ALU.mult),
                waits=w)
            P.op("vector", lambda e, hloc=hloc, i=hh % 2: e.tensor_scalar(
                out=ccm[i][:, 512:1024], in0=ctxT[:, hloc, :],
                scalar1=masks[:, 1:2], scalar2=None, op0=ALU.mult),
                inc=s_ccm)
            P.op("gpsimd", lambda e, ss=ss, hloc=hloc, i=hh % 2: e.dma_start(
                out=cc_in[ss][hloc * 128:(hloc + 1) * 128, :],
                in_=ccm[i][:, 0:512]),
                waits=[(s_ccm, s_ccm.n)], inc16=s_ccd)
            P.op("gpsimd", lambda e, ss=ss, hloc=hloc, i=hh % 2: e.dma_start(
                out=cc_in[ss][TOWN + hloc * 128:TOWN + (hloc + 1) * 128, :],
                in_=ccm[i][:, 512:1024]), inc16=s_ccd)
            ccd_hist.append(s_ccd.n)

        def emit_back(hh):
            emit_evac(hh)
            if QORDER[hh // NH] >= 4:
                emit_ccm(hh)

        for h in range(NH):
            hg = s * NH + h
            par = hg % 2
            if hg in kv_pref:
                kvv = kv_pref.pop(hg)
            else:
                kvv = emit_kv_load(h, hg, par, qb)

            sc_base = s_mm.n
            for gk in range(16):
                G = hg * 16 + gk
                pq = [ps_a, ps_b][G % 2]
                w = [(s_kvp[par], kvv)]
                if G >= 2:
                    w.append((s_exp, G - 1))
                if gk == 0 and h == 0 and prev_op_end is not None:
                    w.append((s_oe, prev_op_end))
                for half in range(2):
                    kt = gk * 2 + half
                    P.op("tensor", lambda e, kt=kt, half=half, par=par, pq=pq:
                         e.matmul(pq[:, half * 512:(half + 1) * 512],
                                  lhsT=kT_sb[par][:, kt * 128:(kt + 1) * 128],
                                  rhs=qT_sb[par][:], start=True, stop=True),
                         waits=(w if half == 0 else ()),
                         inc=(s_mm if half == 1 else None))
            if hg >= 1 and (hg - 1) % NH != NH - 1:
                emit_den(hg - 1)

            for gk in range(16):
                G = hg * 16 + gk
                pq = [ps_a, ps_b][G % 2]
                w = [(s_mm, sc_base + gk + 1)]
                if hg >= 2:
                    w.append((s_ctx, (hg - 2) * 16 + gk + 1))
                    w.append((s_dt, hg - 1))
                P.op("scalar", lambda e, gk=gk, pq=pq, pb=hg % 2: e.activation(
                    out=probsT2[pb][:, gk, :], in_=pq[:], func=AF.Exp, scale=SCALE),
                    waits=w, inc=s_exp)
            if hg >= 1 and (hg - 1) % NH != NH - 1:
                emit_bcopy(hg - 1)

            if hg >= 2 and (hg - 2) % NH < NH - 2:
                emit_back(hg - 2)
            if hg >= 1 and (hg - 1) % NH != NH - 1:
                emit_recip(hg - 1)
            for gk in range(16):
                G = hg * 16 + gk
                q = gk // 4
                w = [(s_exp, G + 1)]
                if gk % 4 == 0:
                    P.op("vector", lambda e, gk=gk, q=q, pb=hg % 2: e.tensor_tensor(
                        acc4[:, q, :], probsT2[pb][:, gk, 0:512],
                        probsT2[pb][:, gk, 512:1024], op=ALU.add), waits=w)
                else:
                    P.op("vector", lambda e, gk=gk, q=q, pb=hg % 2: e.tensor_tensor(
                        acc4[:, q, :], acc4[:, q, :], probsT2[pb][:, gk, 0:512],
                        op=ALU.add), waits=w)
                    P.op("vector", lambda e, gk=gk, q=q, pb=hg % 2: e.tensor_tensor(
                        acc4[:, q, :], acc4[:, q, :], probsT2[pb][:, gk, 512:1024],
                        op=ALU.add))
            P.op("vector", lambda e: e.tensor_tensor(
                dfin[:], acc4[:, 0, :], acc4[:, 1, :], op=ALU.add))
            P.op("vector", lambda e: e.tensor_tensor(
                dtmp[:], acc4[:, 2, :], acc4[:, 3, :], op=ALU.add))
            P.op("vector", lambda e: e.tensor_tensor(
                dfin[:], dfin[:], dtmp[:], op=ALU.add))
            w = [(s_dmm, hg)] if hg >= 1 else []
            P.op("vector", lambda e: e.tensor_copy(den_bf[:], dfin[:]),
                 waits=w, inc=s_dt)

            pc = [ps_c0, ps_c1][hg % 2]
            for p in range(16):
                w = [(s_exp, hg * 16 + p + 1)]
                if p == 0 and hg >= 2:
                    w.append((s_ce, hg - 1))
                if p == 0 and h < 2 and prev_op_end is not None:
                    w.append((s_oe, prev_op_end))
                for half in range(2):
                    kt = p * 2 + half
                    P.op("tensor", lambda e, kt=kt, p=p, half=half, par=par,
                         pc=pc, pb=hg % 2: e.matmul(
                             pc[:], lhsT=v_sb[par][:, kt, :],
                             rhs=probsT2[pb][:, p, half * 512:(half + 1) * 512],
                             start=(kt == 0), stop=(kt == 31)),
                         waits=(w if half == 0 else ()),
                         inc=(s_ctx if half == 1 else None))
            if hg >= 1 and (hg - 1) % NH != NH - 1:
                emit_bcast(hg - 1)

        hg_last = s * NH + NH - 1
        emit_den(hg_last)
        emit_recip(hg_last)
        emit_bcast(hg_last)
        emit_bcopy(hg_last)
        emit_back(hg_last - 1)
        emit_back(hg_last)

        if qb >= 4:
            n_cc_slots[0] += 1
            cj = qb - 4
            P.op("gpsimd", lambda e, cj=cj: e.collective_compute(
                "ReduceScatter", ALU.add, ins=[cc_in[cj][:]],
                outs=[cc_out[cj][:]], replica_groups=GROUPS),
                waits=[(s_ccd, n_cc_slots[0] * NH * 32)], inc=s_cc)
        else:
            j = qb
            equads = [(ps_a, 0, 256), (ps_a, 512, 256), (ps_b, 0, 256),
                      (ps_b, 512, 256)]
            oquads = [(ps_c0, 0, 256), (ps_c0, 256, 256), (ps_c1, 0, 256),
                      (ps_c1, 256, 256)]
            slot_exp = (s + 1) * NH * 16
            slot_ce = (s + 1) * NH
            oe_slot0 = s_oe.n
            use_spill = s >= 4
            ow_sems = [s_wp[0], s_wp[1], s_row]
            for ob in range(16):
                quads = equads if ob % 2 == 0 else oquads
                hb = {}
                hv = {}
                for half in range(2):
                    oi = ow_state["i"]
                    ow_state["i"] += 1
                    bi = oi % 3
                    hb[half] = bi
                    w = []
                    if bi in ow_state["free"]:
                        w.append(ow_state["free"][bi])
                    if use_spill and ob < 2 and half == 0:
                        w.append((s_vsp[0], s_vsp[0].n))
                        w.append((s_vsp[1], s_vsp[1].n))
                    osrc = ow_bf_s[ob] if use_spill else ow_h[ob]
                    P.op("gpsimd", lambda e, osrc=osrc, bi=bi, half=half:
                         e.dma_start(out=ow_cb[bi][:],
                                     in_=osrc[:, half * 4096:(half + 1) * 4096]),
                         waits=w, inc16=ow_sems[bi])
                    hv[half] = ow_sems[bi].n
                ow_ld += 1
                for tt in range(4):
                    w = []
                    if len(oe_vals) >= 4:
                        w.append((s_oe, oe_vals[len(oe_vals) - 4 + tt]))
                    P.op("sync", lambda e, qb=qb, tt=tt, ob=ob:
                         e.dma_start(
                             out=xsl[tt][:],
                             in_=x_l[qb * 512 + tt * 128:qb * 512 + (tt + 1) * 128,
                                     ob * 256:(ob + 1) * 256]),
                         waits=w, inc16=s_xsl)
                xslv = s_xsl.n
                w = [(ow_sems[hb[0]], hv[0]), (ow_sems[hb[1]], hv[1]),
                     (s_exp, slot_exp), (s_ce, slot_ce), (s_rx, rx_v)]
                if ob >= 2:
                    w.append((s_oe, oe_vals[-5]))
                for cf in range(DC):
                    src_ = ctxT if cf < 16 else rx
                    hh = cf % 16
                    bi = hb[cf // 16]
                    for tt in range(4):
                        pq, off, _qw = quads[tt]
                        P.op("tensor", lambda e, src_=src_, hh=hh, tt=tt, cf=cf,
                             pq=pq, off=off, bi=bi: e.matmul(
                                 pq[:, off:off + 256],
                                 lhsT=src_[:, hh, tt * 128:(tt + 1) * 128],
                                 rhs=ow_cb[bi][:, (cf % 16) * 256:
                                               (cf % 16 + 1) * 256],
                                 start=(cf == 0), stop=(cf == DC - 1)),
                             waits=(w if (cf == 0 and tt == 0) else ()),
                             inc=(s_op if (tt == 3 and cf in (15, DC - 1))
                                  else None))
                    if cf == 15:
                        ow_state["free"][hb[0]] = (s_op, s_op.n)
                op_group_v.append(s_op.n)
                ow_state["free"][hb[1]] = (s_op, s_op.n)
                for tt in range(4):
                    gi = ob * 4 + tt
                    pq, off, _qw = quads[tt]
                    P.op("vector", lambda e, pq=pq, off=off, ob=ob: e.tensor_tensor(
                        t1sl[:], pq[:, off:off + 256],
                        ob_bc[:, ob * 256:(ob + 1) * 256], op=ALU.add),
                        waits=[(s_op, s_op.n), (s_xsl, xslv)])
                    ii = gi % 2
                    w2 = [(s_x2p[x2_idx % 2], x2_store_v[x2_idx - 2])] \
                        if x2_idx >= 2 else []
                    P.op("vector", lambda e, ii=ii, tt=tt: e.tensor_tensor(
                        x2st[ii][:], t1sl[:], xsl[tt][:], op=ALU.add), waits=w2)
                    P.op("vector", lambda e, ii=ii, tt=tt, ob=ob: e.bn_stats(
                        stats2[:, tt, ob * 6:(ob + 1) * 6], x2st[ii][:]),
                        inc=s_oe)
                    oe_vals.append(s_oe.n)
                    P.op("sync", lambda e, ii=ii, qb=qb, tt=tt, ob=ob: e.dma_start(
                        out=x2_s[qb * 512 + tt * 128:qb * 512 + (tt + 1) * 128,
                                 ob * 256:(ob + 1) * 256],
                        in_=x2st[ii][:]),
                        waits=[(s_oe, s_oe.n)], inc16=s_x2p[x2_idx % 2])
                    x2_store_v.append(s_x2p[x2_idx % 2].n)
                    x2_idx += 1
            x2_stores[qb] = (s_x2p[0].n, s_x2p[1].n)
            oe_slot_end[s] = s_oe.n
            prev_op_end = s_oe.n
            if s + 1 < NSL:
                hgn = (s + 1) * NH
                kv_pref[hgn] = emit_kv_load(0, hgn, hgn % 2, QORDER[s + 1])
            for tt in range(4):
                pp = ln2_gt % 2
                w = [(s_oe, s_oe.n)]
                if ln2_gt >= 2:
                    w.append((s_sc, ln2_ln_v[ln2_gt - 2]))
                P.op("vector", lambda e, tt=tt, pp=pp: e.bn_aggr(
                    mv2[pp][:], stats2[:, tt, :]), waits=w, inc=s_ln)
                P.op("scalar", lambda e, pp=pp: e.activation(
                    out=lnt2[:], in_=mv2[pp][:, 1:2], func=AF.Ln,
                    bias=eps_sb[:, 0:1]),
                    waits=[(s_ln, s_ln.n)], inc=s_sc)
                ln2_ln_v[ln2_gt] = s_sc.n
                w = [(s_sc, s_sc.n)]
                if ln2_gt >= 2:
                    w.append((s_sc, ln2_apply_v[8 * (ln2_gt - 2) + 7]))
                P.op("scalar", lambda e, pp=pp: e.activation(
                    out=rstd2[pp][:], in_=lnt2[:], func=AF.Exp, scale=-0.5),
                    waits=w, inc=s_sc)
                P.op("vector", lambda e, pp=pp: e.scalar_tensor_tensor(
                    out=nmr2[pp][:], in0=mv2[pp][:, 0:1], scalar=-1.0,
                    in1=rstd2[pp][:], op0=ALU.mult, op1=ALU.mult),
                    waits=[(s_sc, s_sc.n)], inc=s_ln)
                nv = s_ln.n
                ln2_gt += 1
                for hf in range(8):
                    k = len(ln2_apply_v)
                    ii = k % 4
                    lb = k % 2
                    w = [(s_x2p[0], x2_stores[qb][0]), (s_x2p[1], x2_stores[qb][1])]
                    if k >= 4:
                        w.append((s_sc, ln2_apply_v[k - 4]))
                    P.op(["sync", "gpsimd"][k % 2],
                         lambda e, qb=qb, tt=tt, hf=hf, ii=ii: e.dma_start(
                             out=x2rd[ii][:],
                             in_=x2_s[qb * 512 + tt * 128:qb * 512 + (tt + 1) * 128,
                                      hf * 512:(hf + 1) * 512]),
                         waits=w, inc16=s_xr[ii])
                    xv = s_xr[ii].n
                    lw = [(s_ln, nv), (s_xr[ii], xv)]
                    if k >= 2:
                        lw.append((s_l2p[lb], ln2_st_v[k - 2]))
                    P.op("scalar", lambda e, ii=ii, pp=pp, lb=lb: e.activation(
                        out=ln2h[lb][:], in_=x2rd[ii][:], func=AF.Identity,
                        bias=nmr2[pp][:, 0:1], scale=rstd2[pp][:, 0:1]),
                        waits=lw, inc=s_sc)
                    ln2_apply_v.append(s_sc.n)
                    P.op(["gpsimd", "sync"][k % 2],
                         lambda e, qb=qb, tt=tt, hf=hf, lb=lb: e.dma_start(
                             out=ln2_s[qb * 512 + tt * 128:qb * 512 + (tt + 1) * 128,
                                       hf * 512:(hf + 1) * 512],
                             in_=ln2h[lb][:]),
                         waits=[(s_sc, s_sc.n)], inc16=s_l2p[lb])
                    ln2_st_v.append(s_l2p[lb].n)
            l2_stores[qb] = (s_l2p[0].n, s_l2p[1].n)

    P.op("sync", lambda e: e.nop(),
         waits=[(s_l2p[0], s_l2p[0].n), (s_l2p[1], s_l2p[1].n),
                (s_x2p[0], s_x2p[0].n), (s_x2p[1], s_x2p[1].n)], inc=s_bar)
    P.op("tensor", lambda e: e.nop(), waits=[(s_op, s_op.n)], inc=s_bar)
    P.op("scalar", lambda e: e.nop(), waits=[(s_bcp, s_bcp.n)], inc=s_bar)
    P.op("vector", lambda e: e.nop(), waits=[(s_ln, s_ln.n)], inc=s_bar)
    P.op("gpsimd", lambda e: e.nop(), waits=[(s_cc, 4)], inc=s_bar)
    bar2 = s_bar.n
    p2.close()

    p3 = contextlib.ExitStack()
    ln2T = sbuf("ln2T", [128, DC, 1024], BF16, p3)
    ln2rd = [sbuf(f"ln2rd{i}", [128, H], BF16, p3) for i in range(2)]
    WT = [sbuf(f"WT{i}", [128, DC, OB3], BF16, p3) for i in range(2)]
    wrow = [sbuf(f"wrow{i}", [128, 2048], F32, p3) for i in range(4)]
    x2sl = [sbuf(f"x2sl{i}", [128, OB3], F32, p3) for i in range(2)]
    outst = [sbuf(f"outst{i}", [128, OB3], F32, p3) for i in range(2)]
    t1sl3 = sbuf("t1sl3", [128, OB3], F32, p3)
    b_bc = sbuf("b_bc", [128, H], BF16, p3)
    a_bf = sbuf("a_bf", [64, H], BF16, p3)
    AT_sb = sbuf("AT_sb", [128, DC, 64], BF16, p3)
    b2_bf = sbuf("b2_bf", [128, DC * 64], BF16, p3)
    BT_sb = sbuf("BT_sb", [64, H], BF16, p3)
    gw_bf = sbuf("gw_bf", [4, H], BF16, p3)
    gwT_sb = sbuf("gwT_sb", [128, DC, 4], BF16, p3)
    h1T = sbuf("h1T", [64, 1024], BF16, p3)
    mT_sb = sbuf("mT_sb", [64, 1024], BF16, p3)
    m_sb = sbuf("m_sb", [64, 1024], BF16, p3)
    g_f32 = sbuf("g_f32", [4, 512], F32, p3)
    gn_sb = sbuf("gn_sb", [128, 16], F32, p3)
    gm1 = sbuf("gm1", [128, 4], F32, p3)
    ga = sbuf("ga", [128, 1], F32, p3)
    maskn = sbuf("maskn", [128, 64], BF16, p3)

    for e_ in ("sync", "tensor", "scalar", "vector", "gpsimd"):
        P.wait(e_, [(s_bar, bar2)])

    emit_row_bcast(b_row, H, b_bc)

    P.op("gpsimd", lambda e: e.dma_start(out=a_bf[:], in_=A_in[:]), inc16=s_wr)
    P.op("gpsimd", lambda e: e.dma_start(out=b2_bf[:], in_=B_in[:]), inc16=s_wr)
    P.op("gpsimd", lambda e: e.dma_start(out=gw_bf[:], in_=gw_in[:]), inc16=s_wr)
    pre_w = s_wr.n

    for grp in range(2):
        for k in range(16):
            c = grp * 16 + k
            w = [(s_wr, pre_w)]
            if k == 0:
                w.append((s_te, s_te.n))
            P.op("tensor", lambda e, c=c, k=k: e.transpose(
                ps_t[:, k * 64:(k + 1) * 64],
                in_=a_bf[0:64, c * 128:(c + 1) * 128],
                identity=identb[0:64, 0:64]),
                waits=w, inc=(s_tp if k == 15 else None))
        P.op("scalar", lambda e, grp=grp: e.copy(
            out=AT_sb[:, grp * 16:(grp + 1) * 16, :],
            in_=ps_t[:].rearrange("p (c f) -> p c f", c=16)),
            waits=[(s_tp, s_tp.n)], inc=s_te)
    for grp in range(4):
        for k in range(8):
            c = grp * 8 + k
            w = [(s_wr, pre_w)]
            if k == 0:
                w.append((s_te, s_te.n))
            P.op("tensor", lambda e, c=c, k=k: e.transpose(
                ps_t[0:64, k * 128:(k + 1) * 128],
                in_=b2_bf[:, c * 64:(c + 1) * 64], identity=identb[:]),
                waits=w, inc=(s_tp if k == 7 else None))
        P.op("scalar", lambda e, grp=grp: e.copy(
            out=BT_sb[0:64, grp * 1024:(grp + 1) * 1024], in_=ps_t[0:64, :]),
            waits=[(s_tp, s_tp.n)], inc=s_te)
    for k in range(DC):
        w = [(s_wr, pre_w)]
        if k == 0:
            w.append((s_te, s_te.n))
        P.op("tensor", lambda e, k=k: e.transpose(
            ps_t[:, k * 4:(k + 1) * 4], in_=gw_bf[0:4, k * 128:(k + 1) * 128],
            identity=identb[0:4, 0:4]),
            waits=w, inc=(s_tp if k == DC - 1 else None))
    P.op("scalar", lambda e: e.copy(
        out=gwT_sb[:], in_=ps_t[:, 0:128].rearrange("p (c f) -> p c f", c=32)),
        waits=[(s_tp, s_tp.n)], inc=s_te)

    ffn_grp = 0
    wtb_state = {"g": 0, "bcp": {}}
    wrow_half_tp = {}
    nonlocal_state = {"ffn_grp": 0, "out_ct": 0}
    pend_mm = None
    out_store_v = []
    g_tp_v = {}
    gn_free_v = {}
    ffn_grp_mm = []
    wrow_tp = {}
    out_ct = 0
    ln2rd_tp = {}

    for th in range(2):
        tbase = th * 1024
        if pend_mm is not None:
            pend_mm()
            pend_mm = None
        for tt in range(8):
            gti = th * 8 + tt
            row = tbase + tt * 128
            qb_src = row // 512
            w = [(s_l2p[0], l2_stores[qb_src][0]),
                 (s_l2p[1], l2_stores[qb_src][1])]
            if gti >= 2:
                w.append((s_tp, ln2rd_tp[gti - 2]))
            P.op("sync", lambda e, row=row, i=tt % 2: e.dma_start(
                out=ln2rd[i][:], in_=ln2_s[row:row + 128, :]),
                waits=w, inc16=s_xh[tt % 2])
            xv = s_xh[tt % 2].n
            for grp in range(4):
                for k in range(8):
                    c = grp * 8 + k
                    w2 = [(s_xh[tt % 2], xv)]
                    if k == 0:
                        w2.append((s_te, s_te.n))
                    P.op("tensor", lambda e, c=c, k=k, i=tt % 2: e.transpose(
                        ps_t[:, k * 128:(k + 1) * 128],
                        in_=ln2rd[i][:, c * 128:(c + 1) * 128],
                        identity=identb[:]),
                        waits=w2, inc=(s_tp if k == 7 else None))
                P.op("scalar", lambda e, grp=grp, tt=tt: e.copy(
                    out=ln2T[:, grp * 8:(grp + 1) * 8, tt * 128:(tt + 1) * 128],
                    in_=ps_t[:].rearrange("p (c f) -> p c f", c=8)),
                    waits=[(s_tp, s_tp.n)], inc=s_te)
            ln2rd_tp[gti] = s_tp.n
        ln2T_done = s_te.n

        for tq in range(2):
            gtq = th * 2 + tq
            gate_ps_free = [(s_ev, s_ev.n)] if (th > 0 or tq > 0) else []
            if th > 0 and tq == 0:
                gate_ps_free.append((s_bcp, s_bcp.n))
            for c in range(DC):
                w = [(s_te, ln2T_done)] + (gate_ps_free if c == 0 else [])
                P.op("tensor", lambda e, c=c, tq=tq: e.matmul(
                    ps_c0[0:4, :], lhsT=gwT_sb[:, c, :],
                    rhs=ln2T[:, c, tq * 512:(tq + 1) * 512],
                    start=(c == 0), stop=(c == DC - 1)),
                    waits=(w if c == 0 else ()),
                    inc=(s_mm if c == DC - 1 else None))
            w = [(s_mm, s_mm.n)]
            if gtq >= 1:
                w.append((s_tp, g_tp_v[gtq - 1]))
            P.op("vector", lambda e: e.tensor_scalar(
                out=g_f32[:], in0=ps_c0[0:4, :], scalar1=gb_sb[0:4, 0:1],
                scalar2=None, op0=ALU.add),
                waits=w, inc=s_ev)
            gev = s_ev.n
            for k in range(4):
                w = [(s_ev, gev), (s_bcp, s_bcp.n)]
                if k > 0:
                    w = []
                P.op("tensor", lambda e, k=k: e.transpose(
                    ps_d[:, k * 4:(k + 1) * 4],
                    in_=g_f32[0:4, k * 128:(k + 1) * 128],
                    identity=identf[0:4, 0:4]),
                    waits=w, inc=(s_tp if k == 3 else None))
            g_tp_v[gtq] = s_tp.n
            w = [(s_tp, s_tp.n)]
            if gtq >= 1:
                w.append((s_ln, gn_free_v[gtq - 1]))
            P.op("scalar", lambda e: e.copy(out=gn_sb[:], in_=ps_d[:, 0:16]),
                 waits=w, inc=s_bcp)
            gnv = s_bcp.n
            for k in range(4):
                base = k * 4
                P.op("vector", lambda e, base=base: e.reduce_max(
                    gm1[:, 0:1], gn_sb[:, base + 1:base + 4], axis=AX.X),
                    waits=[(s_bcp, gnv)])
                P.op("vector", lambda e, base=base: e.reduce_max(
                    gm1[:, 1:2], gn_sb[:, base + 2:base + 4], axis=AX.X))
                P.op("vector", lambda e, base=base: e.reduce_max(
                    gm1[:, 2:3], gn_sb[:, base + 0:base + 2], axis=AX.X))
                P.op("vector", lambda e, base=base: e.reduce_max(
                    gm1[:, 3:4], gn_sb[:, base + 0:base + 3], axis=AX.X),
                    inc=s_ln)
                mxv = s_ln.n
                P.op("vector", lambda e, base=base: e.tensor_tensor(
                    ga[:], gm1[:, 0:1], gn_sb[:, base:base + 1], op=ALU.is_gt),
                    waits=[(s_ln, mxv)])
                P.op("vector", lambda e: e.tensor_tensor(
                    gm1[:, 0:1], gm1[:, 1:2], gm1[:, 2:3], op=ALU.is_gt))
                P.op("vector", lambda e, base=base: e.tensor_tensor(
                    gm1[:, 1:2], gn_sb[:, base + 3:base + 4], gm1[:, 3:4],
                    op=ALU.is_gt), inc=s_ln)
                cv = s_ln.n
                gn_free_v[gtq] = cv
                P.op("vector", lambda e: e.tensor_tensor(
                    gm1[:, 2:3], gm1[:, 0:1], gm1[:, 1:2], op=ALU.add),
                    waits=[(s_ln, cv)], inc=s_ln)
                sumv = s_ln.n
                P.op("vector", lambda e: e.tensor_tensor(
                    ga[:], ga[:], gm1[:, 2:3], op=ALU.add),
                    waits=[(s_ln, sumv)], inc=s_ln)
                av = s_ln.n
                P.op("vector", lambda e: e.tensor_scalar(
                    out=maskn[:], in0=levels[:], scalar1=ga[:, 0:1],
                    scalar2=None, op0=ALU.is_le),
                    waits=[(s_ln, av), (s_te, s_te.n)], inc=s_ln)
                mk = s_ln.n
                P.op("tensor", lambda e: e.transpose(
                    ps_t[0:64, 0:128], in_=maskn[:], identity=identb[:]),
                    waits=[(s_ln, mk), (s_te, s_te.n)], inc=s_tp)
                P.op("scalar", lambda e, tq=tq, k=k: e.copy(
                    out=mT_sb[0:64, (tq * 4 + k) * 128:(tq * 4 + k + 1) * 128],
                    in_=ps_t[0:64, 0:128]),
                    waits=[(s_tp, s_tp.n)], inc=s_te)
            h1_ps_free = [(s_ev, s_ev.n)] if (th > 0 or tq > 0) else []
            if th > 0 and tq == 0:
                h1_ps_free.append((s_bcp, s_bcp.n))
            for c in range(DC):
                w = [(s_te, ln2T_done)] + (h1_ps_free if c == 0 else [])
                P.op("tensor", lambda e, c=c, tq=tq: e.matmul(
                    ps_c1[0:64, :], lhsT=AT_sb[:, c, :],
                    rhs=ln2T[:, c, tq * 512:(tq + 1) * 512],
                    start=(c == 0), stop=(c == DC - 1)),
                    waits=(w if c == 0 else ()),
                    inc=(s_mm if c == DC - 1 else None))
            P.op("scalar", lambda e, tq=tq: e.copy(
                out=h1T[0:64, tq * 512:(tq + 1) * 512], in_=ps_c1[0:64, :]),
                waits=[(s_mm, s_mm.n)], inc=s_ev)
        P.op("vector", lambda e: e.tensor_tensor(
            m_sb[0:64, :], h1T[0:64, :], mT_sb[0:64, :], op=ALU.mult),
            waits=[(s_ev, s_ev.n), (s_te, s_te.n)], inc=s_ln)
        msk_done = s_ln.n

        for ob in range(NOB3):
            gob = th * NOB3 + ob
            wldv = {}
            for li in range(4):
                osb, dhalf = li // 2, li % 2
                gli = gob * 4 + li
                w = []
                if gli >= 4:
                    w.append((s_tp, wrow_half_tp[gli - 4]))
                P.op("sync", lambda e, ob=ob, osb=osb, dhalf=dhalf, li=li:
                     e.dma_start(
                         out=wrow[li][:],
                         in_=Wn[ob * 256 + osb * 128:ob * 256 + (osb + 1) * 128,
                                dhalf * 2048:(dhalf + 1) * 2048]),
                     waits=w, inc16=s_wr)
                wldv[li] = s_wr.n
            wti = gob % 2
            wtbanks = [ps_d, ps_c0, ps_c1]
            for osb in range(2):
                for grp in range(8):
                    wg = wtb_state["g"]
                    wtb_state["g"] += 1
                    bank = wtbanks[wg % 3]
                    li = osb * 2 + grp // 4
                    for k in range(4):
                        c = grp * 4 + k
                        w = [(s_wr, wldv[li])]
                        if k == 0 and wg >= 3:
                            w.append((s_bcp, wtb_state["bcp"][wg - 3]))
                        if k == 0 and (wg % (NOB3 * 16)) < 3:
                            w.append((s_ev, s_ev.n))
                            w.append((s_bcp, s_bcp.n))
                        if k == 0 and grp == 0 and osb == 0 and gob >= 2:
                            w.append((s_mm, ffn_grp_mm[(gob - 2) * 8 + 7]))
                        P.op("tensor", lambda e, c=c, k=k, li=li, bank=bank:
                             e.transpose(
                                 bank[:, k * 128:(k + 1) * 128],
                                 in_=wrow[li][:, (c % 16) * 128:
                                              (c % 16 + 1) * 128],
                                 identity=identf[:]),
                             waits=w, inc=(s_tp if k == 3 else None))
                    P.op("scalar", lambda e, grp=grp, osb=osb, wti=wti, bank=bank:
                         e.copy(
                             out=WT[wti][:, grp * 4:(grp + 1) * 4,
                                         osb * 128:(osb + 1) * 128],
                             in_=bank[:].rearrange("p (c f) -> p c f", c=4)),
                         waits=[(s_tp, s_tp.n)], inc=s_bcp)
                    wtb_state["bcp"][wg] = s_bcp.n
                    if grp % 4 == 3:
                        wrow_half_tp[gob * 4 + osb * 2 + grp // 4] = s_tp.n
            wrow_tp[gob] = s_tp.n
            wt_done = s_bcp.n

            def mk_mm(ob, tbase, wt_done, wti, msk_done):
                def go():
                    global_unused = None
                    for tt in range(8):
                        nonlocal_state["ffn_grp"] += 1
                        fg = nonlocal_state["ffn_grp"] - 1
                        pq = [ps_a, ps_b][fg % 2]
                        w = [(s_bcp, wt_done), (s_ln, msk_done)]
                        if fg >= 2:
                            w.append((s_oe, oe_vals[-2]))
                        for c in range(DC):
                            P.op("tensor", lambda e, c=c, tt=tt, pq=pq, wti=wti:
                                 e.matmul(pq[:, 0:OB3],
                                          lhsT=ln2T[:, c, tt * 128:(tt + 1) * 128],
                                          rhs=WT[wti][:, c, :],
                                          start=(c == 0), stop=False),
                                 waits=(w if c == 0 else ()))
                        P.op("tensor", lambda e, tt=tt, ob=ob, pq=pq: e.matmul(
                            pq[:, 0:OB3],
                            lhsT=m_sb[0:64, tt * 128:(tt + 1) * 128],
                            rhs=BT_sb[0:64, ob * 256:(ob + 1) * 256],
                            start=False, stop=True), inc=s_mm)
                        mmv = s_mm.n
                        ffn_grp_mm.append(mmv)
                        ii = (fg + 1) % 2
                        qb_src = (tbase + tt * 128) // 512
                        w2 = [(s_x2p[0], x2_stores[qb_src][0]),
                              (s_x2p[1], x2_stores[qb_src][1])]
                        if fg >= 2:
                            w2.append((s_oe, oe_vals[-2]))
                        P.op("sync", lambda e, tbase=tbase, tt=tt, ob=ob, ii=ii:
                             e.dma_start(
                                 out=x2sl[ii][:],
                                 in_=x2_s[tbase + tt * 128:tbase + (tt + 1) * 128,
                                          ob * 256:(ob + 1) * 256]),
                             waits=w2, inc16=s_wp[ii])
                        xv2 = s_wp[ii].n
                        P.op("vector", lambda e, pq=pq, ob=ob, ii=ii:
                             e.tensor_tensor(
                                 t1sl3[:], pq[:, 0:OB3],
                                 b_bc[:, ob * 256:(ob + 1) * 256],
                                 op=ALU.add), waits=[(s_mm, mmv), (s_wp[ii], xv2)])
                        oc = nonlocal_state["out_ct"]
                        w3 = [(s_stp[oc % 2], out_store_v[oc - 2])] \
                            if oc >= 2 else []
                        P.op("vector", lambda e, ii=ii: e.tensor_tensor(
                            outst[ii][:], t1sl3[:], x2sl[ii][:], op=ALU.add),
                            waits=w3, inc=s_oe)
                        oe_vals.append(s_oe.n)
                        P.op("sync", lambda e, tbase=tbase, tt=tt, ob=ob, ii=ii:
                             e.dma_start(
                                 out=out_d[tbase + tt * 128:tbase + (tt + 1) * 128,
                                           ob * 256:(ob + 1) * 256],
                                 in_=outst[ii][:]),
                             waits=[(s_oe, s_oe.n)], inc16=s_stp[oc % 2])
                        out_store_v.append(s_stp[oc % 2].n)
                        nonlocal_state["out_ct"] += 1
                return go

            new_mm = mk_mm(ob, tbase, wt_done, wti, msk_done)
            if pend_mm is not None:
                pend_mm()
            pend_mm = new_mm
    if pend_mm is not None:
        pend_mm()
        pend_mm = None

    P.wait("sync", [(s_stp[0], s_stp[0].n), (s_stp[1], s_stp[1].n)])

    _check_deadlock(P)

    with nc.Block() as blk:
        for eng_name in ("sync", "gpsimd", "tensor", "scalar", "vector"):
            items = [o for o in P.ops if o[0] == eng_name]

            def make_body(items):
                def body(eng):
                    for _, waits, fn, incs in items:
                        for (hh, v) in waits:
                            eng.wait_ge(hh, v)
                        if fn is None:
                            continue
                        ins = fn(eng)
                        for (hh, amt) in incs:
                            ins = ins.then_inc(hh, amt)
                return body

            getattr(blk, eng_name)(make_body(items))

    p3.close()
    es.close()
    return nc


_CACHE = {}


def _get_program():
    if "nc" not in _CACHE:
        _CACHE["nc"] = build_program()
    return _CACHE["nc"]


def _prep_inputs(c, inputs):
    bidx, gg = c // 2, c % 2
    f32 = np.float32
    x = np.asarray(inputs["L0_attn_x"], f32)
    qkvw = np.asarray(inputs["L0_attn_attn_qkvw"], f32)
    qkvb = np.asarray(inputs["L0_attn_attn_qkvb"], f32)
    ow = np.asarray(inputs["L0_attn_attn_ow"], f32)
    obv = np.asarray(inputs["L0_attn_attn_ob"], f32)
    W = np.asarray(inputs["L0_ffn_W"], f32)
    bv = np.asarray(inputs["L0_ffn_b"], f32)
    A = np.asarray(inputs["L0_ffn_A"], f32)
    Bm = np.asarray(inputs["L0_ffn_B"], f32)
    gw = np.asarray(inputs["L0_ffn_gate_w"], f32)
    gb = np.asarray(inputs["L0_ffn_gate_b"], f32)

    own = slice(2048 * gg, 2048 * gg + 2048)
    oth = slice(2048 * (1 - gg), 2048 * (1 - gg) + 2048)
    x_loc = np.ascontiguousarray(np.concatenate([x[bidx, own], x[bidx, oth]], 0))

    qcols = qkvw[:, 2048 * gg:2048 * gg + 2048]
    kcols = qkvw[:, 4096 + 2048 * gg:4096 + 2048 * gg + 2048]
    qk = np.concatenate([qcols, kcols], 1)
    wqk_h = np.ascontiguousarray(
        qk.reshape(32, 128, 32, 128).transpose(2, 1, 0, 3).reshape(32, 128, -1))
    vcols = qkvw[:, 8192 + 2048 * gg:8192 + 2048 * gg + 2048]
    wv_h = np.ascontiguousarray(
        vcols.reshape(32, 128, 8, 256).transpose(2, 1, 0, 3).reshape(8, 128, -1))
    qkvb_qk = np.ascontiguousarray(np.concatenate(
        [qkvb[2048 * gg:2048 * gg + 2048],
         qkvb[4096 + 2048 * gg:4096 + 2048 * gg + 2048]]).reshape(32, 128).T)
    vb = np.ascontiguousarray(qkvb[8192 + 2048 * gg:8192 + 2048 * gg + 2048])
    ow_p = np.concatenate([ow[2048 * gg:2048 * gg + 2048],
                           ow[2048 * (1 - gg):2048 * (1 - gg) + 2048]], 0)
    ow_hh = np.ascontiguousarray(
        ow_p.reshape(32, 128, 16, 256).transpose(2, 1, 0, 3).reshape(16, 128, -1))
    B_h = np.ascontiguousarray(
        Bm.reshape(32, 128, 64).transpose(1, 0, 2).reshape(128, -1))

    bf = ml_dtypes.bfloat16
    lev = np.array([0] * 8 + [1] * 8 + [2] * 16 + [3] * 32, f32)
    return {
        "x_l": x_loc, "wqk": wqk_h, "wv": wv_h, "qkvb_qk": qkvb_qk,
        "vb_row": vb, "ow_h": ow_hh, "ob_row": obv, "Wn": W, "b_row": bv,
        "A_in": A, "B_in": B_h, "gw_in": gw, "gb_in": gb,
        "identb_in": np.eye(128, dtype=bf),
        "identf_in": np.eye(128, dtype=f32),
        "onesf_in": np.ones((1, 128), f32),
        "onesb_in": np.ones((128, 1), bf),
        "levels_in": np.tile(lev, (128, 1)),
        "masks_in": np.tile(np.array([0.0, 1.0] if gg == 0 else [1.0, 0.0], f32),
                            (128, 1)),
        "eps_in": np.full((128, 1), 1e-5, f32),
    }


def kernel(**inputs):
    nc = _get_program()
    in_maps = [_prep_inputs(c, inputs) for c in range(8)]
    res = run_bass_kernel_spmd(nc, in_maps, core_ids=list(range(8)))
    out = np.empty((4, 4096, 4096), np.float32)
    for c in range(8):
        bidx, gg = c // 2, c % 2
        out[bidx, 2048 * gg:2048 * gg + 2048] = res.results[c]["out"]
    return out


# version 21
# speedup vs baseline: 1.0932x; 1.0203x over previous
import sys

for _p in ("/opt/trn_rl_repo", "/root/.axon_site/_ro/trn_rl_repo"):
    if _p not in sys.path:
        sys.path.append(_p)

import contextlib

import numpy as np
import ml_dtypes

import concourse.bass as bass
import concourse.mybir as mybir
from concourse.bass_utils import run_bass_kernel_spmd

F32 = mybir.dt.float32
BF16 = mybir.dt.bfloat16
AF = mybir.ActivationFunctionType
ALU = mybir.AluOpType
AX = mybir.AxisListType

H = 4096
DC = 32
NH = 16
SL = 4096
TOWN = 2048
QB = 512
NSL = 8
QORDER = [4, 0, 5, 1, 6, 2, 7, 3]
SCALE = float(1.0 / np.sqrt(128.0))
EPS = 1e-5
GROUPS = [[0, 1], [2, 3], [4, 5], [6, 7]]
OB3 = 256
NOB3 = H // OB3


class Sem:
    def __init__(self, handle):
        self.h = handle
        self.n = 0


class Prog:

    def __init__(self):
        self.ops = []

    def op(self, engine, fn, waits=(), inc=None, inc16=None):
        incs = []
        if inc is not None:
            inc.n += 1
            incs.append((inc.h, 1))
        if inc16 is not None:
            inc16.n += 16
            incs.append((inc16.h, 16))
        self.ops.append((engine, tuple((s.h, v) for (s, v) in waits), fn, incs))

    def wait(self, engine, waits):
        self.ops.append((engine, tuple((s.h, v) for (s, v) in waits), None, []))


def _check_deadlock(prog):
    qs = {}
    for it in prog.ops:
        qs.setdefault(it[0], []).append(it)
    heads = {k: 0 for k in qs}
    counts = {}
    progress = True
    while progress:
        progress = False
        for eng, lst in qs.items():
            while heads[eng] < len(lst):
                _, waits, _, incs = lst[heads[eng]]
                if all(counts.get(h, 0) >= v for (h, v) in waits):
                    for (h, amt) in incs:
                        counts[h] = counts.get(h, 0) + amt
                    heads[eng] += 1
                    progress = True
                else:
                    break
    stuck = {e: (heads[e], len(l)) for e, l in qs.items() if heads[e] < len(l)}
    if stuck:
        lines = []
        for e, (i, n) in stuck.items():
            _, waits, _, _ = qs[e][i]
            lines.append(f"{e}: op {i}/{n} waits " + str(
                [(getattr(h, "name", h), v, counts.get(h, 0)) for h, v in waits]))
        raise RuntimeError("sync deadlock:\n" + "\n".join(lines))


def build_program():
    nc = bass.Bass("TRN2", target_bir_lowering=False)
    es = contextlib.ExitStack()

    def dram(name, shape, dt, kind=None):
        if kind is None:
            return nc.dram_tensor(name, shape, dt).ap()
        return nc.dram_tensor(name, shape, dt, kind=kind).ap()

    x_l = dram("x_l", [SL, H], F32, "ExternalInput")
    wqk = dram("wqk", [32, 128, DC * 128], F32, "ExternalInput")
    wv = dram("wv", [8, 128, DC * 256], F32, "ExternalInput")
    qkvb_qk = dram("qkvb_qk", [128, 32], F32, "ExternalInput")
    vb_row = dram("vb_row", [2048], F32, "ExternalInput")
    ow_h = dram("ow_h", [16, 128, DC * 256], F32, "ExternalInput")
    ob_row = dram("ob_row", [H], F32, "ExternalInput")
    Wn = dram("Wn", [H, H], F32, "ExternalInput")
    b_row = dram("b_row", [H], F32, "ExternalInput")
    A_in = dram("A_in", [64, H], F32, "ExternalInput")
    B_in = dram("B_in", [128, DC * 64], F32, "ExternalInput")
    gw_in = dram("gw_in", [4, H], F32, "ExternalInput")
    gb_in = dram("gb_in", [4], F32, "ExternalInput")
    identb_in = dram("identb_in", [128, 128], BF16, "ExternalInput")
    identf_in = dram("identf_in", [128, 128], F32, "ExternalInput")
    onesf_in = dram("onesf_in", [1, 128], F32, "ExternalInput")
    onesb_in = dram("onesb_in", [128, 1], BF16, "ExternalInput")
    levels_in = dram("levels_in", [128, 64], F32, "ExternalInput")
    masks_in = dram("masks_in", [128, 2], F32, "ExternalInput")
    eps_in = dram("eps_in", [128, 1], F32, "ExternalInput")
    out_d = dram("out", [TOWN, H], F32, "ExternalOutput")

    qT_s = dram("qT_s", [NH, 128, SL], BF16)
    kT_s = dram("kT_s", [NH, 128, SL], BF16)
    v_s = dram("v_s", [NH, 128, DC, 128], BF16)
    cc_in = [dram(f"cc_in{j}", [2 * TOWN, QB], BF16) for j in range(4)]
    cc_out = [dram(f"cc_out{j}", [TOWN, QB], BF16) for j in range(4)]
    ow_bf_s = dram("ow_bf_s", [16, 128, DC * 256], BF16)
    x2_s = dram("x2_s", [TOWN, H], F32)
    ln2_s = dram("ln2_s", [TOWN, H], BF16)

    def sbuf(name, shape, dt, scope=es):
        return scope.enter_context(nc.sbuf_tensor(name, shape, dt))

    def psum(name, shape, dt):
        return es.enter_context(nc.psum_tensor(name, shape, dt))

    identb = sbuf("identb", [128, 128], BF16)
    identf = sbuf("identf", [128, 128], F32)
    onesf = sbuf("onesf", [1, 128], F32)
    onesb = sbuf("onesb", [128, 1], BF16)
    levels = sbuf("levels", [128, 64], F32)
    masks = sbuf("masks", [128, 2], F32)
    qkvb_sb = sbuf("qkvb_sb", [128, 32], F32)
    row_sb = sbuf("row_sb", [1, 1024], F32)
    gb_sb = sbuf("gb_sb", [4, 1], F32)
    eps_sb = sbuf("eps_sb", [128, 1], F32)

    ps_a = psum("ps_a", [128, 1024], F32)
    ps_b = psum("ps_b", [128, 1024], F32)
    ps_c0 = psum("ps_c0", [128, 512], F32)
    ps_c1 = psum("ps_c1", [128, 512], F32)
    ps_d = psum("ps_d", [128, 512], F32)
    ps_t = psum("ps_t", [128, 1024], BF16)

    semnames = ["s_cst", "s_row", "s_ln", "s_sc", "s_tp", "s_te", "s_mm",
                "s_ev", "s_exp", "s_ctx", "s_dt", "s_dmm", "s_rc", "s_bc",
                "s_bcp", "s_ce", "s_ccm", "s_ccd", "s_cc", "s_rx", "s_op",
                "s_oe", "s_l2", "s_bar", "s_xsl", "s_wr",
                "s_xh0", "s_xh1", "s_wp0", "s_wp1", "s_stp0", "s_stp1",
                "s_vsp0", "s_vsp1", "s_kvp0", "s_kvp1", "s_x2p0", "s_x2p1",
                "s_xr2", "s_xr3", "s_l2b", "s_lh2", "s_lh3"]
    S = {n: Sem(es.enter_context(nc.semaphore(n))) for n in semnames}
    s_cst, s_row, s_ln, s_sc, s_tp, s_te = (S[k] for k in semnames[0:6])
    s_mm, s_ev, s_exp, s_ctx, s_dt, s_dmm = (S[k] for k in semnames[6:12])
    s_rc, s_bc, s_bcp, s_ce, s_ccm, s_ccd = (S[k] for k in semnames[12:18])
    s_cc, s_rx, s_op, s_oe, s_l2, s_bar = (S[k] for k in semnames[18:24])
    s_xsl, s_wr = S["s_xsl"], S["s_wr"]
    s_xh = [S["s_xh0"], S["s_xh1"]]
    s_xr = [S["s_xh0"], S["s_xh1"], S["s_xr2"], S["s_xr3"]]
    s_l2p = [S["s_l2"], S["s_l2b"]]
    s_lh = [S["s_l2"], S["s_l2b"], S["s_lh2"], S["s_lh3"]]
    s_wp = [S["s_wp0"], S["s_wp1"]]
    s_stp = [S["s_stp0"], S["s_stp1"]]
    s_vsp = [S["s_vsp0"], S["s_vsp1"]]
    s_kvp = [S["s_kvp0"], S["s_kvp1"]]
    s_x2p = [S["s_x2p0"], S["s_x2p1"]]

    P = Prog()

    for dst, src in [(identb, identb_in), (identf, identf_in), (onesf, onesf_in),
                     (onesb, onesb_in), (levels, levels_in), (masks, masks_in),
                     (qkvb_sb, qkvb_qk), (eps_sb, eps_in)]:
        P.op("sync", lambda e, d=dst, s=src: e.dma_start(out=d[:], in_=s[:]),
             inc16=s_cst)
    P.op("sync", lambda e: e.dma_start(
        out=gb_sb[:, :], in_=gb_in.rearrange("(p a) -> p a", a=1)), inc16=s_cst)
    consts_done = s_cst.n

    rb_state = {"tp": {}, "te": {}, "i": 0}

    def emit_row_bcast(row_src, n, dest_sb):
        row2 = row_src.rearrange("(a f) -> a f", a=1)
        rbanks = [ps_d, ps_c0, ps_c1]
        for i in range((n + 511) // 512):
            w = min(512, n - i * 512)
            gi = rb_state["i"]
            rb_state["i"] += 1
            par = gi % 2
            bank = rbanks[gi % 3]
            ldsem = [s_row, s_cst][par]
            wv = [(s_tp, rb_state["tp"][gi - 2])] if gi >= 2 else []
            P.op("sync", lambda e, i=i, w=w, par=par: e.dma_start(
                out=row_sb[:, par * 512:par * 512 + w],
                in_=row2[0:1, i * 512:i * 512 + w]),
                waits=wv, inc16=ldsem)
            ldv = ldsem.n
            wv2 = [(ldsem, ldv), (s_cst, consts_done)]
            if gi >= 3:
                wv2.append((s_te, rb_state["te"][gi - 3]))
            P.op("tensor", lambda e, w=w, par=par, bank=bank: e.matmul(
                bank[:, 0:w], lhsT=onesf[:, :],
                rhs=row_sb[:, par * 512:par * 512 + w],
                start=True, stop=True),
                waits=wv2, inc=s_tp)
            rb_state["tp"][gi] = s_tp.n
            P.op("scalar", lambda e, i=i, w=w, bank=bank: e.copy(
                out=dest_sb[:, i * 512:i * 512 + w], in_=bank[:, 0:w]),
                waits=[(s_tp, s_tp.n)], inc=s_te)
            rb_state["te"][gi] = s_te.n

    p1 = contextlib.ExitStack()
    vb_bc = sbuf("vb_bc", [128, 2048], BF16, p1)
    emit_row_bcast(vb_row, 2048, vb_bc)
    vbbc_done = s_te.n


    ln1T = sbuf("ln1T", [128, DC, 2048], BF16, p1)
    x_sb = [sbuf(f"x_sb{i}", [128, 2048], F32, p1) for i in range(2)]
    ln1h = sbuf("ln1h", [128, 2048], BF16, p1)
    w_u = [sbuf(f"w_u{i}", [128, 8192], BF16, p1) for i in range(2)]
    qkst = [sbuf(f"qkst{i}", [128, 512], BF16, p1) for i in range(2)]
    vst = [sbuf(f"vst{i}", [128, 256], BF16, p1) for i in range(2)]
    stats1 = sbuf("stats1", [128, 48], F32, p1)
    mv1 = [sbuf(f"mv1_{i}", [128, 2], F32, p1) for i in range(2)]
    rstd1 = [sbuf(f"rstd1_{i}", [128, 1], F32, p1) for i in range(2)]
    lnt1 = sbuf("lnt1", [128, 1], F32, p1)
    nmr1 = [sbuf(f"nmr1_{i}", [128, 1], F32, p1) for i in range(2)]

    x_free = {}
    ln_v = {}
    grp_idx = 0
    grp_ev = []
    qk_store_v = []
    v_store_v = []
    wu_free = {}
    wu_lidx = 0

    for tsb in range(2):
        t0 = tsb * 2048
        for tt in range(16):
            gt = tsb * 16 + tt
            row = t0 + tt * 128
            xv = {}
            for hf in range(2):
                w = [(s_sc, x_free[(gt - 1, hf)])] if gt >= 1 else \
                    [(s_cst, consts_done)]
                P.op("sync", lambda e, row=row, hf=hf: e.dma_start(
                    out=x_sb[hf][:],
                    in_=x_l[row:row + 128, hf * 2048:(hf + 1) * 2048]),
                    waits=w, inc16=s_xh[hf])
                xv[hf] = s_xh[hf].n
                for i in range(4):
                    P.op("vector", lambda e, hf=hf, i=i: e.bn_stats(
                        stats1[:, (hf * 4 + i) * 6:(hf * 4 + i) * 6 + 6],
                        x_sb[hf][:, i * 512:(i + 1) * 512]),
                        waits=[(s_xh[hf], xv[hf])],
                        inc=(s_ln if (hf == 1 and i == 3) else None))
            pp = gt % 2
            w = [(s_ln, s_ln.n)]
            if gt >= 2:
                w.append((s_sc, ln_v[gt - 2]))
            P.op("vector", lambda e, pp=pp: e.bn_aggr(mv1[pp][:], stats1[:]),
                 waits=w, inc=s_ln)
            P.op("scalar", lambda e, pp=pp: e.activation(
                out=lnt1[:], in_=mv1[pp][:, 1:2], func=AF.Ln,
                bias=eps_sb[:, 0:1]),
                waits=[(s_ln, s_ln.n)], inc=s_sc)
            ln_v[gt] = s_sc.n
            w = [(s_sc, s_sc.n)]
            if gt >= 2:
                w.append((s_sc, x_free[(gt - 2, 1)]))
            P.op("scalar", lambda e, pp=pp: e.activation(
                out=rstd1[pp][:], in_=lnt1[:], func=AF.Exp, scale=-0.5),
                waits=w, inc=s_sc)
            P.op("vector", lambda e, pp=pp: e.scalar_tensor_tensor(
                out=nmr1[pp][:], in0=mv1[pp][:, 0:1], scalar=-1.0,
                in1=rstd1[pp][:], op0=ALU.mult, op1=ALU.mult),
                waits=[(s_sc, s_sc.n)], inc=s_ln)
            nmr_v = s_ln.n
            for hf in range(2):
                P.op("scalar", lambda e, hf=hf, pp=pp: e.activation(
                    out=ln1h[:], in_=x_sb[hf][:], func=AF.Identity,
                    bias=nmr1[pp][:, 0:1], scale=rstd1[pp][:, 0:1]),
                    waits=[(s_ln, nmr_v), (s_tp, s_tp.n)], inc=s_sc)
                x_free[(gt, hf)] = s_sc.n
                lh = s_sc.n
                for grp in range(2):
                    for k in range(8):
                        c_loc = grp * 8 + k
                        w = [(s_sc, lh)]
                        if k == 0:
                            w.append((s_te, s_te.n))
                        P.op("tensor", lambda e, c_loc=c_loc, k=k: e.transpose(
                            ps_t[:, k * 128:(k + 1) * 128],
                            in_=ln1h[:, c_loc * 128:(c_loc + 1) * 128],
                            identity=identb[:]),
                            waits=w, inc=(s_tp if k == 7 else None))
                    c0 = hf * 16 + grp * 8
                    P.op("scalar", lambda e, c0=c0, tt=tt: e.copy(
                        out=ln1T[:, c0:c0 + 8, tt * 128:(tt + 1) * 128],
                        in_=ps_t[:].rearrange("p (c f) -> p c f", c=8)),
                        waits=[(s_tp, s_tp.n)], inc=s_te)
        ln1_done = s_te.n

        for ft in range(32):
            w = [(s_te, ln1_done)]
            if wu_lidx >= 2:
                w.append((s_mm, wu_free[wu_lidx - 2]))
            my_wu = wu_lidx % 2
            P.op("gpsimd", lambda e, ft=ft, my_wu=my_wu: e.dma_start(
                out=w_u[my_wu][:, 0:4096], in_=wqk[ft][:, 0:4096]),
                waits=w, inc16=s_wp[my_wu])
            wld = s_wp[my_wu].n
            wu_lidx += 1
            for tb in range(4):
                pq = [ps_a, ps_b][grp_idx % 2]
                w = [(s_wp[my_wu], wld), (s_te, ln1_done)]
                if grp_idx >= 2:
                    w.append((s_ev, grp_ev[grp_idx - 2]))
                for c in range(DC):
                    P.op("tensor", lambda e, c=c, tb=tb, pq=pq, my_wu=my_wu:
                         e.matmul(pq[:, 0:512],
                                  lhsT=w_u[my_wu][:, c * 128:(c + 1) * 128],
                                  rhs=ln1T[:, c, tb * 512:(tb + 1) * 512],
                                  start=(c == 0), stop=(c == DC - 1)),
                         waits=(w if c == 0 else ()),
                         inc=(s_mm if c == DC - 1 else None))
                wu_free[wu_lidx - 1] = s_mm.n
                qe = len(qk_store_v)
                w = [(s_mm, s_mm.n)]
                if qe >= 2:
                    w.append((s_stp[qe % 2], qk_store_v[qe - 2]))
                P.op("scalar", lambda e, ft=ft, pq=pq, i=qe % 2: e.activation(
                    out=qkst[i][:], in_=pq[:, 0:512], func=AF.Identity,
                    bias=qkvb_sb[:, ft:ft + 1]),
                    waits=w, inc=s_ev)
                grp_ev.append(s_ev.n)
                grp_idx += 1
                dst = qT_s if ft < 16 else kT_s
                hh = ft % 16
                P.op("sync", lambda e, dst=dst, hh=hh, t0=t0, tb=tb, i=qe % 2:
                     e.dma_start(
                         out=dst[hh, :, t0 + tb * 512:t0 + (tb + 1) * 512],
                         in_=qkst[i][:]),
                     waits=[(s_ev, s_ev.n)], inc16=s_stp[qe % 2])
                qk_store_v.append(s_stp[qe % 2].n)

        for fb in range(8):
            w = [(s_te, ln1_done)]
            if wu_lidx >= 2:
                w.append((s_mm, wu_free[wu_lidx - 2]))
            my_wu = wu_lidx % 2
            P.op("gpsimd", lambda e, fb=fb, my_wu=my_wu: e.dma_start(
                out=w_u[my_wu][:], in_=wv[fb][:]),
                waits=w, inc16=s_wp[my_wu])
            wld = s_wp[my_wu].n
            wu_lidx += 1
            for tt in range(16):
                pq = [ps_a, ps_b][grp_idx % 2]
                w = [(s_wp[my_wu], wld)]
                if grp_idx >= 2:
                    w.append((s_ev, grp_ev[grp_idx - 2]))
                for c in range(DC):
                    P.op("tensor", lambda e, c=c, tt=tt, pq=pq, my_wu=my_wu:
                         e.matmul(pq[:, 0:256],
                                  lhsT=ln1T[:, c, tt * 128:(tt + 1) * 128],
                                  rhs=w_u[my_wu][:, c * 256:(c + 1) * 256],
                                  start=(c == 0), stop=(c == DC - 1)),
                         waits=(w if c == 0 else ()),
                         inc=(s_mm if c == DC - 1 else None))
                wu_free[wu_lidx - 1] = s_mm.n
                ve = len(v_store_v)
                w = [(s_mm, s_mm.n), (s_te, vbbc_done)]
                if ve >= 2:
                    w.append((s_vsp[ve % 2], v_store_v[ve - 2]))
                P.op("vector", lambda e, fb=fb, pq=pq, i=ve % 2: e.tensor_tensor(
                    out=vst[i][:], in0=pq[:, 0:256],
                    in1=vb_bc[:, fb * 256:(fb + 1) * 256], op=ALU.add),
                    waits=w, inc=s_ev)
                grp_ev.append(s_ev.n)
                grp_idx += 1
                ktile = tsb * 16 + tt
                for hh in range(2):
                    head = fb * 2 + hh
                    P.op("sync", lambda e, head=head, ktile=ktile, hh=hh, i=ve % 2:
                         e.dma_start(out=v_s[head, :, ktile, :],
                                     in_=vst[i][:, hh * 128:(hh + 1) * 128]),
                         waits=[(s_ev, s_ev.n)], inc16=s_vsp[ve % 2])
                v_store_v.append(s_vsp[ve % 2].n)

    P.op("sync", lambda e: e.nop(),
         waits=[(s_stp[0], s_stp[0].n), (s_stp[1], s_stp[1].n),
                (s_vsp[0], s_vsp[0].n), (s_vsp[1], s_vsp[1].n)], inc=s_bar)
    P.op("tensor", lambda e: e.nop(), waits=[(s_mm, s_mm.n)], inc=s_bar)
    P.op("scalar", lambda e: e.nop(), waits=[(s_ev, s_ev.n)], inc=s_bar)
    P.op("vector", lambda e: e.nop(), waits=[(s_ev, s_ev.n)], inc=s_bar)
    P.op("gpsimd", lambda e: e.nop(),
         waits=[(s_wp[0], s_wp[0].n), (s_wp[1], s_wp[1].n)], inc=s_bar)
    bar1 = s_bar.n
    p1.close()

    p2 = contextlib.ExitStack()
    probsT2 = [sbuf(f"probsT{i}", [128, 16, 1024], BF16, p2) for i in range(2)]
    kT_sb = [sbuf(f"kT_sb{i}", [128, SL], BF16, p2) for i in range(2)]
    v_sb = [sbuf(f"v_sb{i}", [128, DC, 128], BF16, p2) for i in range(2)]
    qT_sb = [sbuf(f"qT_sb{i}", [128, QB], BF16, p2) for i in range(2)]
    ctxT = sbuf("ctxT", [128, NH, QB], BF16, p2)
    rx = sbuf("rx", [128, NH, QB], BF16, p2)
    ccm = [sbuf(f"ccm{i}", [128, 1024], BF16, p2) for i in range(2)]
    acc4 = sbuf("acc4", [128, 4, 512], BF16, p2)
    dtmp = sbuf("dtmp", [128, 512], F32, p2)
    dfin = sbuf("dfin", [128, 512], F32, p2)
    den_bf = sbuf("den_bf", [128, 512], BF16, p2)
    recip = [sbuf(f"recip{i}", [1, 512], F32, p2) for i in range(2)]
    bcast_sb = [sbuf(f"bcast{i}", [128, 512], F32, p2) for i in range(2)]
    ow_cb = [sbuf(f"ow_cb{i}", [128, 4096], BF16, p2) for i in range(3)]
    xsl = [sbuf(f"xsl{i}", [128, 256], F32, p2) for i in range(4)]
    t1sl = sbuf("t1sl", [128, 256], F32, p2)
    x2st = [sbuf(f"x2st{i}", [128, 256], F32, p2) for i in range(2)]
    stats2 = sbuf("stats2", [128, 4, 96], F32, p2)
    mv2 = [sbuf(f"mv2_{i}", [128, 2], F32, p2) for i in range(2)]
    lnt2 = sbuf("lnt2", [128, 1], F32, p2)
    rstd2 = [sbuf(f"rstd2_{i}", [128, 1], F32, p2) for i in range(2)]
    nmr2 = [sbuf(f"nmr2_{i}", [128, 1], F32, p2) for i in range(2)]
    x2rd = [sbuf(f"x2rd{i}", [128, 512], F32, p2) for i in range(3)]
    ln2h = [sbuf(f"ln2h{i}", [128, 512], BF16, p2) for i in range(4)]
    ob_bc = sbuf("ob_bc", [128, H], BF16, p2)

    for e_ in ("sync", "tensor", "scalar", "vector", "gpsimd"):
        P.wait(e_, [(s_bar, bar1)])
    emit_row_bcast(ob_row, H, ob_bc)
    obbc_done = s_te.n

    ow_ld = 0
    op_group_v = []
    oe_vals = []
    x2_stores = {}
    l2_stores = {}
    x2_idx = 0
    x2_store_v = []
    ln2_apply_v = []
    ln2_st_v = []
    ln2_ln_v = {}
    ln2_gt = 0

    spill_v = {}
    ow_state = {"i": 0, "free": {}}
    oe_slot_end = {}
    ccd_hist = []
    n_cc_slots = [0]
    prev_op_end = None
    kv_pref = {}

    def emit_kv_load(h, hg, par, qb_ld):
        w = [(s_ctx, (hg - 1) * 16)] if hg >= 2 else []
        P.op("sync", lambda e, h=h, par=par: e.dma_start(
            out=kT_sb[par][:], in_=kT_s[h]), waits=w, inc16=s_kvp[par])
        P.op("sync", lambda e, h=h, par=par: e.dma_start(
            out=v_sb[par][:].rearrange("p c f -> p (c f)"),
            in_=v_s[h].rearrange("p c f -> p (c f)")), inc16=s_kvp[par])
        P.op("sync", lambda e, h=h, par=par, qb_ld=qb_ld: e.dma_start(
            out=qT_sb[par][:], in_=qT_s[h, :, qb_ld * QB:(qb_ld + 1) * QB]),
            inc16=s_kvp[par])
        return s_kvp[par].n

    for s in range(NSL):
        qb = QORDER[s]
        pend_spills = list(range((s // 2) * 8, (s // 2) * 8 + 8)) \
            if s in (0, 2) else []

        def emit_spill():
            if not pend_spills:
                return
            ob = pend_spills.pop(0)
            i = ob % 2
            P.op("gpsimd", lambda e, ob=ob: e.dma_start(
                out=ow_bf_s[ob][:], in_=ow_h[ob][:]),
                waits=[(s_bar, bar1)], inc16=s_vsp[i])
            spill_v[ob] = s_vsp[i].n
        if qb < 4:
            j = qb
            w = [(s_cc, j + 1)]
            if op_group_v:
                w.append((s_op, op_group_v[-1]))
            P.op("gpsimd", lambda e, j=j: e.dma_start(
                out=rx[:], in_=cc_out[j].rearrange("(h p) f -> p h f", p=128)),
                waits=w, inc16=s_rx)
        rx_v = s_rx.n
        has_prev_op = op_group_v and prev_op_end is not None

        def emit_den(hh):
            w = [(s_dt, hh + 1)]
            if hh >= 1:
                w.append((s_bcp, hh))
            else:
                w.append((s_te, obbc_done))
            P.op("tensor", lambda e, hh=hh: e.matmul(
                ps_d[0:1, :], lhsT=onesb[:], rhs=den_bf[:], start=True,
                stop=True), waits=w, inc=s_dmm)

        def emit_recip(hh):
            w = [(s_dmm, hh + 1)]
            if hh >= 2:
                w.append((s_bc, hh - 1))
            P.op("vector", lambda e, i=hh % 2: e.reciprocal(
                recip[i][:], ps_d[0:1, :]), waits=w, inc=s_rc)

        def emit_bcast(hh):
            P.op("tensor", lambda e, i=hh % 2: e.matmul(
                ps_d[:], lhsT=onesf[:], rhs=recip[i][:], start=True, stop=True),
                waits=[(s_rc, hh + 1)], inc=s_bc)

        def emit_bcopy(hh):
            w = [(s_bc, hh + 1)]
            if hh >= 2:
                w.append((s_ce, hh - 1))
            P.op("scalar", lambda e, i=hh % 2: e.copy(
                out=bcast_sb[i][:], in_=ps_d[:]), waits=w, inc=s_bcp)

        def emit_evac(hh):
            hloc = hh % NH
            pc2 = [ps_c0, ps_c1][hh % 2]
            P.op("vector", lambda e, hloc=hloc, pc2=pc2, i=hh % 2: e.tensor_tensor(
                ctxT[:, hloc, :], pc2[:], bcast_sb[i][:], op=ALU.mult),
                waits=[(s_ctx, (hh + 1) * 16), (s_bcp, hh + 1)], inc=s_ce)

        def emit_ccm(hh):
            hloc = hh % NH
            ss = QORDER[hh // NH] - 4
            w = [(s_ccd, ccd_hist[-2])] if len(ccd_hist) >= 2 else []
            P.op("vector", lambda e, hloc=hloc, i=hh % 2: e.tensor_scalar(
                out=ccm[i][:, 0:512], in0=ctxT[:, hloc, :],
                scalar1=masks[:, 0:1], scalar2=None, op0=ALU.mult),
                waits=w)
            P.op("vector", lambda e, hloc=hloc, i=hh % 2: e.tensor_scalar(
                out=ccm[i][:, 512:1024], in0=ctxT[:, hloc, :],
                scalar1=masks[:, 1:2], scalar2=None, op0=ALU.mult),
                inc=s_ccm)
            P.op("gpsimd", lambda e, ss=ss, hloc=hloc, i=hh % 2: e.dma_start(
                out=cc_in[ss][hloc * 128:(hloc + 1) * 128, :],
                in_=ccm[i][:, 0:512]),
                waits=[(s_ccm, s_ccm.n)], inc16=s_ccd)
            P.op("gpsimd", lambda e, ss=ss, hloc=hloc, i=hh % 2: e.dma_start(
                out=cc_in[ss][TOWN + hloc * 128:TOWN + (hloc + 1) * 128, :],
                in_=ccm[i][:, 512:1024]), inc16=s_ccd)
            ccd_hist.append(s_ccd.n)

        def emit_back(hh):
            emit_evac(hh)
            if QORDER[hh // NH] >= 4:
                emit_ccm(hh)

        for h in range(NH):
            hg = s * NH + h
            par = hg % 2
            if hg in kv_pref:
                kvv = kv_pref.pop(hg)
            else:
                kvv = emit_kv_load(h, hg, par, qb)

            sc_base = s_mm.n
            for gk in range(16):
                G = hg * 16 + gk
                pq = [ps_a, ps_b][G % 2]
                w = [(s_kvp[par], kvv)]
                if G >= 2:
                    w.append((s_exp, G - 1))
                if gk == 0 and h == 0 and prev_op_end is not None:
                    w.append((s_oe, prev_op_end))
                for half in range(2):
                    kt = gk * 2 + half
                    P.op("tensor", lambda e, kt=kt, half=half, par=par, pq=pq:
                         e.matmul(pq[:, half * 512:(half + 1) * 512],
                                  lhsT=kT_sb[par][:, kt * 128:(kt + 1) * 128],
                                  rhs=qT_sb[par][:], start=True, stop=True),
                         waits=(w if half == 0 else ()),
                         inc=(s_mm if half == 1 else None))
            if hg >= 1 and (hg - 1) % NH != NH - 1:
                emit_den(hg - 1)

            for gk in range(16):
                G = hg * 16 + gk
                pq = [ps_a, ps_b][G % 2]
                w = [(s_mm, sc_base + gk + 1)]
                if hg >= 2:
                    w.append((s_ctx, (hg - 2) * 16 + gk + 1))
                    w.append((s_dt, hg - 1))
                P.op("scalar", lambda e, gk=gk, pq=pq, pb=hg % 2: e.activation(
                    out=probsT2[pb][:, gk, :], in_=pq[:], func=AF.Exp, scale=SCALE),
                    waits=w, inc=s_exp)
            if hg >= 1 and (hg - 1) % NH != NH - 1:
                emit_bcopy(hg - 1)
            if h >= 2 and h % 2 == 0:
                emit_spill()

            if hg >= 2 and (hg - 2) % NH < NH - 2:
                emit_back(hg - 2)
            if hg >= 1 and (hg - 1) % NH != NH - 1:
                emit_recip(hg - 1)
            for gk in range(16):
                G = hg * 16 + gk
                q = gk // 4
                w = [(s_exp, G + 1)]
                if gk % 4 == 0:
                    P.op("vector", lambda e, gk=gk, q=q, pb=hg % 2: e.tensor_tensor(
                        acc4[:, q, :], probsT2[pb][:, gk, 0:512],
                        probsT2[pb][:, gk, 512:1024], op=ALU.add), waits=w)
                else:
                    P.op("vector", lambda e, gk=gk, q=q, pb=hg % 2: e.tensor_tensor(
                        acc4[:, q, :], acc4[:, q, :], probsT2[pb][:, gk, 0:512],
                        op=ALU.add), waits=w)
                    P.op("vector", lambda e, gk=gk, q=q, pb=hg % 2: e.tensor_tensor(
                        acc4[:, q, :], acc4[:, q, :], probsT2[pb][:, gk, 512:1024],
                        op=ALU.add))
            P.op("vector", lambda e: e.tensor_tensor(
                dfin[:], acc4[:, 0, :], acc4[:, 1, :], op=ALU.add))
            P.op("vector", lambda e: e.tensor_tensor(
                dtmp[:], acc4[:, 2, :], acc4[:, 3, :], op=ALU.add))
            P.op("vector", lambda e: e.tensor_tensor(
                dfin[:], dfin[:], dtmp[:], op=ALU.add))
            w = [(s_dmm, hg)] if hg >= 1 else []
            P.op("vector", lambda e: e.tensor_copy(den_bf[:], dfin[:]),
                 waits=w, inc=s_dt)

            pc = [ps_c0, ps_c1][hg % 2]
            for p in range(16):
                w = [(s_exp, hg * 16 + p + 1)]
                if p == 0 and hg >= 2:
                    w.append((s_ce, hg - 1))
                if p == 0 and h < 2 and prev_op_end is not None:
                    w.append((s_oe, prev_op_end))
                for half in range(2):
                    kt = p * 2 + half
                    P.op("tensor", lambda e, kt=kt, p=p, half=half, par=par,
                         pc=pc, pb=hg % 2: e.matmul(
                             pc[:], lhsT=v_sb[par][:, kt, :],
                             rhs=probsT2[pb][:, p, half * 512:(half + 1) * 512],
                             start=(kt == 0), stop=(kt == 31)),
                         waits=(w if half == 0 else ()),
                         inc=(s_ctx if half == 1 else None))
            if hg >= 1 and (hg - 1) % NH != NH - 1:
                emit_bcast(hg - 1)

        hg_last = s * NH + NH - 1
        emit_den(hg_last)
        emit_recip(hg_last)
        emit_bcast(hg_last)
        emit_bcopy(hg_last)
        emit_back(hg_last - 1)
        emit_back(hg_last)

        while pend_spills:
            emit_spill()
        if qb >= 4:
            n_cc_slots[0] += 1
            cj = qb - 4
            P.op("gpsimd", lambda e, cj=cj: e.collective_compute(
                "ReduceScatter", ALU.add, ins=[cc_in[cj][:]],
                outs=[cc_out[cj][:]], replica_groups=GROUPS),
                waits=[(s_ccd, n_cc_slots[0] * NH * 32)], inc=s_cc)
        else:
            j = qb
            equads = [(ps_a, 0, 256), (ps_a, 512, 256), (ps_b, 0, 256),
                      (ps_b, 512, 256)]
            oquads = [(ps_c0, 0, 256), (ps_c0, 256, 256), (ps_c1, 0, 256),
                      (ps_c1, 256, 256)]
            slot_exp = (s + 1) * NH * 16
            slot_ce = (s + 1) * NH
            oe_slot0 = s_oe.n
            use_spill = s >= 4
            ow_sems = [s_wp[0], s_wp[1], s_row]
            for ob in range(16):
                quads = equads if ob % 2 == 0 else oquads
                hb = {}
                hv = {}
                for half in range(2):
                    oi = ow_state["i"]
                    ow_state["i"] += 1
                    bi = oi % 3
                    hb[half] = bi
                    w = []
                    if bi in ow_state["free"]:
                        w.append(ow_state["free"][bi])
                    if use_spill and ob < 2 and half == 0:
                        w.append((s_vsp[0], s_vsp[0].n))
                        w.append((s_vsp[1], s_vsp[1].n))
                    osrc = ow_bf_s[ob] if use_spill else ow_h[ob]
                    P.op("gpsimd", lambda e, osrc=osrc, bi=bi, half=half:
                         e.dma_start(out=ow_cb[bi][:],
                                     in_=osrc[:, half * 4096:(half + 1) * 4096]),
                         waits=w, inc16=ow_sems[bi])
                    hv[half] = ow_sems[bi].n
                ow_ld += 1
                for tt in range(4):
                    w = []
                    if len(oe_vals) >= 4:
                        w.append((s_oe, oe_vals[len(oe_vals) - 4 + tt]))
                    P.op("sync", lambda e, qb=qb, tt=tt, ob=ob:
                         e.dma_start(
                             out=xsl[tt][:],
                             in_=x_l[qb * 512 + tt * 128:qb * 512 + (tt + 1) * 128,
                                     ob * 256:(ob + 1) * 256]),
                         waits=w, inc16=s_xsl)
                xslv = s_xsl.n
                w = [(ow_sems[hb[0]], hv[0]), (ow_sems[hb[1]], hv[1]),
                     (s_exp, slot_exp), (s_ce, slot_ce), (s_rx, rx_v)]
                if ob >= 2:
                    w.append((s_oe, oe_vals[-5]))
                for cf in range(DC):
                    src_ = ctxT if cf < 16 else rx
                    hh = cf % 16
                    bi = hb[cf // 16]
                    for tt in range(4):
                        pq, off, _qw = quads[tt]
                        P.op("tensor", lambda e, src_=src_, hh=hh, tt=tt, cf=cf,
                             pq=pq, off=off, bi=bi: e.matmul(
                                 pq[:, off:off + 256],
                                 lhsT=src_[:, hh, tt * 128:(tt + 1) * 128],
                                 rhs=ow_cb[bi][:, (cf % 16) * 256:
                                               (cf % 16 + 1) * 256],
                                 start=(cf == 0), stop=(cf == DC - 1)),
                             waits=(w if (cf == 0 and tt == 0) else ()),
                             inc=(s_op if (tt == 3 and cf in (15, DC - 1))
                                  else None))
                    if cf == 15:
                        ow_state["free"][hb[0]] = (s_op, s_op.n)
                op_group_v.append(s_op.n)
                ow_state["free"][hb[1]] = (s_op, s_op.n)
                for tt in range(4):
                    gi = ob * 4 + tt
                    pq, off, _qw = quads[tt]
                    P.op("vector", lambda e, pq=pq, off=off, ob=ob: e.tensor_tensor(
                        t1sl[:], pq[:, off:off + 256],
                        ob_bc[:, ob * 256:(ob + 1) * 256], op=ALU.add),
                        waits=[(s_op, s_op.n), (s_xsl, xslv)])
                    ii = gi % 2
                    w2 = [(s_x2p[x2_idx % 2], x2_store_v[x2_idx - 2])] \
                        if x2_idx >= 2 else []
                    P.op("vector", lambda e, ii=ii, tt=tt: e.tensor_tensor(
                        x2st[ii][:], t1sl[:], xsl[tt][:], op=ALU.add), waits=w2)
                    P.op("vector", lambda e, ii=ii, tt=tt, ob=ob: e.bn_stats(
                        stats2[:, tt, ob * 6:(ob + 1) * 6], x2st[ii][:]),
                        inc=s_oe)
                    oe_vals.append(s_oe.n)
                    P.op("sync", lambda e, ii=ii, qb=qb, tt=tt, ob=ob: e.dma_start(
                        out=x2_s[qb * 512 + tt * 128:qb * 512 + (tt + 1) * 128,
                                 ob * 256:(ob + 1) * 256],
                        in_=x2st[ii][:]),
                        waits=[(s_oe, s_oe.n)], inc16=s_x2p[x2_idx % 2])
                    x2_store_v.append(s_x2p[x2_idx % 2].n)
                    x2_idx += 1
            x2_stores[qb] = (s_x2p[0].n, s_x2p[1].n)
            oe_slot_end[s] = s_oe.n
            prev_op_end = s_oe.n
            if s + 1 < NSL:
                hgn = (s + 1) * NH
                kv_pref[hgn] = emit_kv_load(0, hgn, hgn % 2, QORDER[s + 1])
            for tt in range(4):
                pp = ln2_gt % 2
                w = [(s_oe, s_oe.n)]
                if ln2_gt >= 2:
                    w.append((s_sc, ln2_ln_v[ln2_gt - 2]))
                P.op("vector", lambda e, tt=tt, pp=pp: e.bn_aggr(
                    mv2[pp][:], stats2[:, tt, :]), waits=w, inc=s_ln)
                P.op("scalar", lambda e, pp=pp: e.activation(
                    out=lnt2[:], in_=mv2[pp][:, 1:2], func=AF.Ln,
                    bias=eps_sb[:, 0:1]),
                    waits=[(s_ln, s_ln.n)], inc=s_sc)
                ln2_ln_v[ln2_gt] = s_sc.n
                w = [(s_sc, s_sc.n)]
                if ln2_gt >= 2:
                    w.append((s_sc, ln2_apply_v[8 * (ln2_gt - 2) + 7]))
                P.op("scalar", lambda e, pp=pp: e.activation(
                    out=rstd2[pp][:], in_=lnt2[:], func=AF.Exp, scale=-0.5),
                    waits=w, inc=s_sc)
                P.op("vector", lambda e, pp=pp: e.scalar_tensor_tensor(
                    out=nmr2[pp][:], in0=mv2[pp][:, 0:1], scalar=-1.0,
                    in1=rstd2[pp][:], op0=ALU.mult, op1=ALU.mult),
                    waits=[(s_sc, s_sc.n)], inc=s_ln)
                nv = s_ln.n
                ln2_gt += 1
                for hf in range(8):
                    k = len(ln2_apply_v)
                    ii = k % 3
                    lb = k % 4
                    w = [(s_x2p[0], x2_stores[qb][0]), (s_x2p[1], x2_stores[qb][1])]
                    if k >= 3:
                        w.append((s_sc, ln2_apply_v[k - 3]))
                    P.op(["sync", "gpsimd"][k % 2],
                         lambda e, qb=qb, tt=tt, hf=hf, ii=ii: e.dma_start(
                             out=x2rd[ii][:],
                             in_=x2_s[qb * 512 + tt * 128:qb * 512 + (tt + 1) * 128,
                                      hf * 512:(hf + 1) * 512]),
                         waits=w, inc16=s_xr[ii])
                    xv = s_xr[ii].n
                    lw = [(s_ln, nv), (s_xr[ii], xv)]
                    if k >= 4:
                        lw.append((s_lh[lb], ln2_st_v[k - 4]))
                    P.op("scalar", lambda e, ii=ii, pp=pp, lb=lb: e.activation(
                        out=ln2h[lb][:], in_=x2rd[ii][:], func=AF.Identity,
                        bias=nmr2[pp][:, 0:1], scale=rstd2[pp][:, 0:1]),
                        waits=lw, inc=s_sc)
                    ln2_apply_v.append(s_sc.n)
                    P.op(["gpsimd", "sync"][k % 2],
                         lambda e, qb=qb, tt=tt, hf=hf, lb=lb: e.dma_start(
                             out=ln2_s[qb * 512 + tt * 128:qb * 512 + (tt + 1) * 128,
                                       hf * 512:(hf + 1) * 512],
                             in_=ln2h[lb][:]),
                         waits=[(s_sc, s_sc.n)], inc16=s_lh[lb])
                    ln2_st_v.append(s_lh[lb].n)
            l2_stores[qb] = tuple(sem.n for sem in s_lh)

    P.op("sync", lambda e: e.nop(),
         waits=[(sem, sem.n) for sem in s_lh]
               + [(s_x2p[0], s_x2p[0].n), (s_x2p[1], s_x2p[1].n)], inc=s_bar)
    P.op("tensor", lambda e: e.nop(), waits=[(s_op, s_op.n)], inc=s_bar)
    P.op("scalar", lambda e: e.nop(), waits=[(s_bcp, s_bcp.n)], inc=s_bar)
    P.op("vector", lambda e: e.nop(), waits=[(s_ln, s_ln.n)], inc=s_bar)
    P.op("gpsimd", lambda e: e.nop(), waits=[(s_cc, 4)], inc=s_bar)
    bar2 = s_bar.n
    p2.close()

    p3 = contextlib.ExitStack()
    ln2T = sbuf("ln2T", [128, DC, 1024], BF16, p3)
    ln2rd = [sbuf(f"ln2rd{i}", [128, H], BF16, p3) for i in range(2)]
    WT = [sbuf(f"WT{i}", [128, DC, OB3], BF16, p3) for i in range(2)]
    wrow = [sbuf(f"wrow{i}", [128, 2048], F32, p3) for i in range(4)]
    x2sl = [sbuf(f"x2sl{i}", [128, OB3], F32, p3) for i in range(2)]
    outst = [sbuf(f"outst{i}", [128, OB3], F32, p3) for i in range(2)]
    t1sl3 = sbuf("t1sl3", [128, OB3], F32, p3)
    b_bc = sbuf("b_bc", [128, H], BF16, p3)
    a_bf = sbuf("a_bf", [64, H], BF16, p3)
    AT_sb = sbuf("AT_sb", [128, DC, 64], BF16, p3)
    b2_bf = sbuf("b2_bf", [128, DC * 64], BF16, p3)
    BT_sb = sbuf("BT_sb", [64, H], BF16, p3)
    gw_bf = sbuf("gw_bf", [4, H], BF16, p3)
    gwT_sb = sbuf("gwT_sb", [128, DC, 4], BF16, p3)
    h1T = sbuf("h1T", [64, 1024], BF16, p3)
    mT_sb = sbuf("mT_sb", [64, 1024], BF16, p3)
    m_sb = sbuf("m_sb", [64, 1024], BF16, p3)
    g_f32 = sbuf("g_f32", [4, 512], F32, p3)
    gn_sb = sbuf("gn_sb", [128, 16], F32, p3)
    gm1 = sbuf("gm1", [128, 4], F32, p3)
    ga = sbuf("ga", [128, 1], F32, p3)
    maskn = sbuf("maskn", [128, 64], BF16, p3)

    for e_ in ("sync", "tensor", "scalar", "vector", "gpsimd"):
        P.wait(e_, [(s_bar, bar2)])

    emit_row_bcast(b_row, H, b_bc)

    P.op("gpsimd", lambda e: e.dma_start(out=a_bf[:], in_=A_in[:]), inc16=s_wr)
    P.op("gpsimd", lambda e: e.dma_start(out=b2_bf[:], in_=B_in[:]), inc16=s_wr)
    P.op("gpsimd", lambda e: e.dma_start(out=gw_bf[:], in_=gw_in[:]), inc16=s_wr)
    pre_w = s_wr.n

    for grp in range(2):
        for k in range(16):
            c = grp * 16 + k
            w = [(s_wr, pre_w)]
            if k == 0:
                w.append((s_te, s_te.n))
            P.op("tensor", lambda e, c=c, k=k: e.transpose(
                ps_t[:, k * 64:(k + 1) * 64],
                in_=a_bf[0:64, c * 128:(c + 1) * 128],
                identity=identb[0:64, 0:64]),
                waits=w, inc=(s_tp if k == 15 else None))
        P.op("scalar", lambda e, grp=grp: e.copy(
            out=AT_sb[:, grp * 16:(grp + 1) * 16, :],
            in_=ps_t[:].rearrange("p (c f) -> p c f", c=16)),
            waits=[(s_tp, s_tp.n)], inc=s_te)
    for grp in range(4):
        for k in range(8):
            c = grp * 8 + k
            w = [(s_wr, pre_w)]
            if k == 0:
                w.append((s_te, s_te.n))
            P.op("tensor", lambda e, c=c, k=k: e.transpose(
                ps_t[0:64, k * 128:(k + 1) * 128],
                in_=b2_bf[:, c * 64:(c + 1) * 64], identity=identb[:]),
                waits=w, inc=(s_tp if k == 7 else None))
        P.op("scalar", lambda e, grp=grp: e.copy(
            out=BT_sb[0:64, grp * 1024:(grp + 1) * 1024], in_=ps_t[0:64, :]),
            waits=[(s_tp, s_tp.n)], inc=s_te)
    for k in range(DC):
        w = [(s_wr, pre_w)]
        if k == 0:
            w.append((s_te, s_te.n))
        P.op("tensor", lambda e, k=k: e.transpose(
            ps_t[:, k * 4:(k + 1) * 4], in_=gw_bf[0:4, k * 128:(k + 1) * 128],
            identity=identb[0:4, 0:4]),
            waits=w, inc=(s_tp if k == DC - 1 else None))
    P.op("scalar", lambda e: e.copy(
        out=gwT_sb[:], in_=ps_t[:, 0:128].rearrange("p (c f) -> p c f", c=32)),
        waits=[(s_tp, s_tp.n)], inc=s_te)

    ffn_grp = 0
    wtb_state = {"g": 0, "bcp": {}}
    wrow_half_tp = {}
    nonlocal_state = {"ffn_grp": 0, "out_ct": 0}
    pend_mm = None
    out_store_v = []
    g_tp_v = {}
    gn_free_v = {}
    ffn_grp_mm = []
    wrow_tp = {}
    out_ct = 0
    ln2rd_tp = {}

    for th in range(2):
        tbase = th * 1024
        if pend_mm is not None:
            pend_mm()
            pend_mm = None
        for tt in range(8):
            gti = th * 8 + tt
            row = tbase + tt * 128
            qb_src = row // 512
            w = [(s_lh[i_], l2_stores[qb_src][i_]) for i_ in range(4)]
            if gti >= 2:
                w.append((s_tp, ln2rd_tp[gti - 2]))
            P.op("sync", lambda e, row=row, i=tt % 2: e.dma_start(
                out=ln2rd[i][:], in_=ln2_s[row:row + 128, :]),
                waits=w, inc16=s_xh[tt % 2])
            xv = s_xh[tt % 2].n
            for grp in range(4):
                for k in range(8):
                    c = grp * 8 + k
                    w2 = [(s_xh[tt % 2], xv)]
                    if k == 0:
                        w2.append((s_te, s_te.n))
                    P.op("tensor", lambda e, c=c, k=k, i=tt % 2: e.transpose(
                        ps_t[:, k * 128:(k + 1) * 128],
                        in_=ln2rd[i][:, c * 128:(c + 1) * 128],
                        identity=identb[:]),
                        waits=w2, inc=(s_tp if k == 7 else None))
                P.op("scalar", lambda e, grp=grp, tt=tt: e.copy(
                    out=ln2T[:, grp * 8:(grp + 1) * 8, tt * 128:(tt + 1) * 128],
                    in_=ps_t[:].rearrange("p (c f) -> p c f", c=8)),
                    waits=[(s_tp, s_tp.n)], inc=s_te)
            ln2rd_tp[gti] = s_tp.n
        ln2T_done = s_te.n

        for tq in range(2):
            gtq = th * 2 + tq
            gate_ps_free = [(s_ev, s_ev.n)] if (th > 0 or tq > 0) else []
            if th > 0 and tq == 0:
                gate_ps_free.append((s_bcp, s_bcp.n))
            for c in range(DC):
                w = [(s_te, ln2T_done)] + (gate_ps_free if c == 0 else [])
                P.op("tensor", lambda e, c=c, tq=tq: e.matmul(
                    ps_c0[0:4, :], lhsT=gwT_sb[:, c, :],
                    rhs=ln2T[:, c, tq * 512:(tq + 1) * 512],
                    start=(c == 0), stop=(c == DC - 1)),
                    waits=(w if c == 0 else ()),
                    inc=(s_mm if c == DC - 1 else None))
            w = [(s_mm, s_mm.n)]
            if gtq >= 1:
                w.append((s_tp, g_tp_v[gtq - 1]))
            P.op("vector", lambda e: e.tensor_scalar(
                out=g_f32[:], in0=ps_c0[0:4, :], scalar1=gb_sb[0:4, 0:1],
                scalar2=None, op0=ALU.add),
                waits=w, inc=s_ev)
            gev = s_ev.n
            for k in range(4):
                w = [(s_ev, gev), (s_bcp, s_bcp.n)]
                if k > 0:
                    w = []
                P.op("tensor", lambda e, k=k: e.transpose(
                    ps_d[:, k * 4:(k + 1) * 4],
                    in_=g_f32[0:4, k * 128:(k + 1) * 128],
                    identity=identf[0:4, 0:4]),
                    waits=w, inc=(s_tp if k == 3 else None))
            g_tp_v[gtq] = s_tp.n
            w = [(s_tp, s_tp.n)]
            if gtq >= 1:
                w.append((s_ln, gn_free_v[gtq - 1]))
            P.op("scalar", lambda e: e.copy(out=gn_sb[:], in_=ps_d[:, 0:16]),
                 waits=w, inc=s_bcp)
            gnv = s_bcp.n
            for k in range(4):
                base = k * 4
                P.op("vector", lambda e, base=base: e.reduce_max(
                    gm1[:, 0:1], gn_sb[:, base + 1:base + 4], axis=AX.X),
                    waits=[(s_bcp, gnv)])
                P.op("vector", lambda e, base=base: e.reduce_max(
                    gm1[:, 1:2], gn_sb[:, base + 2:base + 4], axis=AX.X))
                P.op("vector", lambda e, base=base: e.reduce_max(
                    gm1[:, 2:3], gn_sb[:, base + 0:base + 2], axis=AX.X))
                P.op("vector", lambda e, base=base: e.reduce_max(
                    gm1[:, 3:4], gn_sb[:, base + 0:base + 3], axis=AX.X),
                    inc=s_ln)
                mxv = s_ln.n
                P.op("vector", lambda e, base=base: e.tensor_tensor(
                    ga[:], gm1[:, 0:1], gn_sb[:, base:base + 1], op=ALU.is_gt),
                    waits=[(s_ln, mxv)])
                P.op("vector", lambda e: e.tensor_tensor(
                    gm1[:, 0:1], gm1[:, 1:2], gm1[:, 2:3], op=ALU.is_gt))
                P.op("vector", lambda e, base=base: e.tensor_tensor(
                    gm1[:, 1:2], gn_sb[:, base + 3:base + 4], gm1[:, 3:4],
                    op=ALU.is_gt), inc=s_ln)
                cv = s_ln.n
                gn_free_v[gtq] = cv
                P.op("vector", lambda e: e.tensor_tensor(
                    gm1[:, 2:3], gm1[:, 0:1], gm1[:, 1:2], op=ALU.add),
                    waits=[(s_ln, cv)], inc=s_ln)
                sumv = s_ln.n
                P.op("vector", lambda e: e.tensor_tensor(
                    ga[:], ga[:], gm1[:, 2:3], op=ALU.add),
                    waits=[(s_ln, sumv)], inc=s_ln)
                av = s_ln.n
                P.op("vector", lambda e: e.tensor_scalar(
                    out=maskn[:], in0=levels[:], scalar1=ga[:, 0:1],
                    scalar2=None, op0=ALU.is_le),
                    waits=[(s_ln, av), (s_te, s_te.n)], inc=s_ln)
                mk = s_ln.n
                P.op("tensor", lambda e: e.transpose(
                    ps_t[0:64, 0:128], in_=maskn[:], identity=identb[:]),
                    waits=[(s_ln, mk), (s_te, s_te.n)], inc=s_tp)
                P.op("scalar", lambda e, tq=tq, k=k: e.copy(
                    out=mT_sb[0:64, (tq * 4 + k) * 128:(tq * 4 + k + 1) * 128],
                    in_=ps_t[0:64, 0:128]),
                    waits=[(s_tp, s_tp.n)], inc=s_te)
            h1_ps_free = [(s_ev, s_ev.n)] if (th > 0 or tq > 0) else []
            if th > 0 and tq == 0:
                h1_ps_free.append((s_bcp, s_bcp.n))
            for c in range(DC):
                w = [(s_te, ln2T_done)] + (h1_ps_free if c == 0 else [])
                P.op("tensor", lambda e, c=c, tq=tq: e.matmul(
                    ps_c1[0:64, :], lhsT=AT_sb[:, c, :],
                    rhs=ln2T[:, c, tq * 512:(tq + 1) * 512],
                    start=(c == 0), stop=(c == DC - 1)),
                    waits=(w if c == 0 else ()),
                    inc=(s_mm if c == DC - 1 else None))
            P.op("scalar", lambda e, tq=tq: e.copy(
                out=h1T[0:64, tq * 512:(tq + 1) * 512], in_=ps_c1[0:64, :]),
                waits=[(s_mm, s_mm.n)], inc=s_ev)
        P.op("vector", lambda e: e.tensor_tensor(
            m_sb[0:64, :], h1T[0:64, :], mT_sb[0:64, :], op=ALU.mult),
            waits=[(s_ev, s_ev.n), (s_te, s_te.n)], inc=s_ln)
        msk_done = s_ln.n

        for ob in range(NOB3):
            gob = th * NOB3 + ob
            wldv = {}
            for li in range(4):
                osb, dhalf = li // 2, li % 2
                gli = gob * 4 + li
                w = []
                if gli >= 4:
                    w.append((s_tp, wrow_half_tp[gli - 4]))
                P.op("sync", lambda e, ob=ob, osb=osb, dhalf=dhalf, li=li:
                     e.dma_start(
                         out=wrow[li][:],
                         in_=Wn[ob * 256 + osb * 128:ob * 256 + (osb + 1) * 128,
                                dhalf * 2048:(dhalf + 1) * 2048]),
                     waits=w, inc16=s_wr)
                wldv[li] = s_wr.n
            wti = gob % 2
            wtbanks = [ps_d, ps_c0, ps_c1]
            for osb in range(2):
                for grp in range(8):
                    wg = wtb_state["g"]
                    wtb_state["g"] += 1
                    bank = wtbanks[wg % 3]
                    li = osb * 2 + grp // 4
                    for k in range(4):
                        c = grp * 4 + k
                        w = [(s_wr, wldv[li])]
                        if k == 0 and wg >= 3:
                            w.append((s_bcp, wtb_state["bcp"][wg - 3]))
                        if k == 0 and (wg % (NOB3 * 16)) < 3:
                            w.append((s_ev, s_ev.n))
                            w.append((s_bcp, s_bcp.n))
                        if k == 0 and grp == 0 and osb == 0 and gob >= 2:
                            w.append((s_mm, ffn_grp_mm[(gob - 2) * 8 + 7]))
                        P.op("tensor", lambda e, c=c, k=k, li=li, bank=bank:
                             e.transpose(
                                 bank[:, k * 128:(k + 1) * 128],
                                 in_=wrow[li][:, (c % 16) * 128:
                                              (c % 16 + 1) * 128],
                                 identity=identf[:]),
                             waits=w, inc=(s_tp if k == 3 else None))
                    P.op("scalar", lambda e, grp=grp, osb=osb, wti=wti, bank=bank:
                         e.copy(
                             out=WT[wti][:, grp * 4:(grp + 1) * 4,
                                         osb * 128:(osb + 1) * 128],
                             in_=bank[:].rearrange("p (c f) -> p c f", c=4)),
                         waits=[(s_tp, s_tp.n)], inc=s_bcp)
                    wtb_state["bcp"][wg] = s_bcp.n
                    if grp % 4 == 3:
                        wrow_half_tp[gob * 4 + osb * 2 + grp // 4] = s_tp.n
            wrow_tp[gob] = s_tp.n
            wt_done = s_bcp.n

            def mk_mm(ob, tbase, wt_done, wti, msk_done):
                def go():
                    global_unused = None
                    for tt in range(8):
                        nonlocal_state["ffn_grp"] += 1
                        fg = nonlocal_state["ffn_grp"] - 1
                        pq = [ps_a, ps_b][fg % 2]
                        w = [(s_bcp, wt_done), (s_ln, msk_done)]
                        if fg >= 2:
                            w.append((s_oe, oe_vals[-2]))
                        for c in range(DC):
                            P.op("tensor", lambda e, c=c, tt=tt, pq=pq, wti=wti:
                                 e.matmul(pq[:, 0:OB3],
                                          lhsT=ln2T[:, c, tt * 128:(tt + 1) * 128],
                                          rhs=WT[wti][:, c, :],
                                          start=(c == 0), stop=False),
                                 waits=(w if c == 0 else ()))
                        P.op("tensor", lambda e, tt=tt, ob=ob, pq=pq: e.matmul(
                            pq[:, 0:OB3],
                            lhsT=m_sb[0:64, tt * 128:(tt + 1) * 128],
                            rhs=BT_sb[0:64, ob * 256:(ob + 1) * 256],
                            start=False, stop=True), inc=s_mm)
                        mmv = s_mm.n
                        ffn_grp_mm.append(mmv)
                        ii = (fg + 1) % 2
                        qb_src = (tbase + tt * 128) // 512
                        w2 = [(s_x2p[0], x2_stores[qb_src][0]),
                              (s_x2p[1], x2_stores[qb_src][1])]
                        if fg >= 2:
                            w2.append((s_oe, oe_vals[-2]))
                        P.op("sync", lambda e, tbase=tbase, tt=tt, ob=ob, ii=ii:
                             e.dma_start(
                                 out=x2sl[ii][:],
                                 in_=x2_s[tbase + tt * 128:tbase + (tt + 1) * 128,
                                          ob * 256:(ob + 1) * 256]),
                             waits=w2, inc16=s_wp[ii])
                        xv2 = s_wp[ii].n
                        P.op("vector", lambda e, pq=pq, ob=ob, ii=ii:
                             e.tensor_tensor(
                                 t1sl3[:], pq[:, 0:OB3],
                                 b_bc[:, ob * 256:(ob + 1) * 256],
                                 op=ALU.add), waits=[(s_mm, mmv), (s_wp[ii], xv2)])
                        oc = nonlocal_state["out_ct"]
                        w3 = [(s_stp[oc % 2], out_store_v[oc - 2])] \
                            if oc >= 2 else []
                        P.op("vector", lambda e, ii=ii: e.tensor_tensor(
                            outst[ii][:], t1sl3[:], x2sl[ii][:], op=ALU.add),
                            waits=w3, inc=s_oe)
                        oe_vals.append(s_oe.n)
                        P.op("sync", lambda e, tbase=tbase, tt=tt, ob=ob, ii=ii:
                             e.dma_start(
                                 out=out_d[tbase + tt * 128:tbase + (tt + 1) * 128,
                                           ob * 256:(ob + 1) * 256],
                                 in_=outst[ii][:]),
                             waits=[(s_oe, s_oe.n)], inc16=s_stp[oc % 2])
                        out_store_v.append(s_stp[oc % 2].n)
                        nonlocal_state["out_ct"] += 1
                return go

            new_mm = mk_mm(ob, tbase, wt_done, wti, msk_done)
            if pend_mm is not None:
                pend_mm()
            pend_mm = new_mm
    if pend_mm is not None:
        pend_mm()
        pend_mm = None

    P.wait("sync", [(s_stp[0], s_stp[0].n), (s_stp[1], s_stp[1].n)])

    _check_deadlock(P)

    with nc.Block() as blk:
        for eng_name in ("sync", "gpsimd", "tensor", "scalar", "vector"):
            items = [o for o in P.ops if o[0] == eng_name]

            def make_body(items):
                def body(eng):
                    for _, waits, fn, incs in items:
                        for (hh, v) in waits:
                            eng.wait_ge(hh, v)
                        if fn is None:
                            continue
                        ins = fn(eng)
                        for (hh, amt) in incs:
                            ins = ins.then_inc(hh, amt)
                return body

            getattr(blk, eng_name)(make_body(items))

    p3.close()
    es.close()
    return nc


_CACHE = {}


def _get_program():
    if "nc" not in _CACHE:
        _CACHE["nc"] = build_program()
    return _CACHE["nc"]


def _prep_inputs(c, inputs):
    bidx, gg = c // 2, c % 2
    f32 = np.float32
    x = np.asarray(inputs["L0_attn_x"], f32)
    qkvw = np.asarray(inputs["L0_attn_attn_qkvw"], f32)
    qkvb = np.asarray(inputs["L0_attn_attn_qkvb"], f32)
    ow = np.asarray(inputs["L0_attn_attn_ow"], f32)
    obv = np.asarray(inputs["L0_attn_attn_ob"], f32)
    W = np.asarray(inputs["L0_ffn_W"], f32)
    bv = np.asarray(inputs["L0_ffn_b"], f32)
    A = np.asarray(inputs["L0_ffn_A"], f32)
    Bm = np.asarray(inputs["L0_ffn_B"], f32)
    gw = np.asarray(inputs["L0_ffn_gate_w"], f32)
    gb = np.asarray(inputs["L0_ffn_gate_b"], f32)

    own = slice(2048 * gg, 2048 * gg + 2048)
    oth = slice(2048 * (1 - gg), 2048 * (1 - gg) + 2048)
    x_loc = np.ascontiguousarray(np.concatenate([x[bidx, own], x[bidx, oth]], 0))

    qcols = qkvw[:, 2048 * gg:2048 * gg + 2048]
    kcols = qkvw[:, 4096 + 2048 * gg:4096 + 2048 * gg + 2048]
    qk = np.concatenate([qcols, kcols], 1)
    wqk_h = np.ascontiguousarray(
        qk.reshape(32, 128, 32, 128).transpose(2, 1, 0, 3).reshape(32, 128, -1))
    vcols = qkvw[:, 8192 + 2048 * gg:8192 + 2048 * gg + 2048]
    wv_h = np.ascontiguousarray(
        vcols.reshape(32, 128, 8, 256).transpose(2, 1, 0, 3).reshape(8, 128, -1))
    qkvb_qk = np.ascontiguousarray(np.concatenate(
        [qkvb[2048 * gg:2048 * gg + 2048],
         qkvb[4096 + 2048 * gg:4096 + 2048 * gg + 2048]]).reshape(32, 128).T)
    vb = np.ascontiguousarray(qkvb[8192 + 2048 * gg:8192 + 2048 * gg + 2048])
    ow_p = np.concatenate([ow[2048 * gg:2048 * gg + 2048],
                           ow[2048 * (1 - gg):2048 * (1 - gg) + 2048]], 0)
    ow_hh = np.ascontiguousarray(
        ow_p.reshape(32, 128, 16, 256).transpose(2, 1, 0, 3).reshape(16, 128, -1))
    B_h = np.ascontiguousarray(
        Bm.reshape(32, 128, 64).transpose(1, 0, 2).reshape(128, -1))

    bf = ml_dtypes.bfloat16
    lev = np.array([0] * 8 + [1] * 8 + [2] * 16 + [3] * 32, f32)
    return {
        "x_l": x_loc, "wqk": wqk_h, "wv": wv_h, "qkvb_qk": qkvb_qk,
        "vb_row": vb, "ow_h": ow_hh, "ob_row": obv, "Wn": W, "b_row": bv,
        "A_in": A, "B_in": B_h, "gw_in": gw, "gb_in": gb,
        "identb_in": np.eye(128, dtype=bf),
        "identf_in": np.eye(128, dtype=f32),
        "onesf_in": np.ones((1, 128), f32),
        "onesb_in": np.ones((128, 1), bf),
        "levels_in": np.tile(lev, (128, 1)),
        "masks_in": np.tile(np.array([0.0, 1.0] if gg == 0 else [1.0, 0.0], f32),
                            (128, 1)),
        "eps_in": np.full((128, 1), 1e-5, f32),
    }


def kernel(**inputs):
    nc = _get_program()
    in_maps = [_prep_inputs(c, inputs) for c in range(8)]
    res = run_bass_kernel_spmd(nc, in_maps, core_ids=list(range(8)))
    out = np.empty((4, 4096, 4096), np.float32)
    for c in range(8):
        bidx, gg = c // 2, c % 2
        out[bidx, 2048 * gg:2048 * gg + 2048] = res.results[c]["out"]
    return out


# version 22
# speedup vs baseline: 1.1369x; 1.0400x over previous
import sys

for _p in ("/opt/trn_rl_repo", "/root/.axon_site/_ro/trn_rl_repo"):
    if _p not in sys.path:
        sys.path.append(_p)

import contextlib

import numpy as np
import ml_dtypes

import concourse.bass as bass
import concourse.mybir as mybir
from concourse.bass_utils import run_bass_kernel_spmd

F32 = mybir.dt.float32
BF16 = mybir.dt.bfloat16
AF = mybir.ActivationFunctionType
ALU = mybir.AluOpType
AX = mybir.AxisListType

H = 4096
DC = 32
NH = 16
SL = 4096
TOWN = 2048
QB = 512
NSL = 8
QORDER = [4, 0, 5, 1, 6, 2, 7, 3]
SCALE = float(1.0 / np.sqrt(128.0))
EPS = 1e-5
GROUPS = [[0, 1], [2, 3], [4, 5], [6, 7]]
OB3 = 256
NOB3 = H // OB3


class Sem:
    def __init__(self, handle):
        self.h = handle
        self.n = 0


class Prog:

    def __init__(self):
        self.ops = []

    def op(self, engine, fn, waits=(), inc=None, inc16=None):
        incs = []
        if inc is not None:
            inc.n += 1
            incs.append((inc.h, 1))
        if inc16 is not None:
            inc16.n += 16
            incs.append((inc16.h, 16))
        self.ops.append((engine, tuple((s.h, v) for (s, v) in waits), fn, incs))

    def wait(self, engine, waits):
        self.ops.append((engine, tuple((s.h, v) for (s, v) in waits), None, []))


def _check_deadlock(prog):
    qs = {}
    for it in prog.ops:
        qs.setdefault(it[0], []).append(it)
    heads = {k: 0 for k in qs}
    counts = {}
    progress = True
    while progress:
        progress = False
        for eng, lst in qs.items():
            while heads[eng] < len(lst):
                _, waits, _, incs = lst[heads[eng]]
                if all(counts.get(h, 0) >= v for (h, v) in waits):
                    for (h, amt) in incs:
                        counts[h] = counts.get(h, 0) + amt
                    heads[eng] += 1
                    progress = True
                else:
                    break
    stuck = {e: (heads[e], len(l)) for e, l in qs.items() if heads[e] < len(l)}
    if stuck:
        lines = []
        for e, (i, n) in stuck.items():
            _, waits, _, _ = qs[e][i]
            lines.append(f"{e}: op {i}/{n} waits " + str(
                [(getattr(h, "name", h), v, counts.get(h, 0)) for h, v in waits]))
        raise RuntimeError("sync deadlock:\n" + "\n".join(lines))


def build_program():
    nc = bass.Bass("TRN2", target_bir_lowering=False)
    es = contextlib.ExitStack()

    def dram(name, shape, dt, kind=None):
        if kind is None:
            return nc.dram_tensor(name, shape, dt).ap()
        return nc.dram_tensor(name, shape, dt, kind=kind).ap()

    x_l = dram("x_l", [SL, H], F32, "ExternalInput")
    wqk = dram("wqk", [32, 128, DC * 128], F32, "ExternalInput")
    wv = dram("wv", [8, 128, DC * 256], F32, "ExternalInput")
    qkvb_qk = dram("qkvb_qk", [128, 32], F32, "ExternalInput")
    vb_row = dram("vb_row", [2048], F32, "ExternalInput")
    ow_h = dram("ow_h", [16, 128, DC * 256], F32, "ExternalInput")
    ob_row = dram("ob_row", [H], F32, "ExternalInput")
    Wn = dram("Wn", [H, H], F32, "ExternalInput")
    b_row = dram("b_row", [H], F32, "ExternalInput")
    A_in = dram("A_in", [64, H], F32, "ExternalInput")
    B_in = dram("B_in", [128, DC * 64], F32, "ExternalInput")
    gw_in = dram("gw_in", [4, H], F32, "ExternalInput")
    gb_in = dram("gb_in", [4], F32, "ExternalInput")
    identb_in = dram("identb_in", [128, 128], BF16, "ExternalInput")
    identf_in = dram("identf_in", [128, 128], F32, "ExternalInput")
    onesf_in = dram("onesf_in", [1, 128], F32, "ExternalInput")
    onesb_in = dram("onesb_in", [128, 1], BF16, "ExternalInput")
    levels_in = dram("levels_in", [128, 64], F32, "ExternalInput")
    masks_in = dram("masks_in", [128, 2], F32, "ExternalInput")
    eps_in = dram("eps_in", [128, 1], F32, "ExternalInput")
    out_d = dram("out", [TOWN, H], F32, "ExternalOutput")

    qT_s = dram("qT_s", [NH, 128, SL], BF16)
    kT_s = dram("kT_s", [NH, 128, SL], BF16)
    v_s = dram("v_s", [NH, 128, DC, 128], BF16)
    cc_in = [dram(f"cc_in{j}", [2 * TOWN, QB], BF16) for j in range(4)]
    cc_out = [dram(f"cc_out{j}", [TOWN, QB], BF16) for j in range(4)]
    ow_bf_s = dram("ow_bf_s", [16, 128, DC * 256], BF16)
    x2_s = dram("x2_s", [TOWN, H], F32)
    ln2_s = dram("ln2_s", [TOWN, H], BF16)

    def sbuf(name, shape, dt, scope=es):
        return scope.enter_context(nc.sbuf_tensor(name, shape, dt))

    def psum(name, shape, dt):
        return es.enter_context(nc.psum_tensor(name, shape, dt))

    identb = sbuf("identb", [128, 128], BF16)
    identf = sbuf("identf", [128, 128], F32)
    onesf = sbuf("onesf", [1, 128], F32)
    onesb = sbuf("onesb", [128, 1], BF16)
    levels = sbuf("levels", [128, 64], F32)
    masks = sbuf("masks", [128, 2], F32)
    qkvb_sb = sbuf("qkvb_sb", [128, 32], F32)
    row_sb = sbuf("row_sb", [1, 1024], F32)
    gb_sb = sbuf("gb_sb", [4, 1], F32)
    eps_sb = sbuf("eps_sb", [128, 1], F32)

    ps_a = psum("ps_a", [128, 1024], F32)
    ps_b = psum("ps_b", [128, 1024], F32)
    ps_c0 = psum("ps_c0", [128, 512], F32)
    ps_c1 = psum("ps_c1", [128, 512], F32)
    ps_d = psum("ps_d", [128, 512], F32)
    ps_t = psum("ps_t", [128, 1024], BF16)

    semnames = ["s_cst", "s_row", "s_ln", "s_sc", "s_tp", "s_te", "s_mm",
                "s_ev", "s_exp", "s_ctx", "s_dt", "s_dmm", "s_rc", "s_bc",
                "s_bcp", "s_ce", "s_ccm", "s_ccd", "s_cc", "s_rx", "s_op",
                "s_oe", "s_l2", "s_bar", "s_xsl", "s_wr",
                "s_xh0", "s_xh1", "s_wp0", "s_wp1", "s_stp0", "s_stp1",
                "s_vsp0", "s_vsp1", "s_kvp0", "s_kvp1", "s_x2p0", "s_x2p1",
                "s_xr2", "s_xr3", "s_l2b", "s_lh2", "s_lh3"]
    S = {n: Sem(es.enter_context(nc.semaphore(n))) for n in semnames}
    s_cst, s_row, s_ln, s_sc, s_tp, s_te = (S[k] for k in semnames[0:6])
    s_mm, s_ev, s_exp, s_ctx, s_dt, s_dmm = (S[k] for k in semnames[6:12])
    s_rc, s_bc, s_bcp, s_ce, s_ccm, s_ccd = (S[k] for k in semnames[12:18])
    s_cc, s_rx, s_op, s_oe, s_l2, s_bar = (S[k] for k in semnames[18:24])
    s_xsl, s_wr = S["s_xsl"], S["s_wr"]
    s_xh = [S["s_xh0"], S["s_xh1"]]
    s_xr = [S["s_xh0"], S["s_xh1"], S["s_xr2"], S["s_xr3"]]
    s_l2p = [S["s_l2"], S["s_l2b"]]
    s_lh = [S["s_l2"], S["s_l2b"], S["s_lh2"], S["s_lh3"]]
    s_wp = [S["s_wp0"], S["s_wp1"]]
    s_stp = [S["s_stp0"], S["s_stp1"]]
    s_vsp = [S["s_vsp0"], S["s_vsp1"]]
    s_kvp = [S["s_kvp0"], S["s_kvp1"]]
    s_x2p = [S["s_x2p0"], S["s_x2p1"]]

    P = Prog()

    for dst, src in [(identb, identb_in), (identf, identf_in), (onesf, onesf_in),
                     (onesb, onesb_in), (levels, levels_in), (masks, masks_in),
                     (qkvb_sb, qkvb_qk), (eps_sb, eps_in)]:
        P.op("sync", lambda e, d=dst, s=src: e.dma_start(out=d[:], in_=s[:]),
             inc16=s_cst)
    P.op("sync", lambda e: e.dma_start(
        out=gb_sb[:, :], in_=gb_in.rearrange("(p a) -> p a", a=1)), inc16=s_cst)
    consts_done = s_cst.n

    rb_state = {"tp": {}, "te": {}, "i": 0}

    def emit_row_bcast(row_src, n, dest_sb):
        row2 = row_src.rearrange("(a f) -> a f", a=1)
        rbanks = [ps_d, ps_c0, ps_c1]
        for i in range((n + 511) // 512):
            w = min(512, n - i * 512)
            gi = rb_state["i"]
            rb_state["i"] += 1
            par = gi % 2
            bank = rbanks[gi % 3]
            ldsem = [s_row, s_cst][par]
            wv = [(s_tp, rb_state["tp"][gi - 2])] if gi >= 2 else []
            P.op("sync", lambda e, i=i, w=w, par=par: e.dma_start(
                out=row_sb[:, par * 512:par * 512 + w],
                in_=row2[0:1, i * 512:i * 512 + w]),
                waits=wv, inc16=ldsem)
            ldv = ldsem.n
            wv2 = [(ldsem, ldv), (s_cst, consts_done)]
            if gi >= 3:
                wv2.append((s_te, rb_state["te"][gi - 3]))
            P.op("tensor", lambda e, w=w, par=par, bank=bank: e.matmul(
                bank[:, 0:w], lhsT=onesf[:, :],
                rhs=row_sb[:, par * 512:par * 512 + w],
                start=True, stop=True),
                waits=wv2, inc=s_tp)
            rb_state["tp"][gi] = s_tp.n
            P.op("scalar", lambda e, i=i, w=w, bank=bank: e.copy(
                out=dest_sb[:, i * 512:i * 512 + w], in_=bank[:, 0:w]),
                waits=[(s_tp, s_tp.n)], inc=s_te)
            rb_state["te"][gi] = s_te.n

    p1 = contextlib.ExitStack()
    vb_bc = sbuf("vb_bc", [128, 2048], BF16, p1)
    emit_row_bcast(vb_row, 2048, vb_bc)
    vbbc_done = s_te.n


    ln1T = sbuf("ln1T", [128, DC, 2048], BF16, p1)
    x_sb = [sbuf(f"x_sb{i}", [128, 2048], F32, p1) for i in range(2)]
    ln1h = sbuf("ln1h", [128, 2048], BF16, p1)
    w_u = [sbuf(f"w_u{i}", [128, 8192], BF16, p1) for i in range(2)]
    qkst = [sbuf(f"qkst{i}", [128, 512], BF16, p1) for i in range(2)]
    vst = [sbuf(f"vst{i}", [128, 256], BF16, p1) for i in range(2)]
    stats1 = sbuf("stats1", [128, 48], F32, p1)
    mv1 = [sbuf(f"mv1_{i}", [128, 2], F32, p1) for i in range(2)]
    rstd1 = [sbuf(f"rstd1_{i}", [128, 1], F32, p1) for i in range(2)]
    lnt1 = sbuf("lnt1", [128, 1], F32, p1)
    nmr1 = [sbuf(f"nmr1_{i}", [128, 1], F32, p1) for i in range(2)]

    x_free = {}
    ln_v = {}
    grp_idx = 0
    grp_ev = []
    qk_store_v = []
    v_store_v = []
    wu_free = {}
    wu_lidx = 0

    for tsb in range(2):
        t0 = tsb * 2048
        for tt in range(16):
            gt = tsb * 16 + tt
            row = t0 + tt * 128
            xv = {}
            for hf in range(2):
                w = [(s_sc, x_free[(gt - 1, hf)])] if gt >= 1 else \
                    [(s_cst, consts_done)]
                P.op("sync", lambda e, row=row, hf=hf: e.dma_start(
                    out=x_sb[hf][:],
                    in_=x_l[row:row + 128, hf * 2048:(hf + 1) * 2048]),
                    waits=w, inc16=s_xh[hf])
                xv[hf] = s_xh[hf].n
                for i in range(4):
                    P.op("vector", lambda e, hf=hf, i=i: e.bn_stats(
                        stats1[:, (hf * 4 + i) * 6:(hf * 4 + i) * 6 + 6],
                        x_sb[hf][:, i * 512:(i + 1) * 512]),
                        waits=[(s_xh[hf], xv[hf])],
                        inc=(s_ln if (hf == 1 and i == 3) else None))
            pp = gt % 2
            w = [(s_ln, s_ln.n)]
            if gt >= 2:
                w.append((s_sc, ln_v[gt - 2]))
            P.op("vector", lambda e, pp=pp: e.bn_aggr(mv1[pp][:], stats1[:]),
                 waits=w, inc=s_ln)
            P.op("scalar", lambda e, pp=pp: e.activation(
                out=lnt1[:], in_=mv1[pp][:, 1:2], func=AF.Ln,
                bias=eps_sb[:, 0:1]),
                waits=[(s_ln, s_ln.n)], inc=s_sc)
            ln_v[gt] = s_sc.n
            w = [(s_sc, s_sc.n)]
            if gt >= 2:
                w.append((s_sc, x_free[(gt - 2, 1)]))
            P.op("scalar", lambda e, pp=pp: e.activation(
                out=rstd1[pp][:], in_=lnt1[:], func=AF.Exp, scale=-0.5),
                waits=w, inc=s_sc)
            P.op("vector", lambda e, pp=pp: e.scalar_tensor_tensor(
                out=nmr1[pp][:], in0=mv1[pp][:, 0:1], scalar=-1.0,
                in1=rstd1[pp][:], op0=ALU.mult, op1=ALU.mult),
                waits=[(s_sc, s_sc.n)], inc=s_ln)
            nmr_v = s_ln.n
            for hf in range(2):
                P.op("scalar", lambda e, hf=hf, pp=pp: e.activation(
                    out=ln1h[:], in_=x_sb[hf][:], func=AF.Identity,
                    bias=nmr1[pp][:, 0:1], scale=rstd1[pp][:, 0:1]),
                    waits=[(s_ln, nmr_v), (s_tp, s_tp.n)], inc=s_sc)
                x_free[(gt, hf)] = s_sc.n
                lh = s_sc.n
                for grp in range(2):
                    for k in range(8):
                        c_loc = grp * 8 + k
                        w = [(s_sc, lh)]
                        if k == 0:
                            w.append((s_te, s_te.n))
                        P.op("tensor", lambda e, c_loc=c_loc, k=k: e.transpose(
                            ps_t[:, k * 128:(k + 1) * 128],
                            in_=ln1h[:, c_loc * 128:(c_loc + 1) * 128],
                            identity=identb[:]),
                            waits=w, inc=(s_tp if k == 7 else None))
                    c0 = hf * 16 + grp * 8
                    P.op("scalar", lambda e, c0=c0, tt=tt: e.copy(
                        out=ln1T[:, c0:c0 + 8, tt * 128:(tt + 1) * 128],
                        in_=ps_t[:].rearrange("p (c f) -> p c f", c=8)),
                        waits=[(s_tp, s_tp.n)], inc=s_te)
        ln1_done = s_te.n

        for ft in range(32):
            w = [(s_te, ln1_done)]
            if wu_lidx >= 2:
                w.append((s_mm, wu_free[wu_lidx - 2]))
            my_wu = wu_lidx % 2
            P.op("gpsimd", lambda e, ft=ft, my_wu=my_wu: e.dma_start(
                out=w_u[my_wu][:, 0:4096], in_=wqk[ft][:, 0:4096]),
                waits=w, inc16=s_wp[my_wu])
            wld = s_wp[my_wu].n
            wu_lidx += 1
            for tb in range(4):
                pq = [ps_a, ps_b][grp_idx % 2]
                w = [(s_wp[my_wu], wld), (s_te, ln1_done)]
                if grp_idx >= 2:
                    w.append((s_ev, grp_ev[grp_idx - 2]))
                for c in range(DC):
                    P.op("tensor", lambda e, c=c, tb=tb, pq=pq, my_wu=my_wu:
                         e.matmul(pq[:, 0:512],
                                  lhsT=w_u[my_wu][:, c * 128:(c + 1) * 128],
                                  rhs=ln1T[:, c, tb * 512:(tb + 1) * 512],
                                  start=(c == 0), stop=(c == DC - 1)),
                         waits=(w if c == 0 else ()),
                         inc=(s_mm if c == DC - 1 else None))
                wu_free[wu_lidx - 1] = s_mm.n
                qe = len(qk_store_v)
                w = [(s_mm, s_mm.n)]
                if qe >= 2:
                    w.append((s_stp[qe % 2], qk_store_v[qe - 2]))
                P.op("scalar", lambda e, ft=ft, pq=pq, i=qe % 2: e.activation(
                    out=qkst[i][:], in_=pq[:, 0:512], func=AF.Identity,
                    bias=qkvb_sb[:, ft:ft + 1]),
                    waits=w, inc=s_ev)
                grp_ev.append(s_ev.n)
                grp_idx += 1
                dst = qT_s if ft < 16 else kT_s
                hh = ft % 16
                P.op("sync", lambda e, dst=dst, hh=hh, t0=t0, tb=tb, i=qe % 2:
                     e.dma_start(
                         out=dst[hh, :, t0 + tb * 512:t0 + (tb + 1) * 512],
                         in_=qkst[i][:]),
                     waits=[(s_ev, s_ev.n)], inc16=s_stp[qe % 2])
                qk_store_v.append(s_stp[qe % 2].n)

        for fb in range(8):
            w = [(s_te, ln1_done)]
            if wu_lidx >= 2:
                w.append((s_mm, wu_free[wu_lidx - 2]))
            my_wu = wu_lidx % 2
            P.op("gpsimd", lambda e, fb=fb, my_wu=my_wu: e.dma_start(
                out=w_u[my_wu][:], in_=wv[fb][:]),
                waits=w, inc16=s_wp[my_wu])
            wld = s_wp[my_wu].n
            wu_lidx += 1
            for tt in range(16):
                pq = [ps_a, ps_b][grp_idx % 2]
                w = [(s_wp[my_wu], wld)]
                if grp_idx >= 2:
                    w.append((s_ev, grp_ev[grp_idx - 2]))
                for c in range(DC):
                    P.op("tensor", lambda e, c=c, tt=tt, pq=pq, my_wu=my_wu:
                         e.matmul(pq[:, 0:256],
                                  lhsT=ln1T[:, c, tt * 128:(tt + 1) * 128],
                                  rhs=w_u[my_wu][:, c * 256:(c + 1) * 256],
                                  start=(c == 0), stop=(c == DC - 1)),
                         waits=(w if c == 0 else ()),
                         inc=(s_mm if c == DC - 1 else None))
                wu_free[wu_lidx - 1] = s_mm.n
                ve = len(v_store_v)
                w = [(s_mm, s_mm.n), (s_te, vbbc_done)]
                if ve >= 2:
                    w.append((s_vsp[ve % 2], v_store_v[ve - 2]))
                P.op("vector", lambda e, fb=fb, pq=pq, i=ve % 2: e.tensor_tensor(
                    out=vst[i][:], in0=pq[:, 0:256],
                    in1=vb_bc[:, fb * 256:(fb + 1) * 256], op=ALU.add),
                    waits=w, inc=s_ev)
                grp_ev.append(s_ev.n)
                grp_idx += 1
                ktile = tsb * 16 + tt
                for hh in range(2):
                    head = fb * 2 + hh
                    P.op("sync", lambda e, head=head, ktile=ktile, hh=hh, i=ve % 2:
                         e.dma_start(out=v_s[head, :, ktile, :],
                                     in_=vst[i][:, hh * 128:(hh + 1) * 128]),
                         waits=[(s_ev, s_ev.n)], inc16=s_vsp[ve % 2])
                v_store_v.append(s_vsp[ve % 2].n)

    P.op("sync", lambda e: e.nop(),
         waits=[(s_stp[0], s_stp[0].n), (s_stp[1], s_stp[1].n),
                (s_vsp[0], s_vsp[0].n), (s_vsp[1], s_vsp[1].n)], inc=s_bar)
    P.op("tensor", lambda e: e.nop(), waits=[(s_mm, s_mm.n)], inc=s_bar)
    P.op("scalar", lambda e: e.nop(), waits=[(s_ev, s_ev.n)], inc=s_bar)
    P.op("vector", lambda e: e.nop(), waits=[(s_ev, s_ev.n)], inc=s_bar)
    P.op("gpsimd", lambda e: e.nop(),
         waits=[(s_wp[0], s_wp[0].n), (s_wp[1], s_wp[1].n)], inc=s_bar)
    bar1 = s_bar.n
    p1.close()

    p2 = contextlib.ExitStack()
    probsT2 = [sbuf(f"probsT{i}", [128, 16, 1024], BF16, p2) for i in range(2)]
    kT_sb = [sbuf(f"kT_sb{i}", [128, SL], BF16, p2) for i in range(2)]
    v_sb = [sbuf(f"v_sb{i}", [128, DC, 128], BF16, p2) for i in range(2)]
    qT_sb = [sbuf(f"qT_sb{i}", [128, QB], BF16, p2) for i in range(2)]
    ctxT = sbuf("ctxT", [128, NH, QB], BF16, p2)
    rx = sbuf("rx", [128, NH, QB], BF16, p2)
    ccm = [sbuf(f"ccm{i}", [128, 1024], BF16, p2) for i in range(2)]
    acc4 = sbuf("acc4", [128, 4, 512], BF16, p2)
    dtmp = sbuf("dtmp", [128, 512], F32, p2)
    dfin = sbuf("dfin", [128, 512], F32, p2)
    den_bf = sbuf("den_bf", [128, 512], BF16, p2)
    recip = [sbuf(f"recip{i}", [1, 512], F32, p2) for i in range(2)]
    bcast_sb = [sbuf(f"bcast{i}", [128, 512], F32, p2) for i in range(2)]
    ow_cb = [sbuf(f"ow_cb{i}", [128, 4096], BF16, p2) for i in range(3)]
    xsl = [sbuf(f"xsl{i}", [128, 256], F32, p2) for i in range(4)]
    t1sl = sbuf("t1sl", [128, 256], F32, p2)
    x2st = [sbuf(f"x2st{i}", [128, 256], F32, p2) for i in range(2)]
    stats2 = sbuf("stats2", [128, 4, 96], F32, p2)
    mv2 = [sbuf(f"mv2_{i}", [128, 2], F32, p2) for i in range(2)]
    lnt2 = sbuf("lnt2", [128, 1], F32, p2)
    rstd2 = [sbuf(f"rstd2_{i}", [128, 1], F32, p2) for i in range(2)]
    nmr2 = [sbuf(f"nmr2_{i}", [128, 1], F32, p2) for i in range(2)]
    x2rd = [sbuf(f"x2rd{i}", [128, 512], F32, p2) for i in range(3)]
    ln2h = [sbuf(f"ln2h{i}", [128, 512], BF16, p2) for i in range(4)]
    ob_bc = sbuf("ob_bc", [128, H], BF16, p2)

    for e_ in ("sync", "tensor", "scalar", "vector", "gpsimd"):
        P.wait(e_, [(s_bar, bar1)])
    emit_row_bcast(ob_row, H, ob_bc)
    obbc_done = s_te.n

    ow_ld = 0
    op_group_v = []
    oe_vals = []
    x2_stores = {}
    l2_stores = {}
    x2_idx = 0
    x2_store_v = []
    ln2_apply_v = []
    ln2_st_v = []
    ln2_ln_v = {}
    ln2_gt = 0

    spill_v = {}
    ow_state = {"i": 0, "free": {}}
    oe_slot_end = {}
    ccd_hist = []
    n_cc_slots = [0]
    prev_op_end = None
    kv_pref = {}

    def emit_kv_load(h, hg, par, qb_ld):
        w = [(s_ctx, (hg - 1) * 16)] if hg >= 2 else []
        P.op("sync", lambda e, h=h, par=par: e.dma_start(
            out=kT_sb[par][:], in_=kT_s[h]), waits=w, inc16=s_kvp[par])
        P.op("sync", lambda e, h=h, par=par: e.dma_start(
            out=v_sb[par][:].rearrange("p c f -> p (c f)"),
            in_=v_s[h].rearrange("p c f -> p (c f)")), inc16=s_kvp[par])
        P.op("sync", lambda e, h=h, par=par, qb_ld=qb_ld: e.dma_start(
            out=qT_sb[par][:], in_=qT_s[h, :, qb_ld * QB:(qb_ld + 1) * QB]),
            inc16=s_kvp[par])
        return s_kvp[par].n

    for s in range(NSL):
        qb = QORDER[s]
        pend_spills = list(range((s // 2) * 8, (s // 2) * 8 + 8)) \
            if s in (0, 2) else []

        def emit_spill():
            if not pend_spills:
                return
            ob = pend_spills.pop(0)
            i = ob % 2
            P.op("gpsimd", lambda e, ob=ob: e.dma_start(
                out=ow_bf_s[ob][:], in_=ow_h[ob][:]),
                waits=[(s_bar, bar1)], inc16=s_vsp[i])
            spill_v[ob] = s_vsp[i].n
        if qb < 4:
            j = qb
            w = [(s_cc, j + 1)]
            if op_group_v:
                w.append((s_op, op_group_v[-1]))
            P.op("gpsimd", lambda e, j=j: e.dma_start(
                out=rx[:], in_=cc_out[j].rearrange("(h p) f -> p h f", p=128)),
                waits=w, inc16=s_rx)
        rx_v = s_rx.n
        has_prev_op = op_group_v and prev_op_end is not None

        def emit_den(hh):
            w = [(s_dt, hh + 1)]
            if hh >= 1:
                w.append((s_bcp, hh))
            else:
                w.append((s_te, obbc_done))
            P.op("tensor", lambda e, hh=hh: e.matmul(
                ps_d[0:1, :], lhsT=onesb[:], rhs=den_bf[:], start=True,
                stop=True), waits=w, inc=s_dmm)

        def emit_recip(hh):
            w = [(s_dmm, hh + 1)]
            if hh >= 2:
                w.append((s_bc, hh - 1))
            P.op("vector", lambda e, i=hh % 2: e.reciprocal(
                recip[i][:], ps_d[0:1, :]), waits=w, inc=s_rc)

        def emit_bcast(hh):
            P.op("tensor", lambda e, i=hh % 2: e.matmul(
                ps_d[:], lhsT=onesf[:], rhs=recip[i][:], start=True, stop=True),
                waits=[(s_rc, hh + 1)], inc=s_bc)

        def emit_bcopy(hh):
            w = [(s_bc, hh + 1)]
            if hh >= 2:
                w.append((s_ce, hh - 1))
            P.op("scalar", lambda e, i=hh % 2: e.copy(
                out=bcast_sb[i][:], in_=ps_d[:]), waits=w, inc=s_bcp)

        def emit_evac(hh):
            hloc = hh % NH
            pc2 = [ps_c0, ps_c1][hh % 2]
            P.op("vector", lambda e, hloc=hloc, pc2=pc2, i=hh % 2: e.tensor_tensor(
                ctxT[:, hloc, :], pc2[:], bcast_sb[i][:], op=ALU.mult),
                waits=[(s_ctx, (hh + 1) * 16), (s_bcp, hh + 1)], inc=s_ce)

        def emit_ccm(hh):
            hloc = hh % NH
            ss = QORDER[hh // NH] - 4
            w = [(s_ccd, ccd_hist[-2])] if len(ccd_hist) >= 2 else []
            P.op("vector", lambda e, hloc=hloc, i=hh % 2: e.tensor_scalar(
                out=ccm[i][:, 0:512], in0=ctxT[:, hloc, :],
                scalar1=masks[:, 0:1], scalar2=None, op0=ALU.mult),
                waits=w)
            P.op("vector", lambda e, hloc=hloc, i=hh % 2: e.tensor_scalar(
                out=ccm[i][:, 512:1024], in0=ctxT[:, hloc, :],
                scalar1=masks[:, 1:2], scalar2=None, op0=ALU.mult),
                inc=s_ccm)
            P.op("gpsimd", lambda e, ss=ss, hloc=hloc, i=hh % 2: e.dma_start(
                out=cc_in[ss][hloc * 128:(hloc + 1) * 128, :],
                in_=ccm[i][:, 0:512]),
                waits=[(s_ccm, s_ccm.n)], inc16=s_ccd)
            P.op("gpsimd", lambda e, ss=ss, hloc=hloc, i=hh % 2: e.dma_start(
                out=cc_in[ss][TOWN + hloc * 128:TOWN + (hloc + 1) * 128, :],
                in_=ccm[i][:, 512:1024]), inc16=s_ccd)
            ccd_hist.append(s_ccd.n)

        def emit_back(hh):
            emit_evac(hh)
            if QORDER[hh // NH] >= 4:
                emit_ccm(hh)

        for h in range(NH):
            hg = s * NH + h
            par = hg % 2
            if hg in kv_pref:
                kvv = kv_pref.pop(hg)
            else:
                kvv = emit_kv_load(h, hg, par, qb)

            sc_base = s_mm.n
            for gk in range(16):
                G = hg * 16 + gk
                pq = [ps_a, ps_b][G % 2]
                w = [(s_kvp[par], kvv)]
                if G >= 2:
                    w.append((s_exp, G - 1))
                if gk == 0 and h == 0 and prev_op_end is not None:
                    w.append((s_oe, prev_op_end))
                for half in range(2):
                    kt = gk * 2 + half
                    P.op("tensor", lambda e, kt=kt, half=half, par=par, pq=pq:
                         e.matmul(pq[:, half * 512:(half + 1) * 512],
                                  lhsT=kT_sb[par][:, kt * 128:(kt + 1) * 128],
                                  rhs=qT_sb[par][:], start=True, stop=True),
                         waits=(w if half == 0 else ()),
                         inc=(s_mm if half == 1 else None))
            if hg >= 1 and (hg - 1) % NH != NH - 1:
                emit_den(hg - 1)

            for gk in range(16):
                G = hg * 16 + gk
                pq = [ps_a, ps_b][G % 2]
                w = [(s_mm, sc_base + gk + 1)]
                if hg >= 2:
                    w.append((s_ctx, (hg - 2) * 16 + gk + 1))
                    w.append((s_dt, hg - 1))
                P.op("scalar", lambda e, gk=gk, pq=pq, pb=hg % 2: e.activation(
                    out=probsT2[pb][:, gk, :], in_=pq[:], func=AF.Exp, scale=SCALE),
                    waits=w, inc=s_exp)
            if hg >= 1 and (hg - 1) % NH != NH - 1:
                emit_bcopy(hg - 1)
            if h >= 2 and h % 2 == 0:
                emit_spill()

            if hg >= 2 and (hg - 2) % NH < NH - 2:
                emit_back(hg - 2)
            if hg >= 1 and (hg - 1) % NH != NH - 1:
                emit_recip(hg - 1)
            for gk in range(16):
                G = hg * 16 + gk
                q = gk // 4
                w = [(s_exp, G + 1)]
                if gk % 4 == 0:
                    P.op("vector", lambda e, gk=gk, q=q, pb=hg % 2: e.tensor_tensor(
                        acc4[:, q, :], probsT2[pb][:, gk, 0:512],
                        probsT2[pb][:, gk, 512:1024], op=ALU.add), waits=w)
                else:
                    P.op("vector", lambda e, gk=gk, q=q, pb=hg % 2: e.tensor_tensor(
                        acc4[:, q, :], acc4[:, q, :], probsT2[pb][:, gk, 0:512],
                        op=ALU.add), waits=w)
                    P.op("vector", lambda e, gk=gk, q=q, pb=hg % 2: e.tensor_tensor(
                        acc4[:, q, :], acc4[:, q, :], probsT2[pb][:, gk, 512:1024],
                        op=ALU.add))
            P.op("vector", lambda e: e.tensor_tensor(
                dfin[:], acc4[:, 0, :], acc4[:, 1, :], op=ALU.add))
            P.op("vector", lambda e: e.tensor_tensor(
                dtmp[:], acc4[:, 2, :], acc4[:, 3, :], op=ALU.add))
            P.op("vector", lambda e: e.tensor_tensor(
                dfin[:], dfin[:], dtmp[:], op=ALU.add))
            w = [(s_dmm, hg)] if hg >= 1 else []
            P.op("vector", lambda e: e.tensor_copy(den_bf[:], dfin[:]),
                 waits=w, inc=s_dt)

            pc = [ps_c0, ps_c1][hg % 2]
            for p in range(16):
                w = [(s_exp, hg * 16 + p + 1)]
                if p == 0 and hg >= 2:
                    w.append((s_ce, hg - 1))
                if p == 0 and h < 2 and prev_op_end is not None:
                    w.append((s_oe, prev_op_end))
                for half in range(2):
                    kt = p * 2 + half
                    P.op("tensor", lambda e, kt=kt, p=p, half=half, par=par,
                         pc=pc, pb=hg % 2: e.matmul(
                             pc[:], lhsT=v_sb[par][:, kt, :],
                             rhs=probsT2[pb][:, p, half * 512:(half + 1) * 512],
                             start=(kt == 0), stop=(kt == 31)),
                         waits=(w if half == 0 else ()),
                         inc=(s_ctx if half == 1 else None))
            if hg >= 1 and (hg - 1) % NH != NH - 1:
                emit_bcast(hg - 1)

        hg_last = s * NH + NH - 1
        emit_den(hg_last)
        emit_recip(hg_last)
        emit_bcast(hg_last)
        emit_bcopy(hg_last)
        emit_back(hg_last - 1)
        emit_back(hg_last)

        while pend_spills:
            emit_spill()
        if qb >= 4:
            n_cc_slots[0] += 1
            cj = qb - 4
            P.op("gpsimd", lambda e, cj=cj: e.collective_compute(
                "ReduceScatter", ALU.add, ins=[cc_in[cj][:]],
                outs=[cc_out[cj][:]], replica_groups=GROUPS),
                waits=[(s_ccd, n_cc_slots[0] * NH * 32)], inc=s_cc)
        else:
            j = qb
            equads = [(ps_a, 0, 256), (ps_a, 512, 256), (ps_b, 0, 256),
                      (ps_b, 512, 256)]
            oquads = [(ps_c0, 0, 256), (ps_c0, 256, 256), (ps_c1, 0, 256),
                      (ps_c1, 256, 256)]
            slot_exp = (s + 1) * NH * 16
            slot_ce = (s + 1) * NH
            oe_slot0 = s_oe.n
            use_spill = s >= 4
            ow_sems = [s_wp[0], s_wp[1], s_row]
            for ob in range(16):
                quads = equads if ob % 2 == 0 else oquads
                hb = {}
                hv = {}
                for half in range(2):
                    oi = ow_state["i"]
                    ow_state["i"] += 1
                    bi = oi % 3
                    hb[half] = bi
                    w = []
                    if bi in ow_state["free"]:
                        w.append(ow_state["free"][bi])
                    if use_spill and ob < 2 and half == 0:
                        w.append((s_vsp[0], s_vsp[0].n))
                        w.append((s_vsp[1], s_vsp[1].n))
                    osrc = ow_bf_s[ob] if use_spill else ow_h[ob]
                    P.op("gpsimd", lambda e, osrc=osrc, bi=bi, half=half:
                         e.dma_start(out=ow_cb[bi][:],
                                     in_=osrc[:, half * 4096:(half + 1) * 4096]),
                         waits=w, inc16=ow_sems[bi])
                    hv[half] = ow_sems[bi].n
                ow_ld += 1
                for tt in range(4):
                    w = []
                    if len(oe_vals) >= 4:
                        w.append((s_oe, oe_vals[len(oe_vals) - 4 + tt]))
                    P.op("sync", lambda e, qb=qb, tt=tt, ob=ob:
                         e.dma_start(
                             out=xsl[tt][:],
                             in_=x_l[qb * 512 + tt * 128:qb * 512 + (tt + 1) * 128,
                                     ob * 256:(ob + 1) * 256]),
                         waits=w, inc16=s_xsl)
                xslv = s_xsl.n
                w = [(ow_sems[hb[0]], hv[0]), (ow_sems[hb[1]], hv[1]),
                     (s_exp, slot_exp), (s_ce, slot_ce), (s_rx, rx_v)]
                if ob >= 2:
                    w.append((s_oe, oe_vals[-5]))
                for cf in range(DC):
                    src_ = ctxT if cf < 16 else rx
                    hh = cf % 16
                    bi = hb[cf // 16]
                    for tt in range(4):
                        pq, off, _qw = quads[tt]
                        P.op("tensor", lambda e, src_=src_, hh=hh, tt=tt, cf=cf,
                             pq=pq, off=off, bi=bi: e.matmul(
                                 pq[:, off:off + 256],
                                 lhsT=src_[:, hh, tt * 128:(tt + 1) * 128],
                                 rhs=ow_cb[bi][:, (cf % 16) * 256:
                                               (cf % 16 + 1) * 256],
                                 start=(cf == 0), stop=(cf == DC - 1)),
                             waits=(w if (cf == 0 and tt == 0) else ()),
                             inc=(s_op if (tt == 3 and cf in (15, DC - 1))
                                  else None))
                    if cf == 15:
                        ow_state["free"][hb[0]] = (s_op, s_op.n)
                op_group_v.append(s_op.n)
                ow_state["free"][hb[1]] = (s_op, s_op.n)
                for tt in range(4):
                    gi = ob * 4 + tt
                    pq, off, _qw = quads[tt]
                    P.op("vector", lambda e, pq=pq, off=off, ob=ob: e.tensor_tensor(
                        t1sl[:], pq[:, off:off + 256],
                        ob_bc[:, ob * 256:(ob + 1) * 256], op=ALU.add),
                        waits=[(s_op, s_op.n), (s_xsl, xslv)])
                    ii = gi % 2
                    w2 = [(s_x2p[x2_idx % 2], x2_store_v[x2_idx - 2])] \
                        if x2_idx >= 2 else []
                    P.op("vector", lambda e, ii=ii, tt=tt: e.tensor_tensor(
                        x2st[ii][:], t1sl[:], xsl[tt][:], op=ALU.add), waits=w2)
                    P.op("vector", lambda e, ii=ii, tt=tt, ob=ob: e.bn_stats(
                        stats2[:, tt, ob * 6:(ob + 1) * 6], x2st[ii][:]),
                        inc=s_oe)
                    oe_vals.append(s_oe.n)
                    P.op("sync", lambda e, ii=ii, qb=qb, tt=tt, ob=ob: e.dma_start(
                        out=x2_s[qb * 512 + tt * 128:qb * 512 + (tt + 1) * 128,
                                 ob * 256:(ob + 1) * 256],
                        in_=x2st[ii][:]),
                        waits=[(s_oe, s_oe.n)], inc16=s_x2p[x2_idx % 2])
                    x2_store_v.append(s_x2p[x2_idx % 2].n)
                    x2_idx += 1
            x2_stores[qb] = (s_x2p[0].n, s_x2p[1].n)
            oe_slot_end[s] = s_oe.n
            prev_op_end = s_oe.n
            if s + 1 < NSL:
                hgn = (s + 1) * NH
                kv_pref[hgn] = emit_kv_load(0, hgn, hgn % 2, QORDER[s + 1])
            for tt in range(4):
                pp = ln2_gt % 2
                w = [(s_oe, s_oe.n)]
                if ln2_gt >= 2:
                    w.append((s_sc, ln2_ln_v[ln2_gt - 2]))
                P.op("vector", lambda e, tt=tt, pp=pp: e.bn_aggr(
                    mv2[pp][:], stats2[:, tt, :]), waits=w, inc=s_ln)
                P.op("scalar", lambda e, pp=pp: e.activation(
                    out=lnt2[:], in_=mv2[pp][:, 1:2], func=AF.Ln,
                    bias=eps_sb[:, 0:1]),
                    waits=[(s_ln, s_ln.n)], inc=s_sc)
                ln2_ln_v[ln2_gt] = s_sc.n
                w = [(s_sc, s_sc.n)]
                if ln2_gt >= 2:
                    w.append((s_sc, ln2_apply_v[8 * (ln2_gt - 2) + 7]))
                P.op("scalar", lambda e, pp=pp: e.activation(
                    out=rstd2[pp][:], in_=lnt2[:], func=AF.Exp, scale=-0.5),
                    waits=w, inc=s_sc)
                P.op("vector", lambda e, pp=pp: e.scalar_tensor_tensor(
                    out=nmr2[pp][:], in0=mv2[pp][:, 0:1], scalar=-1.0,
                    in1=rstd2[pp][:], op0=ALU.mult, op1=ALU.mult),
                    waits=[(s_sc, s_sc.n)], inc=s_ln)
                nv = s_ln.n
                ln2_gt += 1
                for hf in range(8):
                    k = len(ln2_apply_v)
                    ii = k % 3
                    lb = k % 4
                    w = [(s_x2p[0], x2_stores[qb][0]), (s_x2p[1], x2_stores[qb][1])]
                    if k >= 3:
                        w.append((s_sc, ln2_apply_v[k - 3]))
                    P.op(["sync", "gpsimd"][k % 2],
                         lambda e, qb=qb, tt=tt, hf=hf, ii=ii: e.dma_start(
                             out=x2rd[ii][:],
                             in_=x2_s[qb * 512 + tt * 128:qb * 512 + (tt + 1) * 128,
                                      hf * 512:(hf + 1) * 512]),
                         waits=w, inc16=s_xr[ii])
                    xv = s_xr[ii].n
                    lw = [(s_ln, nv), (s_xr[ii], xv)]
                    if k >= 4:
                        lw.append((s_lh[lb], ln2_st_v[k - 4]))
                    P.op("scalar", lambda e, ii=ii, pp=pp, lb=lb: e.activation(
                        out=ln2h[lb][:], in_=x2rd[ii][:], func=AF.Identity,
                        bias=nmr2[pp][:, 0:1], scale=rstd2[pp][:, 0:1]),
                        waits=lw, inc=s_sc)
                    ln2_apply_v.append(s_sc.n)
                    P.op("scalar",
                         lambda e, qb=qb, tt=tt, hf=hf, lb=lb: e.dma_start(
                             out=ln2_s[qb * 512 + tt * 128:qb * 512 + (tt + 1) * 128,
                                       hf * 512:(hf + 1) * 512],
                             in_=ln2h[lb][:]),
                         inc16=s_lh[lb])
                    ln2_st_v.append(s_lh[lb].n)
            l2_stores[qb] = tuple(sem.n for sem in s_lh)

    P.op("sync", lambda e: e.nop(),
         waits=[(sem, sem.n) for sem in s_lh]
               + [(s_x2p[0], s_x2p[0].n), (s_x2p[1], s_x2p[1].n)], inc=s_bar)
    P.op("tensor", lambda e: e.nop(), waits=[(s_op, s_op.n)], inc=s_bar)
    P.op("scalar", lambda e: e.nop(), waits=[(s_bcp, s_bcp.n)], inc=s_bar)
    P.op("vector", lambda e: e.nop(), waits=[(s_ln, s_ln.n)], inc=s_bar)
    P.op("gpsimd", lambda e: e.nop(), waits=[(s_cc, 4)], inc=s_bar)
    bar2 = s_bar.n
    p2.close()

    p3 = contextlib.ExitStack()
    ln2T = sbuf("ln2T", [128, DC, 1024], BF16, p3)
    ln2rd = [sbuf(f"ln2rd{i}", [128, H], BF16, p3) for i in range(2)]
    WT = [sbuf(f"WT{i}", [128, DC, OB3], BF16, p3) for i in range(2)]
    wrow = [sbuf(f"wrow{i}", [128, 2048], F32, p3) for i in range(4)]
    x2sl = [sbuf(f"x2sl{i}", [128, OB3], F32, p3) for i in range(2)]
    outst = [sbuf(f"outst{i}", [128, OB3], F32, p3) for i in range(2)]
    t1sl3 = sbuf("t1sl3", [128, OB3], F32, p3)
    b_bc = sbuf("b_bc", [128, H], BF16, p3)
    a_bf = sbuf("a_bf", [64, H], BF16, p3)
    AT_sb = sbuf("AT_sb", [128, DC, 64], BF16, p3)
    b2_bf = sbuf("b2_bf", [128, DC * 64], BF16, p3)
    BT_sb = sbuf("BT_sb", [64, H], BF16, p3)
    gw_bf = sbuf("gw_bf", [4, H], BF16, p3)
    gwT_sb = sbuf("gwT_sb", [128, DC, 4], BF16, p3)
    h1T = sbuf("h1T", [64, 1024], BF16, p3)
    mT_sb = sbuf("mT_sb", [64, 1024], BF16, p3)
    m_sb = sbuf("m_sb", [64, 1024], BF16, p3)
    g_f32 = sbuf("g_f32", [4, 512], F32, p3)
    gn_sb = sbuf("gn_sb", [128, 16], F32, p3)
    gm1 = sbuf("gm1", [128, 4], F32, p3)
    ga = sbuf("ga", [128, 1], F32, p3)
    maskn = sbuf("maskn", [128, 64], BF16, p3)

    for e_ in ("sync", "tensor", "scalar", "vector", "gpsimd"):
        P.wait(e_, [(s_bar, bar2)])

    emit_row_bcast(b_row, H, b_bc)

    P.op("gpsimd", lambda e: e.dma_start(out=a_bf[:], in_=A_in[:]), inc16=s_wr)
    P.op("gpsimd", lambda e: e.dma_start(out=b2_bf[:], in_=B_in[:]), inc16=s_wr)
    P.op("gpsimd", lambda e: e.dma_start(out=gw_bf[:], in_=gw_in[:]), inc16=s_wr)
    pre_w = s_wr.n

    for grp in range(2):
        for k in range(16):
            c = grp * 16 + k
            w = [(s_wr, pre_w)]
            if k == 0:
                w.append((s_te, s_te.n))
            P.op("tensor", lambda e, c=c, k=k: e.transpose(
                ps_t[:, k * 64:(k + 1) * 64],
                in_=a_bf[0:64, c * 128:(c + 1) * 128],
                identity=identb[0:64, 0:64]),
                waits=w, inc=(s_tp if k == 15 else None))
        P.op("scalar", lambda e, grp=grp: e.copy(
            out=AT_sb[:, grp * 16:(grp + 1) * 16, :],
            in_=ps_t[:].rearrange("p (c f) -> p c f", c=16)),
            waits=[(s_tp, s_tp.n)], inc=s_te)
    for grp in range(4):
        for k in range(8):
            c = grp * 8 + k
            w = [(s_wr, pre_w)]
            if k == 0:
                w.append((s_te, s_te.n))
            P.op("tensor", lambda e, c=c, k=k: e.transpose(
                ps_t[0:64, k * 128:(k + 1) * 128],
                in_=b2_bf[:, c * 64:(c + 1) * 64], identity=identb[:]),
                waits=w, inc=(s_tp if k == 7 else None))
        P.op("scalar", lambda e, grp=grp: e.copy(
            out=BT_sb[0:64, grp * 1024:(grp + 1) * 1024], in_=ps_t[0:64, :]),
            waits=[(s_tp, s_tp.n)], inc=s_te)
    for k in range(DC):
        w = [(s_wr, pre_w)]
        if k == 0:
            w.append((s_te, s_te.n))
        P.op("tensor", lambda e, k=k: e.transpose(
            ps_t[:, k * 4:(k + 1) * 4], in_=gw_bf[0:4, k * 128:(k + 1) * 128],
            identity=identb[0:4, 0:4]),
            waits=w, inc=(s_tp if k == DC - 1 else None))
    P.op("scalar", lambda e: e.copy(
        out=gwT_sb[:], in_=ps_t[:, 0:128].rearrange("p (c f) -> p c f", c=32)),
        waits=[(s_tp, s_tp.n)], inc=s_te)

    ffn_grp = 0
    wtb_state = {"g": 0, "bcp": {}}
    wrow_half_tp = {}
    nonlocal_state = {"ffn_grp": 0, "out_ct": 0}
    pend_mm = None
    out_store_v = []
    g_tp_v = {}
    gn_free_v = {}
    ffn_grp_mm = []
    wrow_tp = {}
    out_ct = 0
    ln2rd_tp = {}

    for th in range(2):
        tbase = th * 1024
        if pend_mm is not None:
            pend_mm()
            pend_mm = None
        for tt in range(8):
            gti = th * 8 + tt
            row = tbase + tt * 128
            qb_src = row // 512
            w = [(s_lh[i_], l2_stores[qb_src][i_]) for i_ in range(4)]
            if gti >= 2:
                w.append((s_tp, ln2rd_tp[gti - 2]))
            P.op("sync", lambda e, row=row, i=tt % 2: e.dma_start(
                out=ln2rd[i][:], in_=ln2_s[row:row + 128, :]),
                waits=w, inc16=s_xh[tt % 2])
            xv = s_xh[tt % 2].n
            for grp in range(4):
                for k in range(8):
                    c = grp * 8 + k
                    w2 = [(s_xh[tt % 2], xv)]
                    if k == 0:
                        w2.append((s_te, s_te.n))
                    P.op("tensor", lambda e, c=c, k=k, i=tt % 2: e.transpose(
                        ps_t[:, k * 128:(k + 1) * 128],
                        in_=ln2rd[i][:, c * 128:(c + 1) * 128],
                        identity=identb[:]),
                        waits=w2, inc=(s_tp if k == 7 else None))
                P.op("scalar", lambda e, grp=grp, tt=tt: e.copy(
                    out=ln2T[:, grp * 8:(grp + 1) * 8, tt * 128:(tt + 1) * 128],
                    in_=ps_t[:].rearrange("p (c f) -> p c f", c=8)),
                    waits=[(s_tp, s_tp.n)], inc=s_te)
            ln2rd_tp[gti] = s_tp.n
        ln2T_done = s_te.n

        for tq in range(2):
            gtq = th * 2 + tq
            gate_ps_free = [(s_ev, s_ev.n)] if (th > 0 or tq > 0) else []
            if th > 0 and tq == 0:
                gate_ps_free.append((s_bcp, s_bcp.n))
            for c in range(DC):
                w = [(s_te, ln2T_done)] + (gate_ps_free if c == 0 else [])
                P.op("tensor", lambda e, c=c, tq=tq: e.matmul(
                    ps_c0[0:4, :], lhsT=gwT_sb[:, c, :],
                    rhs=ln2T[:, c, tq * 512:(tq + 1) * 512],
                    start=(c == 0), stop=(c == DC - 1)),
                    waits=(w if c == 0 else ()),
                    inc=(s_mm if c == DC - 1 else None))
            w = [(s_mm, s_mm.n)]
            if gtq >= 1:
                w.append((s_tp, g_tp_v[gtq - 1]))
            P.op("vector", lambda e: e.tensor_scalar(
                out=g_f32[:], in0=ps_c0[0:4, :], scalar1=gb_sb[0:4, 0:1],
                scalar2=None, op0=ALU.add),
                waits=w, inc=s_ev)
            gev = s_ev.n
            for k in range(4):
                w = [(s_ev, gev), (s_bcp, s_bcp.n)]
                if k > 0:
                    w = []
                P.op("tensor", lambda e, k=k: e.transpose(
                    ps_d[:, k * 4:(k + 1) * 4],
                    in_=g_f32[0:4, k * 128:(k + 1) * 128],
                    identity=identf[0:4, 0:4]),
                    waits=w, inc=(s_tp if k == 3 else None))
            g_tp_v[gtq] = s_tp.n
            w = [(s_tp, s_tp.n)]
            if gtq >= 1:
                w.append((s_ln, gn_free_v[gtq - 1]))
            P.op("scalar", lambda e: e.copy(out=gn_sb[:], in_=ps_d[:, 0:16]),
                 waits=w, inc=s_bcp)
            gnv = s_bcp.n
            for k in range(4):
                base = k * 4
                P.op("vector", lambda e, base=base: e.reduce_max(
                    gm1[:, 0:1], gn_sb[:, base + 1:base + 4], axis=AX.X),
                    waits=[(s_bcp, gnv)])
                P.op("vector", lambda e, base=base: e.reduce_max(
                    gm1[:, 1:2], gn_sb[:, base + 2:base + 4], axis=AX.X))
                P.op("vector", lambda e, base=base: e.reduce_max(
                    gm1[:, 2:3], gn_sb[:, base + 0:base + 2], axis=AX.X))
                P.op("vector", lambda e, base=base: e.reduce_max(
                    gm1[:, 3:4], gn_sb[:, base + 0:base + 3], axis=AX.X),
                    inc=s_ln)
                mxv = s_ln.n
                P.op("vector", lambda e, base=base: e.tensor_tensor(
                    ga[:], gm1[:, 0:1], gn_sb[:, base:base + 1], op=ALU.is_gt),
                    waits=[(s_ln, mxv)])
                P.op("vector", lambda e: e.tensor_tensor(
                    gm1[:, 0:1], gm1[:, 1:2], gm1[:, 2:3], op=ALU.is_gt))
                P.op("vector", lambda e, base=base: e.tensor_tensor(
                    gm1[:, 1:2], gn_sb[:, base + 3:base + 4], gm1[:, 3:4],
                    op=ALU.is_gt), inc=s_ln)
                cv = s_ln.n
                gn_free_v[gtq] = cv
                P.op("vector", lambda e: e.tensor_tensor(
                    gm1[:, 2:3], gm1[:, 0:1], gm1[:, 1:2], op=ALU.add),
                    waits=[(s_ln, cv)], inc=s_ln)
                sumv = s_ln.n
                P.op("vector", lambda e: e.tensor_tensor(
                    ga[:], ga[:], gm1[:, 2:3], op=ALU.add),
                    waits=[(s_ln, sumv)], inc=s_ln)
                av = s_ln.n
                P.op("vector", lambda e: e.tensor_scalar(
                    out=maskn[:], in0=levels[:], scalar1=ga[:, 0:1],
                    scalar2=None, op0=ALU.is_le),
                    waits=[(s_ln, av), (s_te, s_te.n)], inc=s_ln)
                mk = s_ln.n
                P.op("tensor", lambda e: e.transpose(
                    ps_t[0:64, 0:128], in_=maskn[:], identity=identb[:]),
                    waits=[(s_ln, mk), (s_te, s_te.n)], inc=s_tp)
                P.op("scalar", lambda e, tq=tq, k=k: e.copy(
                    out=mT_sb[0:64, (tq * 4 + k) * 128:(tq * 4 + k + 1) * 128],
                    in_=ps_t[0:64, 0:128]),
                    waits=[(s_tp, s_tp.n)], inc=s_te)
            h1_ps_free = [(s_ev, s_ev.n)] if (th > 0 or tq > 0) else []
            if th > 0 and tq == 0:
                h1_ps_free.append((s_bcp, s_bcp.n))
            for c in range(DC):
                w = [(s_te, ln2T_done)] + (h1_ps_free if c == 0 else [])
                P.op("tensor", lambda e, c=c, tq=tq: e.matmul(
                    ps_c1[0:64, :], lhsT=AT_sb[:, c, :],
                    rhs=ln2T[:, c, tq * 512:(tq + 1) * 512],
                    start=(c == 0), stop=(c == DC - 1)),
                    waits=(w if c == 0 else ()),
                    inc=(s_mm if c == DC - 1 else None))
            P.op("scalar", lambda e, tq=tq: e.copy(
                out=h1T[0:64, tq * 512:(tq + 1) * 512], in_=ps_c1[0:64, :]),
                waits=[(s_mm, s_mm.n)], inc=s_ev)
        P.op("vector", lambda e: e.tensor_tensor(
            m_sb[0:64, :], h1T[0:64, :], mT_sb[0:64, :], op=ALU.mult),
            waits=[(s_ev, s_ev.n), (s_te, s_te.n)], inc=s_ln)
        msk_done = s_ln.n

        for ob in range(NOB3):
            gob = th * NOB3 + ob
            wldv = {}
            for li in range(4):
                osb, dhalf = li // 2, li % 2
                gli = gob * 4 + li
                w = []
                if gli >= 4:
                    w.append((s_tp, wrow_half_tp[gli - 4]))
                P.op("sync", lambda e, ob=ob, osb=osb, dhalf=dhalf, li=li:
                     e.dma_start(
                         out=wrow[li][:],
                         in_=Wn[ob * 256 + osb * 128:ob * 256 + (osb + 1) * 128,
                                dhalf * 2048:(dhalf + 1) * 2048]),
                     waits=w, inc16=s_wr)
                wldv[li] = s_wr.n
            wti = gob % 2
            wtbanks = [ps_d, ps_c0, ps_c1]
            for osb in range(2):
                for grp in range(8):
                    wg = wtb_state["g"]
                    wtb_state["g"] += 1
                    bank = wtbanks[wg % 3]
                    li = osb * 2 + grp // 4
                    for k in range(4):
                        c = grp * 4 + k
                        w = [(s_wr, wldv[li])]
                        if k == 0 and wg >= 3:
                            w.append((s_bcp, wtb_state["bcp"][wg - 3]))
                        if k == 0 and (wg % (NOB3 * 16)) < 3:
                            w.append((s_ev, s_ev.n))
                            w.append((s_bcp, s_bcp.n))
                        if k == 0 and grp == 0 and osb == 0 and gob >= 2:
                            w.append((s_mm, ffn_grp_mm[(gob - 2) * 8 + 7]))
                        P.op("tensor", lambda e, c=c, k=k, li=li, bank=bank:
                             e.transpose(
                                 bank[:, k * 128:(k + 1) * 128],
                                 in_=wrow[li][:, (c % 16) * 128:
                                              (c % 16 + 1) * 128],
                                 identity=identf[:]),
                             waits=w, inc=(s_tp if k == 3 else None))
                    P.op("scalar", lambda e, grp=grp, osb=osb, wti=wti, bank=bank:
                         e.copy(
                             out=WT[wti][:, grp * 4:(grp + 1) * 4,
                                         osb * 128:(osb + 1) * 128],
                             in_=bank[:].rearrange("p (c f) -> p c f", c=4)),
                         waits=[(s_tp, s_tp.n)], inc=s_bcp)
                    wtb_state["bcp"][wg] = s_bcp.n
                    if grp % 4 == 3:
                        wrow_half_tp[gob * 4 + osb * 2 + grp // 4] = s_tp.n
            wrow_tp[gob] = s_tp.n
            wt_done = s_bcp.n

            def mk_mm(ob, tbase, wt_done, wti, msk_done):
                def go():
                    global_unused = None
                    for tt in range(8):
                        nonlocal_state["ffn_grp"] += 1
                        fg = nonlocal_state["ffn_grp"] - 1
                        pq = [ps_a, ps_b][fg % 2]
                        w = [(s_bcp, wt_done), (s_ln, msk_done)]
                        if fg >= 2:
                            w.append((s_oe, oe_vals[-2]))
                        for c in range(DC):
                            P.op("tensor", lambda e, c=c, tt=tt, pq=pq, wti=wti:
                                 e.matmul(pq[:, 0:OB3],
                                          lhsT=ln2T[:, c, tt * 128:(tt + 1) * 128],
                                          rhs=WT[wti][:, c, :],
                                          start=(c == 0), stop=False),
                                 waits=(w if c == 0 else ()))
                        P.op("tensor", lambda e, tt=tt, ob=ob, pq=pq: e.matmul(
                            pq[:, 0:OB3],
                            lhsT=m_sb[0:64, tt * 128:(tt + 1) * 128],
                            rhs=BT_sb[0:64, ob * 256:(ob + 1) * 256],
                            start=False, stop=True), inc=s_mm)
                        mmv = s_mm.n
                        ffn_grp_mm.append(mmv)
                        ii = (fg + 1) % 2
                        qb_src = (tbase + tt * 128) // 512
                        w2 = [(s_x2p[0], x2_stores[qb_src][0]),
                              (s_x2p[1], x2_stores[qb_src][1])]
                        if fg >= 2:
                            w2.append((s_oe, oe_vals[-2]))
                        P.op("sync", lambda e, tbase=tbase, tt=tt, ob=ob, ii=ii:
                             e.dma_start(
                                 out=x2sl[ii][:],
                                 in_=x2_s[tbase + tt * 128:tbase + (tt + 1) * 128,
                                          ob * 256:(ob + 1) * 256]),
                             waits=w2, inc16=s_wp[ii])
                        xv2 = s_wp[ii].n
                        P.op("vector", lambda e, pq=pq, ob=ob, ii=ii:
                             e.tensor_tensor(
                                 t1sl3[:], pq[:, 0:OB3],
                                 b_bc[:, ob * 256:(ob + 1) * 256],
                                 op=ALU.add), waits=[(s_mm, mmv), (s_wp[ii], xv2)])
                        oc = nonlocal_state["out_ct"]
                        w3 = [(s_stp[oc % 2], out_store_v[oc - 2])] \
                            if oc >= 2 else []
                        P.op("vector", lambda e, ii=ii: e.tensor_tensor(
                            outst[ii][:], t1sl3[:], x2sl[ii][:], op=ALU.add),
                            waits=w3, inc=s_oe)
                        oe_vals.append(s_oe.n)
                        P.op("sync", lambda e, tbase=tbase, tt=tt, ob=ob, ii=ii:
                             e.dma_start(
                                 out=out_d[tbase + tt * 128:tbase + (tt + 1) * 128,
                                           ob * 256:(ob + 1) * 256],
                                 in_=outst[ii][:]),
                             waits=[(s_oe, s_oe.n)], inc16=s_stp[oc % 2])
                        out_store_v.append(s_stp[oc % 2].n)
                        nonlocal_state["out_ct"] += 1
                return go

            new_mm = mk_mm(ob, tbase, wt_done, wti, msk_done)
            if pend_mm is not None:
                pend_mm()
            pend_mm = new_mm
    if pend_mm is not None:
        pend_mm()
        pend_mm = None

    P.wait("sync", [(s_stp[0], s_stp[0].n), (s_stp[1], s_stp[1].n)])

    _check_deadlock(P)

    with nc.Block() as blk:
        for eng_name in ("sync", "gpsimd", "tensor", "scalar", "vector"):
            items = [o for o in P.ops if o[0] == eng_name]

            def make_body(items):
                def body(eng):
                    for _, waits, fn, incs in items:
                        for (hh, v) in waits:
                            eng.wait_ge(hh, v)
                        if fn is None:
                            continue
                        ins = fn(eng)
                        for (hh, amt) in incs:
                            ins = ins.then_inc(hh, amt)
                return body

            getattr(blk, eng_name)(make_body(items))

    p3.close()
    es.close()
    return nc


_CACHE = {}


def _get_program():
    if "nc" not in _CACHE:
        _CACHE["nc"] = build_program()
    return _CACHE["nc"]


def _prep_inputs(c, inputs):
    bidx, gg = c // 2, c % 2
    f32 = np.float32
    x = np.asarray(inputs["L0_attn_x"], f32)
    qkvw = np.asarray(inputs["L0_attn_attn_qkvw"], f32)
    qkvb = np.asarray(inputs["L0_attn_attn_qkvb"], f32)
    ow = np.asarray(inputs["L0_attn_attn_ow"], f32)
    obv = np.asarray(inputs["L0_attn_attn_ob"], f32)
    W = np.asarray(inputs["L0_ffn_W"], f32)
    bv = np.asarray(inputs["L0_ffn_b"], f32)
    A = np.asarray(inputs["L0_ffn_A"], f32)
    Bm = np.asarray(inputs["L0_ffn_B"], f32)
    gw = np.asarray(inputs["L0_ffn_gate_w"], f32)
    gb = np.asarray(inputs["L0_ffn_gate_b"], f32)

    own = slice(2048 * gg, 2048 * gg + 2048)
    oth = slice(2048 * (1 - gg), 2048 * (1 - gg) + 2048)
    x_loc = np.ascontiguousarray(np.concatenate([x[bidx, own], x[bidx, oth]], 0))

    qcols = qkvw[:, 2048 * gg:2048 * gg + 2048]
    kcols = qkvw[:, 4096 + 2048 * gg:4096 + 2048 * gg + 2048]
    qk = np.concatenate([qcols, kcols], 1)
    wqk_h = np.ascontiguousarray(
        qk.reshape(32, 128, 32, 128).transpose(2, 1, 0, 3).reshape(32, 128, -1))
    vcols = qkvw[:, 8192 + 2048 * gg:8192 + 2048 * gg + 2048]
    wv_h = np.ascontiguousarray(
        vcols.reshape(32, 128, 8, 256).transpose(2, 1, 0, 3).reshape(8, 128, -1))
    qkvb_qk = np.ascontiguousarray(np.concatenate(
        [qkvb[2048 * gg:2048 * gg + 2048],
         qkvb[4096 + 2048 * gg:4096 + 2048 * gg + 2048]]).reshape(32, 128).T)
    vb = np.ascontiguousarray(qkvb[8192 + 2048 * gg:8192 + 2048 * gg + 2048])
    ow_p = np.concatenate([ow[2048 * gg:2048 * gg + 2048],
                           ow[2048 * (1 - gg):2048 * (1 - gg) + 2048]], 0)
    ow_hh = np.ascontiguousarray(
        ow_p.reshape(32, 128, 16, 256).transpose(2, 1, 0, 3).reshape(16, 128, -1))
    B_h = np.ascontiguousarray(
        Bm.reshape(32, 128, 64).transpose(1, 0, 2).reshape(128, -1))

    bf = ml_dtypes.bfloat16
    lev = np.array([0] * 8 + [1] * 8 + [2] * 16 + [3] * 32, f32)
    return {
        "x_l": x_loc, "wqk": wqk_h, "wv": wv_h, "qkvb_qk": qkvb_qk,
        "vb_row": vb, "ow_h": ow_hh, "ob_row": obv, "Wn": W, "b_row": bv,
        "A_in": A, "B_in": B_h, "gw_in": gw, "gb_in": gb,
        "identb_in": np.eye(128, dtype=bf),
        "identf_in": np.eye(128, dtype=f32),
        "onesf_in": np.ones((1, 128), f32),
        "onesb_in": np.ones((128, 1), bf),
        "levels_in": np.tile(lev, (128, 1)),
        "masks_in": np.tile(np.array([0.0, 1.0] if gg == 0 else [1.0, 0.0], f32),
                            (128, 1)),
        "eps_in": np.full((128, 1), 1e-5, f32),
    }


def kernel(**inputs):
    nc = _get_program()
    in_maps = [_prep_inputs(c, inputs) for c in range(8)]
    res = run_bass_kernel_spmd(nc, in_maps, core_ids=list(range(8)))
    out = np.empty((4, 4096, 4096), np.float32)
    for c in range(8):
        bidx, gg = c // 2, c % 2
        out[bidx, 2048 * gg:2048 * gg + 2048] = res.results[c]["out"]
    return out


# version 23
# speedup vs baseline: 1.1971x; 1.0529x over previous
import sys

for _p in ("/opt/trn_rl_repo", "/root/.axon_site/_ro/trn_rl_repo"):
    if _p not in sys.path:
        sys.path.append(_p)

import contextlib

import numpy as np
import ml_dtypes

import concourse.bass as bass
import concourse.mybir as mybir
from concourse.bass_utils import run_bass_kernel_spmd

F32 = mybir.dt.float32
BF16 = mybir.dt.bfloat16
AF = mybir.ActivationFunctionType
ALU = mybir.AluOpType
AX = mybir.AxisListType

H = 4096
DC = 32
NH = 16
SL = 4096
TOWN = 2048
QB = 512
NSL = 8
QORDER = [4, 0, 5, 1, 6, 2, 7, 3]
SCALE = float(1.0 / np.sqrt(128.0))
EPS = 1e-5
GROUPS = [[0, 1], [2, 3], [4, 5], [6, 7]]
OB3 = 256
NOB3 = H // OB3


class Sem:
    def __init__(self, handle):
        self.h = handle
        self.n = 0


class Prog:

    def __init__(self):
        self.ops = []

    def op(self, engine, fn, waits=(), inc=None, inc16=None):
        incs = []
        if inc is not None:
            inc.n += 1
            incs.append((inc.h, 1))
        if inc16 is not None:
            inc16.n += 16
            incs.append((inc16.h, 16))
        self.ops.append((engine, tuple((s.h, v) for (s, v) in waits), fn, incs))

    def wait(self, engine, waits):
        self.ops.append((engine, tuple((s.h, v) for (s, v) in waits), None, []))


def _check_deadlock(prog):
    qs = {}
    for it in prog.ops:
        qs.setdefault(it[0], []).append(it)
    heads = {k: 0 for k in qs}
    counts = {}
    progress = True
    while progress:
        progress = False
        for eng, lst in qs.items():
            while heads[eng] < len(lst):
                _, waits, _, incs = lst[heads[eng]]
                if all(counts.get(h, 0) >= v for (h, v) in waits):
                    for (h, amt) in incs:
                        counts[h] = counts.get(h, 0) + amt
                    heads[eng] += 1
                    progress = True
                else:
                    break
    stuck = {e: (heads[e], len(l)) for e, l in qs.items() if heads[e] < len(l)}
    if stuck:
        lines = []
        for e, (i, n) in stuck.items():
            _, waits, _, _ = qs[e][i]
            lines.append(f"{e}: op {i}/{n} waits " + str(
                [(getattr(h, "name", h), v, counts.get(h, 0)) for h, v in waits]))
        raise RuntimeError("sync deadlock:\n" + "\n".join(lines))


def build_program():
    nc = bass.Bass("TRN2", target_bir_lowering=False)
    es = contextlib.ExitStack()

    def dram(name, shape, dt, kind=None):
        if kind is None:
            return nc.dram_tensor(name, shape, dt).ap()
        return nc.dram_tensor(name, shape, dt, kind=kind).ap()

    x_l = dram("x_l", [SL, H], F32, "ExternalInput")
    wqk = dram("wqk", [32, 128, DC * 128], F32, "ExternalInput")
    wv = dram("wv", [8, 128, DC * 256], F32, "ExternalInput")
    qkvb_qk = dram("qkvb_qk", [128, 32], F32, "ExternalInput")
    vb_row = dram("vb_row", [2048], F32, "ExternalInput")
    ow_h = dram("ow_h", [16, 128, DC * 256], F32, "ExternalInput")
    ob_row = dram("ob_row", [H], F32, "ExternalInput")
    Wn = dram("Wn", [H, H], F32, "ExternalInput")
    b_row = dram("b_row", [H], F32, "ExternalInput")
    A_in = dram("A_in", [64, H], F32, "ExternalInput")
    B_in = dram("B_in", [128, DC * 64], F32, "ExternalInput")
    gw_in = dram("gw_in", [4, H], F32, "ExternalInput")
    gb_in = dram("gb_in", [4], F32, "ExternalInput")
    identb_in = dram("identb_in", [128, 128], BF16, "ExternalInput")
    identf_in = dram("identf_in", [128, 128], F32, "ExternalInput")
    onesf_in = dram("onesf_in", [1, 128], F32, "ExternalInput")
    onesb_in = dram("onesb_in", [128, 1], BF16, "ExternalInput")
    levels_in = dram("levels_in", [128, 64], F32, "ExternalInput")
    masks_in = dram("masks_in", [128, 2], F32, "ExternalInput")
    eps_in = dram("eps_in", [128, 1], F32, "ExternalInput")
    out_d = dram("out", [TOWN, H], F32, "ExternalOutput")

    qT_s = dram("qT_s", [NH, 128, SL], BF16)
    kT_s = dram("kT_s", [NH, 128, SL], BF16)
    v_s = dram("v_s", [NH, 128, DC, 128], BF16)
    cc_in = [dram(f"cc_in{j}", [2 * TOWN, QB], BF16) for j in range(4)]
    cc_out = [dram(f"cc_out{j}", [TOWN, QB], BF16) for j in range(4)]
    ow_bf_s = dram("ow_bf_s", [16, 128, DC * 256], BF16)
    x2_s = dram("x2_s", [TOWN, H], F32)
    ln2_s = dram("ln2_s", [TOWN, H], BF16)

    def sbuf(name, shape, dt, scope=es):
        return scope.enter_context(nc.sbuf_tensor(name, shape, dt))

    def psum(name, shape, dt):
        return es.enter_context(nc.psum_tensor(name, shape, dt))

    identb = sbuf("identb", [128, 128], BF16)
    identf = sbuf("identf", [128, 128], F32)
    onesf = sbuf("onesf", [1, 128], F32)
    onesb = sbuf("onesb", [128, 1], BF16)
    levels = sbuf("levels", [128, 64], F32)
    masks = sbuf("masks", [128, 2], F32)
    qkvb_sb = sbuf("qkvb_sb", [128, 32], F32)
    row_sb = sbuf("row_sb", [1, 1024], F32)
    gb_sb = sbuf("gb_sb", [4, 1], F32)
    eps_sb = sbuf("eps_sb", [128, 1], F32)

    ps_a = psum("ps_a", [128, 1024], F32)
    ps_b = psum("ps_b", [128, 1024], F32)
    ps_c0 = psum("ps_c0", [128, 512], F32)
    ps_c1 = psum("ps_c1", [128, 512], F32)
    ps_d = psum("ps_d", [128, 512], F32)
    ps_t = psum("ps_t", [128, 1024], BF16)

    semnames = ["s_cst", "s_row", "s_ln", "s_sc", "s_tp", "s_te", "s_mm",
                "s_ev", "s_exp", "s_ctx", "s_dt", "s_dmm", "s_rc", "s_bc",
                "s_bcp", "s_ce", "s_ccm", "s_ccd", "s_cc", "s_rx", "s_op",
                "s_oe", "s_l2", "s_bar", "s_xsl", "s_wr",
                "s_xh0", "s_xh1", "s_wp0", "s_wp1", "s_stp0", "s_stp1",
                "s_vsp0", "s_vsp1", "s_kvp0", "s_kvp1", "s_x2p0", "s_x2p1",
                "s_xr2", "s_xr3", "s_l2b", "s_lh2", "s_lh3",
                "s_kv20", "s_kv21"]
    S = {n: Sem(es.enter_context(nc.semaphore(n))) for n in semnames}
    s_cst, s_row, s_ln, s_sc, s_tp, s_te = (S[k] for k in semnames[0:6])
    s_mm, s_ev, s_exp, s_ctx, s_dt, s_dmm = (S[k] for k in semnames[6:12])
    s_rc, s_bc, s_bcp, s_ce, s_ccm, s_ccd = (S[k] for k in semnames[12:18])
    s_cc, s_rx, s_op, s_oe, s_l2, s_bar = (S[k] for k in semnames[18:24])
    s_xsl, s_wr = S["s_xsl"], S["s_wr"]
    s_xh = [S["s_xh0"], S["s_xh1"]]
    s_xr = [S["s_xh0"], S["s_xh1"], S["s_xr2"], S["s_xr3"]]
    s_l2p = [S["s_l2"], S["s_l2b"]]
    s_lh = [S["s_l2"], S["s_l2b"], S["s_lh2"], S["s_lh3"]]
    s_kv2 = [S["s_kv20"], S["s_kv21"]]
    s_wp = [S["s_wp0"], S["s_wp1"]]
    s_stp = [S["s_stp0"], S["s_stp1"]]
    s_vsp = [S["s_vsp0"], S["s_vsp1"]]
    s_kvp = [S["s_kvp0"], S["s_kvp1"]]
    s_x2p = [S["s_x2p0"], S["s_x2p1"]]

    P = Prog()

    for dst, src in [(identb, identb_in), (identf, identf_in), (onesf, onesf_in),
                     (onesb, onesb_in), (levels, levels_in), (masks, masks_in),
                     (qkvb_sb, qkvb_qk), (eps_sb, eps_in)]:
        P.op("sync", lambda e, d=dst, s=src: e.dma_start(out=d[:], in_=s[:]),
             inc16=s_cst)
    P.op("sync", lambda e: e.dma_start(
        out=gb_sb[:, :], in_=gb_in.rearrange("(p a) -> p a", a=1)), inc16=s_cst)
    consts_done = s_cst.n

    rb_state = {"tp": {}, "te": {}, "i": 0}

    def emit_row_bcast(row_src, n, dest_sb):
        row2 = row_src.rearrange("(a f) -> a f", a=1)
        rbanks = [ps_d, ps_c0, ps_c1]
        for i in range((n + 511) // 512):
            w = min(512, n - i * 512)
            gi = rb_state["i"]
            rb_state["i"] += 1
            par = gi % 2
            bank = rbanks[gi % 3]
            ldsem = [s_row, s_cst][par]
            wv = [(s_tp, rb_state["tp"][gi - 2])] if gi >= 2 else []
            P.op("sync", lambda e, i=i, w=w, par=par: e.dma_start(
                out=row_sb[:, par * 512:par * 512 + w],
                in_=row2[0:1, i * 512:i * 512 + w]),
                waits=wv, inc16=ldsem)
            ldv = ldsem.n
            wv2 = [(ldsem, ldv), (s_cst, consts_done)]
            if gi >= 3:
                wv2.append((s_te, rb_state["te"][gi - 3]))
            P.op("tensor", lambda e, w=w, par=par, bank=bank: e.matmul(
                bank[:, 0:w], lhsT=onesf[:, :],
                rhs=row_sb[:, par * 512:par * 512 + w],
                start=True, stop=True),
                waits=wv2, inc=s_tp)
            rb_state["tp"][gi] = s_tp.n
            P.op("scalar", lambda e, i=i, w=w, bank=bank: e.copy(
                out=dest_sb[:, i * 512:i * 512 + w], in_=bank[:, 0:w]),
                waits=[(s_tp, s_tp.n)], inc=s_te)
            rb_state["te"][gi] = s_te.n

    p1 = contextlib.ExitStack()
    vb_bc = sbuf("vb_bc", [128, 2048], BF16, p1)
    emit_row_bcast(vb_row, 2048, vb_bc)
    vbbc_done = s_te.n


    ln1T = sbuf("ln1T", [128, DC, 2048], BF16, p1)
    x_sb = [sbuf(f"x_sb{i}", [128, 2048], F32, p1) for i in range(2)]
    ln1h = sbuf("ln1h", [128, 2048], BF16, p1)
    w_u = [sbuf(f"w_u{i}", [128, 8192], BF16, p1) for i in range(2)]
    qkst = [sbuf(f"qkst{i}", [128, 512], BF16, p1) for i in range(2)]
    vst = [sbuf(f"vst{i}", [128, 256], BF16, p1) for i in range(2)]
    stats1 = sbuf("stats1", [128, 48], F32, p1)
    mv1 = [sbuf(f"mv1_{i}", [128, 2], F32, p1) for i in range(2)]
    rstd1 = [sbuf(f"rstd1_{i}", [128, 1], F32, p1) for i in range(2)]
    lnt1 = sbuf("lnt1", [128, 1], F32, p1)
    nmr1 = [sbuf(f"nmr1_{i}", [128, 1], F32, p1) for i in range(2)]

    x_free = {}
    ln_v = {}
    grp_idx = 0
    grp_ev = []
    qk_store_v = []
    v_store_v = []
    wu_free = {}
    wu_lidx = 0

    for tsb in range(2):
        t0 = tsb * 2048
        for tt in range(16):
            gt = tsb * 16 + tt
            row = t0 + tt * 128
            xv = {}
            for hf in range(2):
                w = [(s_sc, x_free[(gt - 1, hf)])] if gt >= 1 else \
                    [(s_cst, consts_done)]
                P.op("sync", lambda e, row=row, hf=hf: e.dma_start(
                    out=x_sb[hf][:],
                    in_=x_l[row:row + 128, hf * 2048:(hf + 1) * 2048]),
                    waits=w, inc16=s_xh[hf])
                xv[hf] = s_xh[hf].n
                for i in range(4):
                    P.op("vector", lambda e, hf=hf, i=i: e.bn_stats(
                        stats1[:, (hf * 4 + i) * 6:(hf * 4 + i) * 6 + 6],
                        x_sb[hf][:, i * 512:(i + 1) * 512]),
                        waits=[(s_xh[hf], xv[hf])],
                        inc=(s_ln if (hf == 1 and i == 3) else None))
            pp = gt % 2
            w = [(s_ln, s_ln.n)]
            if gt >= 2:
                w.append((s_sc, ln_v[gt - 2]))
            P.op("vector", lambda e, pp=pp: e.bn_aggr(mv1[pp][:], stats1[:]),
                 waits=w, inc=s_ln)
            P.op("scalar", lambda e, pp=pp: e.activation(
                out=lnt1[:], in_=mv1[pp][:, 1:2], func=AF.Ln,
                bias=eps_sb[:, 0:1]),
                waits=[(s_ln, s_ln.n)], inc=s_sc)
            ln_v[gt] = s_sc.n
            w = [(s_sc, s_sc.n)]
            if gt >= 2:
                w.append((s_sc, x_free[(gt - 2, 1)]))
            P.op("scalar", lambda e, pp=pp: e.activation(
                out=rstd1[pp][:], in_=lnt1[:], func=AF.Exp, scale=-0.5),
                waits=w, inc=s_sc)
            P.op("vector", lambda e, pp=pp: e.scalar_tensor_tensor(
                out=nmr1[pp][:], in0=mv1[pp][:, 0:1], scalar=-1.0,
                in1=rstd1[pp][:], op0=ALU.mult, op1=ALU.mult),
                waits=[(s_sc, s_sc.n)], inc=s_ln)
            nmr_v = s_ln.n
            for hf in range(2):
                P.op("scalar", lambda e, hf=hf, pp=pp: e.activation(
                    out=ln1h[:], in_=x_sb[hf][:], func=AF.Identity,
                    bias=nmr1[pp][:, 0:1], scale=rstd1[pp][:, 0:1]),
                    waits=[(s_ln, nmr_v), (s_tp, s_tp.n)], inc=s_sc)
                x_free[(gt, hf)] = s_sc.n
                lh = s_sc.n
                for grp in range(2):
                    for k in range(8):
                        c_loc = grp * 8 + k
                        w = [(s_sc, lh)]
                        if k == 0:
                            w.append((s_te, s_te.n))
                        P.op("tensor", lambda e, c_loc=c_loc, k=k: e.transpose(
                            ps_t[:, k * 128:(k + 1) * 128],
                            in_=ln1h[:, c_loc * 128:(c_loc + 1) * 128],
                            identity=identb[:]),
                            waits=w, inc=(s_tp if k == 7 else None))
                    c0 = hf * 16 + grp * 8
                    P.op("scalar", lambda e, c0=c0, tt=tt: e.copy(
                        out=ln1T[:, c0:c0 + 8, tt * 128:(tt + 1) * 128],
                        in_=ps_t[:].rearrange("p (c f) -> p c f", c=8)),
                        waits=[(s_tp, s_tp.n)], inc=s_te)
        ln1_done = s_te.n

        for ft in range(32):
            w = [(s_te, ln1_done)]
            if wu_lidx >= 2:
                w.append((s_mm, wu_free[wu_lidx - 2]))
            my_wu = wu_lidx % 2
            P.op("gpsimd", lambda e, ft=ft, my_wu=my_wu: e.dma_start(
                out=w_u[my_wu][:, 0:4096], in_=wqk[ft][:, 0:4096]),
                waits=w, inc16=s_wp[my_wu])
            wld = s_wp[my_wu].n
            wu_lidx += 1
            for tb in range(4):
                pq = [ps_a, ps_b][grp_idx % 2]
                w = [(s_wp[my_wu], wld), (s_te, ln1_done)]
                if grp_idx >= 2:
                    w.append((s_ev, grp_ev[grp_idx - 2]))
                for c in range(DC):
                    P.op("tensor", lambda e, c=c, tb=tb, pq=pq, my_wu=my_wu:
                         e.matmul(pq[:, 0:512],
                                  lhsT=w_u[my_wu][:, c * 128:(c + 1) * 128],
                                  rhs=ln1T[:, c, tb * 512:(tb + 1) * 512],
                                  start=(c == 0), stop=(c == DC - 1)),
                         waits=(w if c == 0 else ()),
                         inc=(s_mm if c == DC - 1 else None))
                wu_free[wu_lidx - 1] = s_mm.n
                qe = len(qk_store_v)
                w = [(s_mm, s_mm.n)]
                if qe >= 2:
                    w.append((s_stp[qe % 2], qk_store_v[qe - 2]))
                P.op("scalar", lambda e, ft=ft, pq=pq, i=qe % 2: e.activation(
                    out=qkst[i][:], in_=pq[:, 0:512], func=AF.Identity,
                    bias=qkvb_sb[:, ft:ft + 1]),
                    waits=w, inc=s_ev)
                grp_ev.append(s_ev.n)
                grp_idx += 1
                dst = qT_s if ft < 16 else kT_s
                hh = ft % 16
                P.op("sync", lambda e, dst=dst, hh=hh, t0=t0, tb=tb, i=qe % 2:
                     e.dma_start(
                         out=dst[hh, :, t0 + tb * 512:t0 + (tb + 1) * 512],
                         in_=qkst[i][:]),
                     waits=[(s_ev, s_ev.n)], inc16=s_stp[qe % 2])
                qk_store_v.append(s_stp[qe % 2].n)

        for fb in range(8):
            w = [(s_te, ln1_done)]
            if wu_lidx >= 2:
                w.append((s_mm, wu_free[wu_lidx - 2]))
            my_wu = wu_lidx % 2
            P.op("gpsimd", lambda e, fb=fb, my_wu=my_wu: e.dma_start(
                out=w_u[my_wu][:], in_=wv[fb][:]),
                waits=w, inc16=s_wp[my_wu])
            wld = s_wp[my_wu].n
            wu_lidx += 1
            for tt in range(16):
                pq = [ps_a, ps_b][grp_idx % 2]
                w = [(s_wp[my_wu], wld)]
                if grp_idx >= 2:
                    w.append((s_ev, grp_ev[grp_idx - 2]))
                for c in range(DC):
                    P.op("tensor", lambda e, c=c, tt=tt, pq=pq, my_wu=my_wu:
                         e.matmul(pq[:, 0:256],
                                  lhsT=ln1T[:, c, tt * 128:(tt + 1) * 128],
                                  rhs=w_u[my_wu][:, c * 256:(c + 1) * 256],
                                  start=(c == 0), stop=(c == DC - 1)),
                         waits=(w if c == 0 else ()),
                         inc=(s_mm if c == DC - 1 else None))
                wu_free[wu_lidx - 1] = s_mm.n
                ve = len(v_store_v)
                w = [(s_mm, s_mm.n), (s_te, vbbc_done)]
                if ve >= 2:
                    w.append((s_vsp[ve % 2], v_store_v[ve - 2]))
                P.op("vector", lambda e, fb=fb, pq=pq, i=ve % 2: e.tensor_tensor(
                    out=vst[i][:], in0=pq[:, 0:256],
                    in1=vb_bc[:, fb * 256:(fb + 1) * 256], op=ALU.add),
                    waits=w, inc=s_ev)
                grp_ev.append(s_ev.n)
                grp_idx += 1
                ktile = tsb * 16 + tt
                for hh in range(2):
                    head = fb * 2 + hh
                    P.op("sync", lambda e, head=head, ktile=ktile, hh=hh, i=ve % 2:
                         e.dma_start(out=v_s[head, :, ktile, :],
                                     in_=vst[i][:, hh * 128:(hh + 1) * 128]),
                         waits=[(s_ev, s_ev.n)], inc16=s_vsp[ve % 2])
                v_store_v.append(s_vsp[ve % 2].n)

    P.op("sync", lambda e: e.nop(),
         waits=[(s_stp[0], s_stp[0].n), (s_stp[1], s_stp[1].n),
                (s_vsp[0], s_vsp[0].n), (s_vsp[1], s_vsp[1].n)], inc=s_bar)
    P.op("tensor", lambda e: e.nop(), waits=[(s_mm, s_mm.n)], inc=s_bar)
    P.op("scalar", lambda e: e.nop(), waits=[(s_ev, s_ev.n)], inc=s_bar)
    P.op("vector", lambda e: e.nop(), waits=[(s_ev, s_ev.n)], inc=s_bar)
    P.op("gpsimd", lambda e: e.nop(),
         waits=[(s_wp[0], s_wp[0].n), (s_wp[1], s_wp[1].n)], inc=s_bar)
    bar1 = s_bar.n
    p1.close()

    p2 = contextlib.ExitStack()
    probsT2 = [sbuf(f"probsT{i}", [128, 16, 1024], BF16, p2) for i in range(2)]
    kT_sb = [sbuf(f"kT_sb{i}", [128, SL], BF16, p2) for i in range(2)]
    v_sb = [sbuf(f"v_sb{i}", [128, DC, 128], BF16, p2) for i in range(2)]
    qT_sb = [sbuf(f"qT_sb{i}", [128, QB], BF16, p2) for i in range(2)]
    ctxT = sbuf("ctxT", [128, NH, QB], BF16, p2)
    rx = sbuf("rx", [128, NH, QB], BF16, p2)
    ccm = [sbuf(f"ccm{i}", [128, 1024], BF16, p2) for i in range(2)]
    acc4 = sbuf("acc4", [128, 4, 512], BF16, p2)
    dtmp = sbuf("dtmp", [128, 512], F32, p2)
    dfin = sbuf("dfin", [128, 512], F32, p2)
    den_bf = sbuf("den_bf", [128, 512], BF16, p2)
    recip = [sbuf(f"recip{i}", [1, 512], F32, p2) for i in range(2)]
    bcast_sb = [sbuf(f"bcast{i}", [128, 512], F32, p2) for i in range(2)]
    ow_cb = [sbuf(f"ow_cb{i}", [128, 4096], BF16, p2) for i in range(3)]
    xsl = [sbuf(f"xsl{i}", [128, 256], F32, p2) for i in range(4)]
    t1sl = sbuf("t1sl", [128, 256], F32, p2)
    x2st = [sbuf(f"x2st{i}", [128, 256], F32, p2) for i in range(2)]
    stats2 = sbuf("stats2", [128, 4, 96], F32, p2)
    mv2 = [sbuf(f"mv2_{i}", [128, 2], F32, p2) for i in range(2)]
    lnt2 = sbuf("lnt2", [128, 1], F32, p2)
    rstd2 = [sbuf(f"rstd2_{i}", [128, 1], F32, p2) for i in range(2)]
    nmr2 = [sbuf(f"nmr2_{i}", [128, 1], F32, p2) for i in range(2)]
    x2rd = [sbuf(f"x2rd{i}", [128, 512], F32, p2) for i in range(3)]
    ln2h = [sbuf(f"ln2h{i}", [128, 512], BF16, p2) for i in range(4)]
    ob_bc = sbuf("ob_bc", [128, H], BF16, p2)

    for e_ in ("sync", "tensor", "scalar", "vector", "gpsimd"):
        P.wait(e_, [(s_bar, bar1)])
    emit_row_bcast(ob_row, H, ob_bc)
    obbc_done = s_te.n

    ow_ld = 0
    op_group_v = []
    oe_vals = []
    x2_stores = {}
    l2_stores = {}
    x2_idx = 0
    x2_store_v = []
    ln2_apply_v = []
    ln2_st_v = []
    ln2_ln_v = {}
    ln2_gt = 0

    spill_v = {}
    ow_state = {"i": 0, "free": {}}
    oe_slot_end = {}
    ccd_hist = []
    n_cc_slots = [0]
    prev_op_end = None
    kv_pref = {}

    def emit_kv_load(h, hg, par, qb_ld):
        wA = [(s_mm, (hg - 2) * 16 + 8)] if hg >= 2 else []
        wF = [(s_mm, (hg - 2) * 16 + 16)] if hg >= 2 else []
        wV = [(s_ctx, (hg - 1) * 16)] if hg >= 2 else []
        P.op("sync", lambda e, h=h, par=par: e.dma_start(
            out=kT_sb[par][:, 0:2048], in_=kT_s[h][:, 0:2048]),
            waits=wA, inc16=s_kvp[par])
        P.op("sync", lambda e, h=h, par=par, qb_ld=qb_ld: e.dma_start(
            out=qT_sb[par][:], in_=qT_s[h, :, qb_ld * QB:(qb_ld + 1) * QB]),
            waits=wF, inc16=s_kvp[par])
        kvv = s_kvp[par].n
        P.op("sync", lambda e, h=h, par=par: e.dma_start(
            out=kT_sb[par][:, 2048:4096], in_=kT_s[h][:, 2048:4096]),
            inc16=s_kv2[par])
        P.op("sync", lambda e, h=h, par=par: e.dma_start(
            out=v_sb[par][:].rearrange("p c f -> p (c f)"),
            in_=v_s[h].rearrange("p c f -> p (c f)")), waits=wV,
            inc16=s_kv2[par])
        return (kvv, s_kv2[par].n)

    for s in range(NSL):
        qb = QORDER[s]
        pend_spills = list(range((s // 2) * 8, (s // 2) * 8 + 8)) \
            if s in (0, 2) else []

        def emit_spill():
            if not pend_spills:
                return
            ob = pend_spills.pop(0)
            i = ob % 2
            P.op("gpsimd", lambda e, ob=ob: e.dma_start(
                out=ow_bf_s[ob][:], in_=ow_h[ob][:]),
                waits=[(s_bar, bar1)], inc16=s_vsp[i])
            spill_v[ob] = s_vsp[i].n
        if qb < 4:
            j = qb
            w = [(s_cc, j + 1)]
            if op_group_v:
                w.append((s_op, op_group_v[-1]))
            P.op("gpsimd", lambda e, j=j: e.dma_start(
                out=rx[:], in_=cc_out[j].rearrange("(h p) f -> p h f", p=128)),
                waits=w, inc16=s_rx)
        rx_v = s_rx.n
        has_prev_op = op_group_v and prev_op_end is not None

        def emit_den(hh):
            w = [(s_dt, hh + 1)]
            if hh >= 1:
                w.append((s_bcp, hh))
            else:
                w.append((s_te, obbc_done))
            P.op("tensor", lambda e, hh=hh: e.matmul(
                ps_d[0:1, :], lhsT=onesb[:], rhs=den_bf[:], start=True,
                stop=True), waits=w, inc=s_dmm)

        def emit_recip(hh):
            w = [(s_dmm, hh + 1)]
            if hh >= 2:
                w.append((s_bc, hh - 1))
            P.op("vector", lambda e, i=hh % 2: e.reciprocal(
                recip[i][:], ps_d[0:1, :]), waits=w, inc=s_rc)

        def emit_bcast(hh):
            P.op("tensor", lambda e, i=hh % 2: e.matmul(
                ps_d[:], lhsT=onesf[:], rhs=recip[i][:], start=True, stop=True),
                waits=[(s_rc, hh + 1)], inc=s_bc)

        def emit_bcopy(hh):
            w = [(s_bc, hh + 1)]
            if hh >= 2:
                w.append((s_ce, hh - 1))
            P.op("scalar", lambda e, i=hh % 2: e.copy(
                out=bcast_sb[i][:], in_=ps_d[:]), waits=w, inc=s_bcp)

        def emit_evac(hh):
            hloc = hh % NH
            pc2 = [ps_c0, ps_c1][hh % 2]
            P.op("vector", lambda e, hloc=hloc, pc2=pc2, i=hh % 2: e.tensor_tensor(
                ctxT[:, hloc, :], pc2[:], bcast_sb[i][:], op=ALU.mult),
                waits=[(s_ctx, (hh + 1) * 16), (s_bcp, hh + 1)], inc=s_ce)

        def emit_ccm(hh):
            hloc = hh % NH
            ss = QORDER[hh // NH] - 4
            w = [(s_ccd, ccd_hist[-2])] if len(ccd_hist) >= 2 else []
            P.op("vector", lambda e, hloc=hloc, i=hh % 2: e.tensor_scalar(
                out=ccm[i][:, 0:512], in0=ctxT[:, hloc, :],
                scalar1=masks[:, 0:1], scalar2=None, op0=ALU.mult),
                waits=w)
            P.op("vector", lambda e, hloc=hloc, i=hh % 2: e.tensor_scalar(
                out=ccm[i][:, 512:1024], in0=ctxT[:, hloc, :],
                scalar1=masks[:, 1:2], scalar2=None, op0=ALU.mult),
                inc=s_ccm)
            P.op("gpsimd", lambda e, ss=ss, hloc=hloc, i=hh % 2: e.dma_start(
                out=cc_in[ss][hloc * 128:(hloc + 1) * 128, :],
                in_=ccm[i][:, 0:512]),
                waits=[(s_ccm, s_ccm.n)], inc16=s_ccd)
            P.op("gpsimd", lambda e, ss=ss, hloc=hloc, i=hh % 2: e.dma_start(
                out=cc_in[ss][TOWN + hloc * 128:TOWN + (hloc + 1) * 128, :],
                in_=ccm[i][:, 512:1024]), inc16=s_ccd)
            ccd_hist.append(s_ccd.n)

        def emit_back(hh):
            emit_evac(hh)
            if QORDER[hh // NH] >= 4:
                emit_ccm(hh)

        for h in range(NH):
            hg = s * NH + h
            par = hg % 2
            if hg in kv_pref:
                kvv, kvv2 = kv_pref.pop(hg)
            else:
                kvv, kvv2 = emit_kv_load(h, hg, par, qb)

            sc_base = s_mm.n
            do_ctx = h >= 1

            def emit_scores_grp(gk):
                G = hg * 16 + gk
                pq = [ps_a, ps_b][G % 2]
                w = [(s_kvp[par], kvv)]
                if gk == 8:
                    w.append((s_kv2[par], kvv2))
                if G >= 2:
                    w.append((s_exp, G - 1))
                if gk == 0 and h == 0 and prev_op_end is not None:
                    w.append((s_oe, prev_op_end))
                for half in range(2):
                    kt = gk * 2 + half
                    P.op("tensor", lambda e, kt=kt, half=half, par=par, pq=pq:
                         e.matmul(pq[:, half * 512:(half + 1) * 512],
                                  lhsT=kT_sb[par][:, kt * 128:(kt + 1) * 128],
                                  rhs=qT_sb[par][:], start=True, stop=True),
                         waits=(w if half == 0 else ()),
                         inc=(s_mm if half == 1 else None))

            def emit_ctx_grp(p):
                hc = hg - 1
                pc = [ps_c0, ps_c1][hc % 2]
                parc = hc % 2
                w = [(s_exp, hc * 16 + p + 1)]
                if p == 0 and hc >= 2:
                    w.append((s_ce, hc - 1))
                if p == 0 and hc % NH < 2 and prev_op_end is not None:
                    w.append((s_oe, prev_op_end))
                for half in range(2):
                    kt = p * 2 + half
                    P.op("tensor", lambda e, kt=kt, p=p, half=half, parc=parc,
                         pc=pc, pb=hc % 2: e.matmul(
                             pc[:], lhsT=v_sb[parc][:, kt, :],
                             rhs=probsT2[pb][:, p, half * 512:(half + 1) * 512],
                             start=(kt == 0), stop=(kt == 31)),
                         waits=(w if half == 0 else ()),
                         inc=(s_ctx if half == 1 else None))

            for j in range(8):
                emit_scores_grp(2 * j)
                emit_scores_grp(2 * j + 1)
                if do_ctx:
                    emit_ctx_grp(2 * j)
                    emit_ctx_grp(2 * j + 1)
                if j == 0 and hg >= 1 and (hg - 1) % NH != NH - 1:
                    emit_den(hg - 1)
                if j == 3 and hg >= 1 and (hg - 1) % NH != NH - 1:
                    emit_bcast(hg - 1)

            for gk in range(16):
                G = hg * 16 + gk
                pq = [ps_a, ps_b][G % 2]
                w = [(s_mm, sc_base + gk + 1)]
                if hg >= 2:
                    w.append((s_ctx, (hg - 2) * 16 + gk + 1))
                    w.append((s_dt, hg - 1))
                P.op("scalar", lambda e, gk=gk, pq=pq, pb=hg % 2: e.activation(
                    out=probsT2[pb][:, gk, :], in_=pq[:], func=AF.Exp, scale=SCALE),
                    waits=w, inc=s_exp)
            if hg >= 1 and (hg - 1) % NH != NH - 1:
                emit_bcopy(hg - 1)
            if h >= 2 and h % 2 == 0:
                emit_spill()

            if hg >= 2 and (hg - 2) % NH < NH - 2:
                emit_back(hg - 2)
            if hg >= 1 and (hg - 1) % NH != NH - 1:
                emit_recip(hg - 1)
            for gk in range(16):
                G = hg * 16 + gk
                q = gk // 4
                w = [(s_exp, G + 1)]
                if gk % 4 == 0:
                    P.op("vector", lambda e, gk=gk, q=q, pb=hg % 2: e.tensor_tensor(
                        acc4[:, q, :], probsT2[pb][:, gk, 0:512],
                        probsT2[pb][:, gk, 512:1024], op=ALU.add), waits=w)
                else:
                    P.op("vector", lambda e, gk=gk, q=q, pb=hg % 2: e.tensor_tensor(
                        acc4[:, q, :], acc4[:, q, :], probsT2[pb][:, gk, 0:512],
                        op=ALU.add), waits=w)
                    P.op("vector", lambda e, gk=gk, q=q, pb=hg % 2: e.tensor_tensor(
                        acc4[:, q, :], acc4[:, q, :], probsT2[pb][:, gk, 512:1024],
                        op=ALU.add))
            P.op("vector", lambda e: e.tensor_tensor(
                dfin[:], acc4[:, 0, :], acc4[:, 1, :], op=ALU.add))
            P.op("vector", lambda e: e.tensor_tensor(
                dtmp[:], acc4[:, 2, :], acc4[:, 3, :], op=ALU.add))
            P.op("vector", lambda e: e.tensor_tensor(
                dfin[:], dfin[:], dtmp[:], op=ALU.add))
            w = [(s_dmm, hg)] if hg >= 1 else []
            P.op("vector", lambda e: e.tensor_copy(den_bf[:], dfin[:]),
                 waits=w, inc=s_dt)

        hg_last = s * NH + NH - 1
        hc = hg_last
        pcf = [ps_c0, ps_c1][hc % 2]
        parf = hc % 2
        pbf = hc % 2
        for p in range(16):
            w = [(s_exp, hc * 16 + p + 1)]
            if p == 0 and hc >= 2:
                w.append((s_ce, hc - 1))
            for half in range(2):
                kt = p * 2 + half
                P.op("tensor", lambda e, kt=kt, p=p, half=half, parf=parf,
                     pcf=pcf, pbf=pbf: e.matmul(
                         pcf[:], lhsT=v_sb[parf][:, kt, :],
                         rhs=probsT2[pbf][:, p, half * 512:(half + 1) * 512],
                         start=(kt == 0), stop=(kt == 31)),
                     waits=(w if half == 0 else ()),
                     inc=(s_ctx if half == 1 else None))
        emit_den(hg_last)
        emit_recip(hg_last)
        emit_bcast(hg_last)
        emit_bcopy(hg_last)
        emit_back(hg_last - 1)
        emit_back(hg_last)

        while pend_spills:
            emit_spill()
        if qb >= 4:
            n_cc_slots[0] += 1
            cj = qb - 4
            P.op("gpsimd", lambda e, cj=cj: e.collective_compute(
                "ReduceScatter", ALU.add, ins=[cc_in[cj][:]],
                outs=[cc_out[cj][:]], replica_groups=GROUPS),
                waits=[(s_ccd, n_cc_slots[0] * NH * 32)], inc=s_cc)
        else:
            j = qb
            equads = [(ps_a, 0, 256), (ps_a, 512, 256), (ps_b, 0, 256),
                      (ps_b, 512, 256)]
            oquads = [(ps_c0, 0, 256), (ps_c0, 256, 256), (ps_c1, 0, 256),
                      (ps_c1, 256, 256)]
            slot_exp = (s + 1) * NH * 16
            slot_ce = (s + 1) * NH
            oe_slot0 = s_oe.n
            use_spill = s >= 4
            ow_sems = [s_wp[0], s_wp[1], s_row]
            for ob in range(16):
                quads = equads if ob % 2 == 0 else oquads
                hb = {}
                hv = {}
                for half in range(2):
                    oi = ow_state["i"]
                    ow_state["i"] += 1
                    bi = oi % 3
                    hb[half] = bi
                    w = []
                    if bi in ow_state["free"]:
                        w.append(ow_state["free"][bi])
                    if use_spill and ob < 2 and half == 0:
                        w.append((s_vsp[0], s_vsp[0].n))
                        w.append((s_vsp[1], s_vsp[1].n))
                    osrc = ow_bf_s[ob] if use_spill else ow_h[ob]
                    P.op("gpsimd", lambda e, osrc=osrc, bi=bi, half=half:
                         e.dma_start(out=ow_cb[bi][:],
                                     in_=osrc[:, half * 4096:(half + 1) * 4096]),
                         waits=w, inc16=ow_sems[bi])
                    hv[half] = ow_sems[bi].n
                ow_ld += 1
                for tt in range(4):
                    w = []
                    if len(oe_vals) >= 4:
                        w.append((s_oe, oe_vals[len(oe_vals) - 4 + tt]))
                    P.op("sync", lambda e, qb=qb, tt=tt, ob=ob:
                         e.dma_start(
                             out=xsl[tt][:],
                             in_=x_l[qb * 512 + tt * 128:qb * 512 + (tt + 1) * 128,
                                     ob * 256:(ob + 1) * 256]),
                         waits=w, inc16=s_xsl)
                xslv = s_xsl.n
                w = [(ow_sems[hb[0]], hv[0]), (ow_sems[hb[1]], hv[1]),
                     (s_exp, slot_exp), (s_ce, slot_ce), (s_rx, rx_v)]
                if ob >= 2:
                    w.append((s_oe, oe_vals[-5]))
                for cf in range(DC):
                    src_ = ctxT if cf < 16 else rx
                    hh = cf % 16
                    bi = hb[cf // 16]
                    for tt in range(4):
                        pq, off, _qw = quads[tt]
                        P.op("tensor", lambda e, src_=src_, hh=hh, tt=tt, cf=cf,
                             pq=pq, off=off, bi=bi: e.matmul(
                                 pq[:, off:off + 256],
                                 lhsT=src_[:, hh, tt * 128:(tt + 1) * 128],
                                 rhs=ow_cb[bi][:, (cf % 16) * 256:
                                               (cf % 16 + 1) * 256],
                                 start=(cf == 0), stop=(cf == DC - 1)),
                             waits=(w if (cf == 0 and tt == 0) else ()),
                             inc=(s_op if (tt == 3 and cf in (15, DC - 1))
                                  else None))
                    if cf == 15:
                        ow_state["free"][hb[0]] = (s_op, s_op.n)
                op_group_v.append(s_op.n)
                ow_state["free"][hb[1]] = (s_op, s_op.n)
                for tt in range(4):
                    gi = ob * 4 + tt
                    pq, off, _qw = quads[tt]
                    P.op("vector", lambda e, pq=pq, off=off, ob=ob: e.tensor_tensor(
                        t1sl[:], pq[:, off:off + 256],
                        ob_bc[:, ob * 256:(ob + 1) * 256], op=ALU.add),
                        waits=[(s_op, s_op.n), (s_xsl, xslv)])
                    ii = gi % 2
                    w2 = [(s_x2p[x2_idx % 2], x2_store_v[x2_idx - 2])] \
                        if x2_idx >= 2 else []
                    P.op("vector", lambda e, ii=ii, tt=tt: e.tensor_tensor(
                        x2st[ii][:], t1sl[:], xsl[tt][:], op=ALU.add), waits=w2)
                    P.op("vector", lambda e, ii=ii, tt=tt, ob=ob: e.bn_stats(
                        stats2[:, tt, ob * 6:(ob + 1) * 6], x2st[ii][:]),
                        inc=s_oe)
                    oe_vals.append(s_oe.n)
                    P.op("sync", lambda e, ii=ii, qb=qb, tt=tt, ob=ob: e.dma_start(
                        out=x2_s[qb * 512 + tt * 128:qb * 512 + (tt + 1) * 128,
                                 ob * 256:(ob + 1) * 256],
                        in_=x2st[ii][:]),
                        waits=[(s_oe, s_oe.n)], inc16=s_x2p[x2_idx % 2])
                    x2_store_v.append(s_x2p[x2_idx % 2].n)
                    x2_idx += 1
            x2_stores[qb] = (s_x2p[0].n, s_x2p[1].n)
            oe_slot_end[s] = s_oe.n
            prev_op_end = s_oe.n
            if s + 1 < NSL:
                hgn = (s + 1) * NH
                kv_pref[hgn] = emit_kv_load(0, hgn, hgn % 2, QORDER[s + 1])
            for tt in range(4):
                pp = ln2_gt % 2
                w = [(s_oe, s_oe.n)]
                if ln2_gt >= 2:
                    w.append((s_sc, ln2_ln_v[ln2_gt - 2]))
                P.op("vector", lambda e, tt=tt, pp=pp: e.bn_aggr(
                    mv2[pp][:], stats2[:, tt, :]), waits=w, inc=s_ln)
                P.op("scalar", lambda e, pp=pp: e.activation(
                    out=lnt2[:], in_=mv2[pp][:, 1:2], func=AF.Ln,
                    bias=eps_sb[:, 0:1]),
                    waits=[(s_ln, s_ln.n)], inc=s_sc)
                ln2_ln_v[ln2_gt] = s_sc.n
                w = [(s_sc, s_sc.n)]
                if ln2_gt >= 2:
                    w.append((s_sc, ln2_apply_v[8 * (ln2_gt - 2) + 7]))
                P.op("scalar", lambda e, pp=pp: e.activation(
                    out=rstd2[pp][:], in_=lnt2[:], func=AF.Exp, scale=-0.5),
                    waits=w, inc=s_sc)
                P.op("vector", lambda e, pp=pp: e.scalar_tensor_tensor(
                    out=nmr2[pp][:], in0=mv2[pp][:, 0:1], scalar=-1.0,
                    in1=rstd2[pp][:], op0=ALU.mult, op1=ALU.mult),
                    waits=[(s_sc, s_sc.n)], inc=s_ln)
                nv = s_ln.n
                ln2_gt += 1
                for hf in range(8):
                    k = len(ln2_apply_v)
                    ii = k % 3
                    lb = k % 4
                    w = [(s_x2p[0], x2_stores[qb][0]), (s_x2p[1], x2_stores[qb][1])]
                    if k >= 3:
                        w.append((s_sc, ln2_apply_v[k - 3]))
                    P.op(["sync", "gpsimd"][k % 2],
                         lambda e, qb=qb, tt=tt, hf=hf, ii=ii: e.dma_start(
                             out=x2rd[ii][:],
                             in_=x2_s[qb * 512 + tt * 128:qb * 512 + (tt + 1) * 128,
                                      hf * 512:(hf + 1) * 512]),
                         waits=w, inc16=s_xr[ii])
                    xv = s_xr[ii].n
                    lw = [(s_ln, nv), (s_xr[ii], xv)]
                    if k >= 4:
                        lw.append((s_lh[lb], ln2_st_v[k - 4]))
                    P.op("scalar", lambda e, ii=ii, pp=pp, lb=lb: e.activation(
                        out=ln2h[lb][:], in_=x2rd[ii][:], func=AF.Identity,
                        bias=nmr2[pp][:, 0:1], scale=rstd2[pp][:, 0:1]),
                        waits=lw, inc=s_sc)
                    ln2_apply_v.append(s_sc.n)
                    P.op("scalar",
                         lambda e, qb=qb, tt=tt, hf=hf, lb=lb: e.dma_start(
                             out=ln2_s[qb * 512 + tt * 128:qb * 512 + (tt + 1) * 128,
                                       hf * 512:(hf + 1) * 512],
                             in_=ln2h[lb][:]),
                         inc16=s_lh[lb])
                    ln2_st_v.append(s_lh[lb].n)
            l2_stores[qb] = tuple(sem.n for sem in s_lh)

    P.op("sync", lambda e: e.nop(),
         waits=[(sem, sem.n) for sem in s_lh]
               + [(s_x2p[0], s_x2p[0].n), (s_x2p[1], s_x2p[1].n)], inc=s_bar)
    P.op("tensor", lambda e: e.nop(), waits=[(s_op, s_op.n)], inc=s_bar)
    P.op("scalar", lambda e: e.nop(), waits=[(s_bcp, s_bcp.n)], inc=s_bar)
    P.op("vector", lambda e: e.nop(), waits=[(s_ln, s_ln.n)], inc=s_bar)
    P.op("gpsimd", lambda e: e.nop(), waits=[(s_cc, 4)], inc=s_bar)
    bar2 = s_bar.n
    p2.close()

    p3 = contextlib.ExitStack()
    ln2T = sbuf("ln2T", [128, DC, 1024], BF16, p3)
    ln2rd = [sbuf(f"ln2rd{i}", [128, H], BF16, p3) for i in range(2)]
    WT = [sbuf(f"WT{i}", [128, DC, OB3], BF16, p3) for i in range(2)]
    wrow = [sbuf(f"wrow{i}", [128, 2048], F32, p3) for i in range(4)]
    x2sl = [sbuf(f"x2sl{i}", [128, OB3], F32, p3) for i in range(2)]
    outst = [sbuf(f"outst{i}", [128, OB3], F32, p3) for i in range(2)]
    t1sl3 = sbuf("t1sl3", [128, OB3], F32, p3)
    b_bc = sbuf("b_bc", [128, H], BF16, p3)
    a_bf = sbuf("a_bf", [64, H], BF16, p3)
    AT_sb = sbuf("AT_sb", [128, DC, 64], BF16, p3)
    b2_bf = sbuf("b2_bf", [128, DC * 64], BF16, p3)
    BT_sb = sbuf("BT_sb", [64, H], BF16, p3)
    gw_bf = sbuf("gw_bf", [4, H], BF16, p3)
    gwT_sb = sbuf("gwT_sb", [128, DC, 4], BF16, p3)
    h1T = sbuf("h1T", [64, 1024], BF16, p3)
    mT_sb = sbuf("mT_sb", [64, 1024], BF16, p3)
    m_sb = sbuf("m_sb", [64, 1024], BF16, p3)
    g_f32 = sbuf("g_f32", [4, 512], F32, p3)
    gn_sb = sbuf("gn_sb", [128, 16], F32, p3)
    gm1 = sbuf("gm1", [128, 4], F32, p3)
    ga = sbuf("ga", [128, 1], F32, p3)
    maskn = sbuf("maskn", [128, 64], BF16, p3)

    for e_ in ("sync", "tensor", "scalar", "vector", "gpsimd"):
        P.wait(e_, [(s_bar, bar2)])

    emit_row_bcast(b_row, H, b_bc)

    P.op("gpsimd", lambda e: e.dma_start(out=a_bf[:], in_=A_in[:]), inc16=s_wr)
    P.op("gpsimd", lambda e: e.dma_start(out=b2_bf[:], in_=B_in[:]), inc16=s_wr)
    P.op("gpsimd", lambda e: e.dma_start(out=gw_bf[:], in_=gw_in[:]), inc16=s_wr)
    pre_w = s_wr.n

    for grp in range(2):
        for k in range(16):
            c = grp * 16 + k
            w = [(s_wr, pre_w)]
            if k == 0:
                w.append((s_te, s_te.n))
            P.op("tensor", lambda e, c=c, k=k: e.transpose(
                ps_t[:, k * 64:(k + 1) * 64],
                in_=a_bf[0:64, c * 128:(c + 1) * 128],
                identity=identb[0:64, 0:64]),
                waits=w, inc=(s_tp if k == 15 else None))
        P.op("scalar", lambda e, grp=grp: e.copy(
            out=AT_sb[:, grp * 16:(grp + 1) * 16, :],
            in_=ps_t[:].rearrange("p (c f) -> p c f", c=16)),
            waits=[(s_tp, s_tp.n)], inc=s_te)
    for grp in range(4):
        for k in range(8):
            c = grp * 8 + k
            w = [(s_wr, pre_w)]
            if k == 0:
                w.append((s_te, s_te.n))
            P.op("tensor", lambda e, c=c, k=k: e.transpose(
                ps_t[0:64, k * 128:(k + 1) * 128],
                in_=b2_bf[:, c * 64:(c + 1) * 64], identity=identb[:]),
                waits=w, inc=(s_tp if k == 7 else None))
        P.op("scalar", lambda e, grp=grp: e.copy(
            out=BT_sb[0:64, grp * 1024:(grp + 1) * 1024], in_=ps_t[0:64, :]),
            waits=[(s_tp, s_tp.n)], inc=s_te)
    for k in range(DC):
        w = [(s_wr, pre_w)]
        if k == 0:
            w.append((s_te, s_te.n))
        P.op("tensor", lambda e, k=k: e.transpose(
            ps_t[:, k * 4:(k + 1) * 4], in_=gw_bf[0:4, k * 128:(k + 1) * 128],
            identity=identb[0:4, 0:4]),
            waits=w, inc=(s_tp if k == DC - 1 else None))
    P.op("scalar", lambda e: e.copy(
        out=gwT_sb[:], in_=ps_t[:, 0:128].rearrange("p (c f) -> p c f", c=32)),
        waits=[(s_tp, s_tp.n)], inc=s_te)

    ffn_grp = 0
    wtb_state = {"g": 0, "bcp": {}}
    wrow_half_tp = {}
    nonlocal_state = {"ffn_grp": 0, "out_ct": 0}
    pend_mm = None
    out_store_v = []
    g_tp_v = {}
    gn_free_v = {}
    ffn_grp_mm = []
    wrow_tp = {}
    out_ct = 0
    ln2rd_tp = {}

    for th in range(2):
        tbase = th * 1024
        if pend_mm is not None:
            pend_mm()
            pend_mm = None
        for tt in range(8):
            gti = th * 8 + tt
            row = tbase + tt * 128
            qb_src = row // 512
            w = [(s_lh[i_], l2_stores[qb_src][i_]) for i_ in range(4)]
            if gti >= 2:
                w.append((s_tp, ln2rd_tp[gti - 2]))
            P.op("sync", lambda e, row=row, i=tt % 2: e.dma_start(
                out=ln2rd[i][:], in_=ln2_s[row:row + 128, :]),
                waits=w, inc16=s_xh[tt % 2])
            xv = s_xh[tt % 2].n
            for grp in range(4):
                for k in range(8):
                    c = grp * 8 + k
                    w2 = [(s_xh[tt % 2], xv)]
                    if k == 0:
                        w2.append((s_te, s_te.n))
                    P.op("tensor", lambda e, c=c, k=k, i=tt % 2: e.transpose(
                        ps_t[:, k * 128:(k + 1) * 128],
                        in_=ln2rd[i][:, c * 128:(c + 1) * 128],
                        identity=identb[:]),
                        waits=w2, inc=(s_tp if k == 7 else None))
                P.op("scalar", lambda e, grp=grp, tt=tt: e.copy(
                    out=ln2T[:, grp * 8:(grp + 1) * 8, tt * 128:(tt + 1) * 128],
                    in_=ps_t[:].rearrange("p (c f) -> p c f", c=8)),
                    waits=[(s_tp, s_tp.n)], inc=s_te)
            ln2rd_tp[gti] = s_tp.n
        ln2T_done = s_te.n

        for tq in range(2):
            gtq = th * 2 + tq
            gate_ps_free = [(s_ev, s_ev.n)] if (th > 0 or tq > 0) else []
            if th > 0 and tq == 0:
                gate_ps_free.append((s_bcp, s_bcp.n))
            for c in range(DC):
                w = [(s_te, ln2T_done)] + (gate_ps_free if c == 0 else [])
                P.op("tensor", lambda e, c=c, tq=tq: e.matmul(
                    ps_c0[0:4, :], lhsT=gwT_sb[:, c, :],
                    rhs=ln2T[:, c, tq * 512:(tq + 1) * 512],
                    start=(c == 0), stop=(c == DC - 1)),
                    waits=(w if c == 0 else ()),
                    inc=(s_mm if c == DC - 1 else None))
            w = [(s_mm, s_mm.n)]
            if gtq >= 1:
                w.append((s_tp, g_tp_v[gtq - 1]))
            P.op("vector", lambda e: e.tensor_scalar(
                out=g_f32[:], in0=ps_c0[0:4, :], scalar1=gb_sb[0:4, 0:1],
                scalar2=None, op0=ALU.add),
                waits=w, inc=s_ev)
            gev = s_ev.n
            for k in range(4):
                w = [(s_ev, gev), (s_bcp, s_bcp.n)]
                if k > 0:
                    w = []
                P.op("tensor", lambda e, k=k: e.transpose(
                    ps_d[:, k * 4:(k + 1) * 4],
                    in_=g_f32[0:4, k * 128:(k + 1) * 128],
                    identity=identf[0:4, 0:4]),
                    waits=w, inc=(s_tp if k == 3 else None))
            g_tp_v[gtq] = s_tp.n
            w = [(s_tp, s_tp.n)]
            if gtq >= 1:
                w.append((s_ln, gn_free_v[gtq - 1]))
            P.op("scalar", lambda e: e.copy(out=gn_sb[:], in_=ps_d[:, 0:16]),
                 waits=w, inc=s_bcp)
            gnv = s_bcp.n
            for k in range(4):
                base = k * 4
                P.op("vector", lambda e, base=base: e.reduce_max(
                    gm1[:, 0:1], gn_sb[:, base + 1:base + 4], axis=AX.X),
                    waits=[(s_bcp, gnv)])
                P.op("vector", lambda e, base=base: e.reduce_max(
                    gm1[:, 1:2], gn_sb[:, base + 2:base + 4], axis=AX.X))
                P.op("vector", lambda e, base=base: e.reduce_max(
                    gm1[:, 2:3], gn_sb[:, base + 0:base + 2], axis=AX.X))
                P.op("vector", lambda e, base=base: e.reduce_max(
                    gm1[:, 3:4], gn_sb[:, base + 0:base + 3], axis=AX.X),
                    inc=s_ln)
                mxv = s_ln.n
                P.op("vector", lambda e, base=base: e.tensor_tensor(
                    ga[:], gm1[:, 0:1], gn_sb[:, base:base + 1], op=ALU.is_gt),
                    waits=[(s_ln, mxv)])
                P.op("vector", lambda e: e.tensor_tensor(
                    gm1[:, 0:1], gm1[:, 1:2], gm1[:, 2:3], op=ALU.is_gt))
                P.op("vector", lambda e, base=base: e.tensor_tensor(
                    gm1[:, 1:2], gn_sb[:, base + 3:base + 4], gm1[:, 3:4],
                    op=ALU.is_gt), inc=s_ln)
                cv = s_ln.n
                gn_free_v[gtq] = cv
                P.op("vector", lambda e: e.tensor_tensor(
                    gm1[:, 2:3], gm1[:, 0:1], gm1[:, 1:2], op=ALU.add),
                    waits=[(s_ln, cv)], inc=s_ln)
                sumv = s_ln.n
                P.op("vector", lambda e: e.tensor_tensor(
                    ga[:], ga[:], gm1[:, 2:3], op=ALU.add),
                    waits=[(s_ln, sumv)], inc=s_ln)
                av = s_ln.n
                P.op("vector", lambda e: e.tensor_scalar(
                    out=maskn[:], in0=levels[:], scalar1=ga[:, 0:1],
                    scalar2=None, op0=ALU.is_le),
                    waits=[(s_ln, av), (s_te, s_te.n)], inc=s_ln)
                mk = s_ln.n
                P.op("tensor", lambda e: e.transpose(
                    ps_t[0:64, 0:128], in_=maskn[:], identity=identb[:]),
                    waits=[(s_ln, mk), (s_te, s_te.n)], inc=s_tp)
                P.op("scalar", lambda e, tq=tq, k=k: e.copy(
                    out=mT_sb[0:64, (tq * 4 + k) * 128:(tq * 4 + k + 1) * 128],
                    in_=ps_t[0:64, 0:128]),
                    waits=[(s_tp, s_tp.n)], inc=s_te)
            h1_ps_free = [(s_ev, s_ev.n)] if (th > 0 or tq > 0) else []
            if th > 0 and tq == 0:
                h1_ps_free.append((s_bcp, s_bcp.n))
            for c in range(DC):
                w = [(s_te, ln2T_done)] + (h1_ps_free if c == 0 else [])
                P.op("tensor", lambda e, c=c, tq=tq: e.matmul(
                    ps_c1[0:64, :], lhsT=AT_sb[:, c, :],
                    rhs=ln2T[:, c, tq * 512:(tq + 1) * 512],
                    start=(c == 0), stop=(c == DC - 1)),
                    waits=(w if c == 0 else ()),
                    inc=(s_mm if c == DC - 1 else None))
            P.op("scalar", lambda e, tq=tq: e.copy(
                out=h1T[0:64, tq * 512:(tq + 1) * 512], in_=ps_c1[0:64, :]),
                waits=[(s_mm, s_mm.n)], inc=s_ev)
        P.op("vector", lambda e: e.tensor_tensor(
            m_sb[0:64, :], h1T[0:64, :], mT_sb[0:64, :], op=ALU.mult),
            waits=[(s_ev, s_ev.n), (s_te, s_te.n)], inc=s_ln)
        msk_done = s_ln.n

        for ob in range(NOB3):
            gob = th * NOB3 + ob
            wldv = {}
            for li in range(4):
                osb, dhalf = li // 2, li % 2
                gli = gob * 4 + li
                w = []
                if gli >= 4:
                    w.append((s_tp, wrow_half_tp[gli - 4]))
                P.op("sync", lambda e, ob=ob, osb=osb, dhalf=dhalf, li=li:
                     e.dma_start(
                         out=wrow[li][:],
                         in_=Wn[ob * 256 + osb * 128:ob * 256 + (osb + 1) * 128,
                                dhalf * 2048:(dhalf + 1) * 2048]),
                     waits=w, inc16=s_wr)
                wldv[li] = s_wr.n
            wti = gob % 2
            wtbanks = [ps_d, ps_c0, ps_c1]
            for osb in range(2):
                for grp in range(8):
                    wg = wtb_state["g"]
                    wtb_state["g"] += 1
                    bank = wtbanks[wg % 3]
                    li = osb * 2 + grp // 4
                    for k in range(4):
                        c = grp * 4 + k
                        w = [(s_wr, wldv[li])]
                        if k == 0 and wg >= 3:
                            w.append((s_bcp, wtb_state["bcp"][wg - 3]))
                        if k == 0 and (wg % (NOB3 * 16)) < 3:
                            w.append((s_ev, s_ev.n))
                            w.append((s_bcp, s_bcp.n))
                        if k == 0 and grp == 0 and osb == 0 and gob >= 2:
                            w.append((s_mm, ffn_grp_mm[(gob - 2) * 8 + 7]))
                        P.op("tensor", lambda e, c=c, k=k, li=li, bank=bank:
                             e.transpose(
                                 bank[:, k * 128:(k + 1) * 128],
                                 in_=wrow[li][:, (c % 16) * 128:
                                              (c % 16 + 1) * 128],
                                 identity=identf[:]),
                             waits=w, inc=(s_tp if k == 3 else None))
                    P.op("scalar", lambda e, grp=grp, osb=osb, wti=wti, bank=bank:
                         e.copy(
                             out=WT[wti][:, grp * 4:(grp + 1) * 4,
                                         osb * 128:(osb + 1) * 128],
                             in_=bank[:].rearrange("p (c f) -> p c f", c=4)),
                         waits=[(s_tp, s_tp.n)], inc=s_bcp)
                    wtb_state["bcp"][wg] = s_bcp.n
                    if grp % 4 == 3:
                        wrow_half_tp[gob * 4 + osb * 2 + grp // 4] = s_tp.n
            wrow_tp[gob] = s_tp.n
            wt_done = s_bcp.n

            def mk_mm(ob, tbase, wt_done, wti, msk_done):
                def go():
                    global_unused = None
                    for tt in range(8):
                        nonlocal_state["ffn_grp"] += 1
                        fg = nonlocal_state["ffn_grp"] - 1
                        pq = [ps_a, ps_b][fg % 2]
                        w = [(s_bcp, wt_done), (s_ln, msk_done)]
                        if fg >= 2:
                            w.append((s_oe, oe_vals[-2]))
                        for c in range(DC):
                            P.op("tensor", lambda e, c=c, tt=tt, pq=pq, wti=wti:
                                 e.matmul(pq[:, 0:OB3],
                                          lhsT=ln2T[:, c, tt * 128:(tt + 1) * 128],
                                          rhs=WT[wti][:, c, :],
                                          start=(c == 0), stop=False),
                                 waits=(w if c == 0 else ()))
                        P.op("tensor", lambda e, tt=tt, ob=ob, pq=pq: e.matmul(
                            pq[:, 0:OB3],
                            lhsT=m_sb[0:64, tt * 128:(tt + 1) * 128],
                            rhs=BT_sb[0:64, ob * 256:(ob + 1) * 256],
                            start=False, stop=True), inc=s_mm)
                        mmv = s_mm.n
                        ffn_grp_mm.append(mmv)
                        ii = (fg + 1) % 2
                        qb_src = (tbase + tt * 128) // 512
                        w2 = [(s_x2p[0], x2_stores[qb_src][0]),
                              (s_x2p[1], x2_stores[qb_src][1])]
                        if fg >= 2:
                            w2.append((s_oe, oe_vals[-2]))
                        P.op("sync", lambda e, tbase=tbase, tt=tt, ob=ob, ii=ii:
                             e.dma_start(
                                 out=x2sl[ii][:],
                                 in_=x2_s[tbase + tt * 128:tbase + (tt + 1) * 128,
                                          ob * 256:(ob + 1) * 256]),
                             waits=w2, inc16=s_wp[ii])
                        xv2 = s_wp[ii].n
                        P.op("vector", lambda e, pq=pq, ob=ob, ii=ii:
                             e.tensor_tensor(
                                 t1sl3[:], pq[:, 0:OB3],
                                 b_bc[:, ob * 256:(ob + 1) * 256],
                                 op=ALU.add), waits=[(s_mm, mmv), (s_wp[ii], xv2)])
                        oc = nonlocal_state["out_ct"]
                        w3 = [(s_stp[oc % 2], out_store_v[oc - 2])] \
                            if oc >= 2 else []
                        P.op("vector", lambda e, ii=ii: e.tensor_tensor(
                            outst[ii][:], t1sl3[:], x2sl[ii][:], op=ALU.add),
                            waits=w3, inc=s_oe)
                        oe_vals.append(s_oe.n)
                        P.op("sync", lambda e, tbase=tbase, tt=tt, ob=ob, ii=ii:
                             e.dma_start(
                                 out=out_d[tbase + tt * 128:tbase + (tt + 1) * 128,
                                           ob * 256:(ob + 1) * 256],
                                 in_=outst[ii][:]),
                             waits=[(s_oe, s_oe.n)], inc16=s_stp[oc % 2])
                        out_store_v.append(s_stp[oc % 2].n)
                        nonlocal_state["out_ct"] += 1
                return go

            new_mm = mk_mm(ob, tbase, wt_done, wti, msk_done)
            if pend_mm is not None:
                pend_mm()
            pend_mm = new_mm
    if pend_mm is not None:
        pend_mm()
        pend_mm = None

    P.wait("sync", [(s_stp[0], s_stp[0].n), (s_stp[1], s_stp[1].n)])

    _check_deadlock(P)

    with nc.Block() as blk:
        for eng_name in ("sync", "gpsimd", "tensor", "scalar", "vector"):
            items = [o for o in P.ops if o[0] == eng_name]

            def make_body(items):
                def body(eng):
                    for _, waits, fn, incs in items:
                        for (hh, v) in waits:
                            eng.wait_ge(hh, v)
                        if fn is None:
                            continue
                        ins = fn(eng)
                        for (hh, amt) in incs:
                            ins = ins.then_inc(hh, amt)
                return body

            getattr(blk, eng_name)(make_body(items))

    p3.close()
    es.close()
    return nc


_CACHE = {}


def _get_program():
    if "nc" not in _CACHE:
        _CACHE["nc"] = build_program()
    return _CACHE["nc"]


def _prep_inputs(c, inputs):
    bidx, gg = c // 2, c % 2
    f32 = np.float32
    x = np.asarray(inputs["L0_attn_x"], f32)
    qkvw = np.asarray(inputs["L0_attn_attn_qkvw"], f32)
    qkvb = np.asarray(inputs["L0_attn_attn_qkvb"], f32)
    ow = np.asarray(inputs["L0_attn_attn_ow"], f32)
    obv = np.asarray(inputs["L0_attn_attn_ob"], f32)
    W = np.asarray(inputs["L0_ffn_W"], f32)
    bv = np.asarray(inputs["L0_ffn_b"], f32)
    A = np.asarray(inputs["L0_ffn_A"], f32)
    Bm = np.asarray(inputs["L0_ffn_B"], f32)
    gw = np.asarray(inputs["L0_ffn_gate_w"], f32)
    gb = np.asarray(inputs["L0_ffn_gate_b"], f32)

    own = slice(2048 * gg, 2048 * gg + 2048)
    oth = slice(2048 * (1 - gg), 2048 * (1 - gg) + 2048)
    x_loc = np.ascontiguousarray(np.concatenate([x[bidx, own], x[bidx, oth]], 0))

    qcols = qkvw[:, 2048 * gg:2048 * gg + 2048]
    kcols = qkvw[:, 4096 + 2048 * gg:4096 + 2048 * gg + 2048]
    qk = np.concatenate([qcols, kcols], 1)
    wqk_h = np.ascontiguousarray(
        qk.reshape(32, 128, 32, 128).transpose(2, 1, 0, 3).reshape(32, 128, -1))
    vcols = qkvw[:, 8192 + 2048 * gg:8192 + 2048 * gg + 2048]
    wv_h = np.ascontiguousarray(
        vcols.reshape(32, 128, 8, 256).transpose(2, 1, 0, 3).reshape(8, 128, -1))
    qkvb_qk = np.ascontiguousarray(np.concatenate(
        [qkvb[2048 * gg:2048 * gg + 2048],
         qkvb[4096 + 2048 * gg:4096 + 2048 * gg + 2048]]).reshape(32, 128).T)
    vb = np.ascontiguousarray(qkvb[8192 + 2048 * gg:8192 + 2048 * gg + 2048])
    ow_p = np.concatenate([ow[2048 * gg:2048 * gg + 2048],
                           ow[2048 * (1 - gg):2048 * (1 - gg) + 2048]], 0)
    ow_hh = np.ascontiguousarray(
        ow_p.reshape(32, 128, 16, 256).transpose(2, 1, 0, 3).reshape(16, 128, -1))
    B_h = np.ascontiguousarray(
        Bm.reshape(32, 128, 64).transpose(1, 0, 2).reshape(128, -1))

    bf = ml_dtypes.bfloat16
    lev = np.array([0] * 8 + [1] * 8 + [2] * 16 + [3] * 32, f32)
    return {
        "x_l": x_loc, "wqk": wqk_h, "wv": wv_h, "qkvb_qk": qkvb_qk,
        "vb_row": vb, "ow_h": ow_hh, "ob_row": obv, "Wn": W, "b_row": bv,
        "A_in": A, "B_in": B_h, "gw_in": gw, "gb_in": gb,
        "identb_in": np.eye(128, dtype=bf),
        "identf_in": np.eye(128, dtype=f32),
        "onesf_in": np.ones((1, 128), f32),
        "onesb_in": np.ones((128, 1), bf),
        "levels_in": np.tile(lev, (128, 1)),
        "masks_in": np.tile(np.array([0.0, 1.0] if gg == 0 else [1.0, 0.0], f32),
                            (128, 1)),
        "eps_in": np.full((128, 1), 1e-5, f32),
    }


def kernel(**inputs):
    nc = _get_program()
    in_maps = [_prep_inputs(c, inputs) for c in range(8)]
    res = run_bass_kernel_spmd(nc, in_maps, core_ids=list(range(8)))
    out = np.empty((4, 4096, 4096), np.float32)
    for c in range(8):
        bidx, gg = c // 2, c % 2
        out[bidx, 2048 * gg:2048 * gg + 2048] = res.results[c]["out"]
    return out
